# version 02
# baseline (speedup 1.0000x reference)
import contextlib
import os
import numpy as np
import ml_dtypes
import concourse.bass as bass
import concourse.mybir as mybir
from concourse.bass_utils import run_bass_kernel_spmd

BFNP = ml_dtypes.bfloat16

F32 = mybir.dt.float32
BF16 = mybir.dt.bfloat16
I32 = mybir.dt.int32
AF = mybir.ActivationFunctionType
ALU = mybir.AluOpType
AX = mybir.AxisListType

T = 4096
D = 1024
NT = T // 128
ND = D // 128
LN_EPS = 1e-5
ACT_NOP = 8000
SIM = os.environ.get("BASS_SIM_FP64") == "1" or os.environ.get("KERNEL_SIM") == "1"

SET_EXP_LN = 6
SET_SIG_ERF = 2
SET_SIN = 9
SET_GELU = 10


def act_switch(scalar, nc, set_id, guard=None):
    if SIM:
        return
    if guard is not None:
        p, gcell, gsem = guard
        w = p.inc(scalar.copy(gcell, gcell), gsem, 1)
        scalar.wait_ge(gsem, w)
    scalar.add_instruction(
        mybir.InstLoadActFuncSet(name=nc.get_next_instruction_name(),
                                 act_func_set_id=set_id, ins=[], outs=[]))
    scalar.nop(cycle_cnt=ACT_NOP)


class P:

    def __init__(self):
        self.nc = bass.Bass("TRN2", target_bir_lowering=False,
                            detect_race_conditions=False)
        self.stack = contextlib.ExitStack()
        self.cnt = {}
        self.cur = None
        self.sems = []

    def sem(self, name):
        h = self.stack.enter_context(self.nc.semaphore(name))
        self.cnt[name] = 0
        self.sems.append(h)
        return h

    def sbuf(self, name, shape, dtype):
        return (self.cur or self.stack).enter_context(
            self.nc.sbuf_tensor(name, shape, dtype))

    def psum(self, name, shape, dtype):
        return (self.cur or self.stack).enter_context(
            self.nc.psum_tensor(name, shape, dtype))

    @contextlib.contextmanager
    def phase(self):
        prev = self.cur
        with contextlib.ExitStack() as st:
            self.cur = st
            try:
                yield
            finally:
                self.cur = prev

    def inc(self, inst, sem, n=1):
        inst.then_inc(sem, n)
        self.cnt[sem.name] += n
        return self.cnt[sem.name]


def emit_ln_transpose(p, block, x_d, xn_d, xnT_d, pfx="", dbg=None,
                      xnTf_d=None):
    nc = p.nc
    xs = [p.sbuf(f"{pfx}xs{i}", [128, D], F32) for i in range(2)]
    xn = [p.sbuf(f"{pfx}xn{i}", [128, D], F32) for i in range(2)]
    stage = [p.sbuf(f"{pfx}stage{i}", [128, D], BF16) for i in range(2)]
    stagef = [p.sbuf(f"{pfx}stagef{i}", [128, D], F32) for i in range(2)] \
        if xnTf_d is not None else None
    sq = p.sbuf(f"{pfx}sq", [128, D], F32)
    stats = p.sbuf(f"{pfx}stats", [128, 32], F32)
    ident = p.sbuf(f"{pfx}ident", [128, 128], F32)
    consts = p.sbuf(f"{pfx}consts", [128, 2], F32)
    pt = [p.psum(f"{pfx}pt{i}", [128, 128], F32) for i in range(2)]

    ld = p.sem(pfx + "ld")
    sox = p.sem(pfx + "sox")
    sost = p.sem(pfx + "sost")
    csq = p.sem(pfx + "csq")
    vs = p.sem(pfx + "vs")
    sr = p.sem(pfx + "sr")
    xnv = p.sem(pfx + "xnv")
    tp = p.sem(pfx + "tp")
    ev = p.sem(pfx + "ev")
    gpi = p.sem(pfx + "gpi")
    vch = p.sem(pfx + "vch")

    @block.gpsimd
    def _(gp):
        gp.memset(ident[:, :], 1.0)
        gp.affine_select(ident[:, :], ident[:, :], pattern=[[-1, 128]],
                         compare_op=ALU.is_equal, fill=0.0, base=0,
                         channel_multiplier=1)
        p.inc(gp.memset(consts[:, 0:1], LN_EPS), gpi, 1)

    @block.sync
    def _(sync):
        for rt in range(NT):
            b = rt % 2
            if rt >= 2:
                sync.wait_ge(csq, rt - 1)
                sync.wait_ge(xnv, rt - 1)
            p.inc(sync.dma_start(out=xs[b][:, :],
                                 in_=x_d[rt * 128:(rt + 1) * 128, :]), ld, 16)

    @block.scalar
    def _(scalar):
        act_switch(scalar, nc, SET_EXP_LN)
        scalar.wait_ge(gpi, 1)
        for rt in range(NT):
            b = rt % 2
            scalar.wait_ge(ld, 16 * (rt + 1))
            p.inc(scalar.activation(out=sq[:, :], in_=xs[b][:, :],
                                    func=AF.Square, accum_out=stats[:, 4:5]),
                  csq, 1)
            scalar.wait_ge(vs, rt + 1)
            lnv = p.inc(scalar.activation(out=stats[:, 20:21],
                                          in_=stats[:, 12:13],
                                          func=AF.Ln, bias=consts[:, 0:1]),
                        sr, 1)
            scalar.wait_ge(sr, lnv)
            p.inc(scalar.activation(out=stats[:, 16:17], in_=stats[:, 20:21],
                                    func=AF.Exp, scale=-0.5), sr, 1)
            scalar.wait_ge(xnv, rt + 1)
            p.inc(scalar.dma_start(out=xn_d[rt * 128:(rt + 1) * 128, :],
                                   in_=xn[b][:, :]), sox, 16)
            if dbg is not None and rt == 0:
                p.inc(scalar.dma_start(out=dbg, in_=stats[:, :]), sox, 16)

    @block.vector
    def _(vector):
        nev = 16 if xnTf_d is not None else 8
        for rt in range(NT):
            b = rt % 2
            vector.wait_ge(ld, 16 * (rt + 1))
            w = p.inc(vector.reduce_sum(stats[:, 0:1], xs[b][:, :], axis=AX.X),
                      vch, 1)
            vector.wait_ge(vch, w)
            w = p.inc(vector.tensor_scalar_mul(stats[:, 8:9], stats[:, 0:1],
                                               1.0 / D), vch, 1)
            vector.wait_ge(vch, w)
            w = p.inc(vector.tensor_mul(stats[:, 24:25], stats[:, 8:9],
                                        stats[:, 8:9]), vch, 1)
            vector.wait_ge(vch, w)
            vector.wait_ge(csq, rt + 1)
            p.inc(vector.scalar_tensor_tensor(stats[:, 12:13], stats[:, 4:5],
                                              1.0 / D, stats[:, 24:25],
                                              op0=ALU.mult, op1=ALU.subtract),
                  vs, 1)
            vector.wait_ge(sr, 2 * (rt + 1))
            if rt >= 2:
                vector.wait_ge(sox, 16 * (rt - 1))
                vector.wait_ge(tp, 8 * (rt - 1))
            p.inc(vector.tensor_scalar(xn[b][:, :], xs[b][:, :], stats[:, 8:9],
                                       stats[:, 16:17], op0=ALU.subtract,
                                       op1=ALU.mult), xnv, 1)
            for cc in range(8):
                vector.wait_ge(tp, 8 * rt + cc + 1)
                if cc == 0 and rt >= 2:
                    vector.wait_ge(sost, (32 if xnTf_d is not None else 16) * (rt - 1))
                ins = vector.tensor_copy(stage[b][:, cc * 128:(cc + 1) * 128],
                                         pt[cc % 2][:, :])
                if xnTf_d is not None:
                    ins = vector.tensor_copy(
                        stagef[b][:, cc * 128:(cc + 1) * 128], pt[cc % 2][:, :])
                p.inc(ins, ev, 1)

    @block.tensor
    def _(tensor):
        tensor.wait_ge(gpi, 1)
        for rt in range(NT):
            b = rt % 2
            tensor.wait_ge(xnv, rt + 1)
            for cc in range(8):
                w = 8 * rt + cc - 1
                if w > 0:
                    tensor.wait_ge(ev, w)
                p.inc(tensor.transpose(pt[cc % 2][:, :],
                                       xn[b][:, cc * 128:(cc + 1) * 128],
                                       ident[:, :]), tp, 1)

    @block.gpsimd
    def _(gp):
        for rt in range(NT):
            b = rt % 2
            gp.wait_ge(ev, 8 * (rt + 1))
            dst = bass.AP(xnT_d.tensor, rt * 128,
                          [[T, 128], [128 * T, 8], [1, 128]])
            p.inc(gp.dma_start(out=dst,
                               in_=stage[b].ap().rearrange("p (c t) -> p c t",
                                                           c=8)), sost, 16)
            if xnTf_d is not None:
                dstf = bass.AP(xnTf_d.tensor, rt * 128,
                               [[T, 128], [128 * T, 8], [1, 128]])
                p.inc(gp.dma_start(out=dstf,
                                   in_=stagef[b].ap().rearrange(
                                       "p (c t) -> p c t", c=8)), sost, 16)

    @block.sync
    def _(sync):
        sync.wait_ge(sox, p.cnt[pfx + "sox"])
        sync.wait_ge(sost, p.cnt[pfx + "sost"])


DEBUG_STATS = False


def build_phase0():
    p = P()
    nc = p.nc
    x_d = nc.dram_tensor("x", [T, D], F32, kind="ExternalInput").ap()
    xn_d = nc.dram_tensor("xn0", [T, D], F32, kind="ExternalOutput").ap()
    xnT_d = nc.dram_tensor("xn0T", [D, T], BF16, kind="ExternalOutput").ap()
    dbg = nc.dram_tensor("dbg", [128, 32], F32, kind="ExternalOutput").ap() \
        if DEBUG_STATS else None
    with p.stack:
        with nc.Block() as block:
            emit_ln_transpose(p, block, x_d, xn_d, xnT_d, pfx="p0_", dbg=dbg)
    return p.nc


H = 16
CH = 512
NCH = T // CH
RING = 1536
P1_STAGE = 99
P1_SUB = 99


def emit_attn0(p, block, xnT_d, wdram, attnT_d, crev_d, pfx="a0_"):
    nc = p.nc
    wq = p.sbuf(pfx + "wq", [128, 8192], BF16)
    wk = p.sbuf(pfx + "wk", [128, 8192], BF16)
    wv = p.sbuf(pfx + "wv", [128, 8192], BF16)
    wp = p.sbuf(pfx + "wp", [128, 8192], BF16)
    wr = p.sbuf(pfx + "wr", [128, 8192], BF16)
    bq8 = p.sbuf(pfx + "bq8", [128, 8], F32)
    bpj = p.sbuf(pfx + "bpj", [128, 8], F32)
    brp = p.sbuf(pfx + "brp", [128, 8], F32)
    bndf = p.sbuf(pfx + "bndf", [16, 1024], F32)
    bndr = p.sbuf(pfx + "bndr", [128, 1024], BF16)
    xch = p.sbuf(pfx + "xch", [128, 8 * CH], BF16)
    qT = p.sbuf(pfx + "qT", [128, 8 * CH], BF16)
    rT = p.sbuf(pfx + "rT", [128, 8 * CH], BF16)
    kT = p.sbuf(pfx + "kT", [128, 8 * RING], BF16)
    va = p.sbuf(pfx + "va", [128, 12 * 1040], BF16)
    aT = p.sbuf(pfx + "aT", [128, 8 * CH], BF16)
    crevs = [p.sbuf(pfx + f"crevs{i}", [128, 1024], BF16) for i in range(2)]
    scb = [p.sbuf(pfx + f"sc{i}", [128, 1152], BF16) for i in range(2)]
    s_sb = p.sbuf(pfx + "s_sb", [128, 1152], F32)
    w_sb = p.sbuf(pfx + "w_sb", [128, 1152], BF16)
    wT_sb = p.sbuf(pfx + "wT_sb", [128, 1152], BF16)
    m0 = p.sbuf(pfx + "m0", [128, 128], F32)
    m8 = p.sbuf(pfx + "m8", [128, 128], F32)
    m0b = p.sbuf(pfx + "m0b", [128, 128], BF16)
    m8b = p.sbuf(pfx + "m8b", [128, 128], BF16)
    idf = p.sbuf(pfx + "idf", [128, 128], F32)
    idb = p.sbuf(pfx + "idb", [128, 128], BF16)
    zrow = p.sbuf(pfx + "zrow", [1, 1024], BF16)
    nmax = p.sbuf(pfx + "nmax", [128, 4], F32)
    dnb = p.sbuf(pfx + "dnb", [1, 128], F32)
    bc_sb = p.sbuf(pfx + "bc_sb", [64, 128], F32)
    ones1 = p.sbuf(pfx + "ones1", [1, 64], F32)
    outsb = [p.sbuf(pfx + f"outsb{i}", [128, 512], F32) for i in range(2)]

    crev_ps = p.psum(pfx + "crev_ps", [128, 512], F32)
    qk_ps = p.psum(pfx + "qk_ps", [128, 1536], F32)
    wtp = [p.psum(pfx + f"wtp{i}", [128, 384], BF16) for i in range(3)]
    av_ps = p.psum(pfx + "av_ps", [128, 256], F32)

    c_pe = p.sem(pfx + "c_pe")
    c_ve = p.sem(pfx + "c_ve")
    c_se = p.sem(pfx + "c_se")
    d_wl = p.sem(pfx + "d_wl")
    d_ldx = p.sem(pfx + "d_ldx")
    d_cw = p.sem(pfx + "d_cw")
    d_sh = p.sem(pfx + "d_sh")
    d_out = p.sem(pfx + "d_out")
    gpi = p.sem(pfx + "gpi")

    def rb(h):
        return 64 * (h % 2)

    def hc(h):
        return h // 2

    @block.gpsimd
    def _(gp):
        for wsb, name in [(wq, "Wq"), (wk, "Wk"), (wv, "Wv"), (wp, "Wproj")]:
            wd = wdram[name]
            srcw = bass.AP(wd.tensor, 0,
                           [[1024, 128], [128 * 1024, 8], [128, 8], [1, 128]])
            dst = wsb.ap().rearrange("p (a b t) -> p a b t", a=8, b=8)
            p.inc(gp.dma_start(out=dst, in_=srcw), d_wl, 16)
        gp.memset(wr[:, :], 0.0)
        wrd = wdram["Wr"]
        for kc in range(8):
            dst = bass.AP(wr, kc * 1024, [[8192, 128], [64, 16], [1, 10]])
            srcw = bass.AP(wrd.tensor, kc * 128 * 160,
                           [[160, 128], [10, 16], [1, 10]])
            p.inc(gp.dma_start(out=dst, in_=srcw), d_wl, 16)
        for dname, dstt in [("bq", bq8), ("bproj", bpj)]:
            bd = wdram[dname]
            for mc in range(8):
                p.inc(gp.dma_start(out=dstt[:, mc:mc + 1],
                                   in_=bass.AP(bd.tensor, mc * 128,
                                               [[1, 128], [1, 1]])), d_wl, 16)
        gp.memset(brp[:, :], 0.0)
        brd = wdram["br"]
        for h in range(H):
            p.inc(gp.dma_start(out=brp[rb(h):rb(h) + 10, hc(h):hc(h) + 1],
                               in_=bass.AP(brd.tensor, 10 * h, [[1, 10], [1, 1]])),
                  d_wl, 16)
        p.inc(gp.dma_start(out=bndf[0:10, :], in_=wdram["b_nd"]), d_wl, 16)
        gp.memset(m0[:, :], 0.0)
        gp.affine_select(m0[:, :], m0[:, :], pattern=[[1, 128]],
                         compare_op=ALU.is_ge, fill=-1e9, base=-1,
                         channel_multiplier=-1)
        gp.memset(m8[:, :], 0.0)
        gp.affine_select(m8[:, :], m8[:, :], pattern=[[-1, 128]],
                         compare_op=ALU.is_ge, fill=-1e9, base=0,
                         channel_multiplier=1)
        gp.tensor_copy(m0b[:, :], m0[:, :])
        gp.tensor_copy(m8b[:, :], m8[:, :])
        gp.memset(idf[:, :], 1.0)
        gp.affine_select(idf[:, :], idf[:, :], pattern=[[-1, 128]],
                         compare_op=ALU.is_equal, fill=0.0, base=0,
                         channel_multiplier=1)
        gp.tensor_copy(idb[:, :], idf[:, :])
        gp.memset(ones1[:, :], 1.0)
        gp.memset(zrow[:, :], 0.0)
        for s in range(12):
            for h in range(H):
                gp.memset(va[:, s * 1040 + 65 * h + 64: s * 1040 + 65 * h + 65],
                          1.0)
        for buf in range(2):
            p.inc(gp.dma_start(out=bass.AP(crev_d[buf].tensor, 0,
                                           [[1024, 1], [1, 1024]]),
                               in_=zrow[0:1, :]), d_wl, 16)
            p.inc(gp.dma_start(out=bass.AP(crev_d[buf].tensor, 129 * 1024,
                                           [[1024, 1], [1, 1024]]),
                               in_=zrow[0:1, :]), d_wl, 16)
        p.inc(gp.memset(nmax[:, 2:3], 0.0), gpi, 1)

    @block.vector
    def _(vector):
        vector.wait_ge(d_wl, p.cnt[pfx + "d_wl"])
        vector.wait_ge(gpi, 1)
        p.inc(vector.tensor_scalar_mul(bq8[:, :], bq8[:, :], 0.125), c_ve, 1)
        rev = bass.AP(bndf, 1023, [[1024, 10], [-1, 1024]])
        p.inc(vector.tensor_copy(bndr[0:10, :], rev), c_ve, 1)
        p.inc(vector.tensor_copy(bndr[64:74, :], rev), c_ve, 1)

    @block.scalar
    def _(scalar):
        act_switch(scalar, nc, SET_EXP_LN)
        scalar.wait_ge(c_ve, p.cnt[pfx + "c_ve"])
        scalar.wait_ge(gpi, 1)

    crevs_freed = [0, 0]
    crevd_freed = [0, 0]
    scb_freed = [("ve", 0), ("ve", 0)]
    tilecnt = 0
    prev_evacadd = 0
    wT_freed = 0
    av_freed = 0
    out_dma = {}

    for c in range(NCH):
        barrier = dict(pe=p.cnt[pfx + "c_pe"], ve=p.cnt[pfx + "c_ve"],
                       se=p.cnt[pfx + "c_se"])

        if P1_SUB < 1:
            continue

        @block.sync
        def _(sync):
            sync.wait_ge(c_pe, barrier["pe"])
            sync.wait_ge(c_ve, barrier["ve"])
            sync.wait_ge(c_se, barrier["se"])
            srcx = bass.AP(xnT_d.tensor, c * CH, [[T, 128], [128 * T, 8], [1, CH]])
            dstx = xch.ap().rearrange("p (a t) -> p a t", a=8)
            p.inc(sync.dma_start(out=dstx, in_=srcx), d_ldx, 16)
        ldx_done = p.cnt[pfx + "d_ldx"]

        kslot = c % 3
        sched_evac = {}

        def proj(dst_fn, wsb, evac_kind, bias_sb=None, scale=1.0, first=False):
            for mc in range(8):
                bank = mc % 2

                @block.tensor
                def _(tensor):
                    if first and mc == 0:
                        tensor.wait_ge(d_ldx, ldx_done)
                        tensor.wait_ge(c_ve, barrier["ve"])
                        tensor.wait_ge(c_se, barrier["se"])
                        tensor.wait_ge(gpi, 1)
                    if mc >= 2:
                        kind, w = sched_evac[mc - 2]
                        tensor.wait_ge(c_ve if kind == "dve" else c_se, w)
                    for kc in range(8):
                        ins = tensor.matmul(
                            qk_ps[:, bank * 512:(bank + 1) * 512],
                            wsb[:, kc * 1024 + mc * 128: kc * 1024 + (mc + 1) * 128],
                            xch[:, kc * CH:(kc + 1) * CH],
                            start=(kc == 0), stop=(kc == 7))
                    p.inc(ins, c_pe, 1)
                mmv = p.cnt[pfx + "c_pe"]

                @block.scalar
                def _(scalar):
                    scalar.wait_ge(c_pe, mmv)
                    if evac_kind == "se_copy":
                        p.inc(scalar.copy(dst_fn(mc),
                                          qk_ps[:, bank * 512:(bank + 1) * 512]),
                              c_se, 1)
                    else:
                        p.inc(scalar.activation(
                            out=dst_fn(mc),
                            in_=qk_ps[:, bank * 512:(bank + 1) * 512],
                            func=AF.Identity, bias=bias_sb[:, mc:mc + 1], scale=scale),
                            c_se, 1)
                sched_evac[mc] = ("se", p.cnt[pfx + "c_se"])

        if P1_SUB >= 2:
            proj(lambda mc: kT[:, mc * RING + kslot * 512: mc * RING + (kslot + 1) * 512],
                 wk, "se_copy", first=True)
        if P1_SUB >= 3:
            sched_evac = {}
            proj(lambda mc: qT[:, mc * CH:(mc + 1) * CH], wq, "se", bq8, 0.125)
            sched_evac = {}
            proj(lambda mc: rT[:, mc * CH:(mc + 1) * CH], wr, "se", brp, 1.0)

        sched_evac = {}
        for tt in range(4 if P1_SUB >= 3 else 0):
            slot = (4 * c + tt) % 12
            for half in range(2):
                idx = 2 * tt + half
                bank = idx % 2

                @block.tensor
                def _(tensor):
                    if idx >= 2:
                        kind, w = sched_evac[idx - 2]
                        tensor.wait_ge(c_ve if kind == "dve" else c_se, w)
                    for kc in range(8):
                        ins = tensor.matmul(
                            qk_ps[:, bank * 512:(bank + 1) * 512],
                            xch[:, kc * CH + tt * 128: kc * CH + (tt + 1) * 128],
                            wv[:, kc * 1024 + half * 512: kc * 1024 + (half + 1) * 512],
                            start=(kc == 0), stop=(kc == 7))
                    p.inc(ins, c_pe, 1)
                mmv = p.cnt[pfx + "c_pe"]

                @block.scalar
                def _(scalar):
                    scalar.wait_ge(c_pe, mmv)
                    for hh in range(8):
                        h = half * 8 + hh
                        ins = scalar.copy(
                            va[:, slot * 1040 + 65 * h: slot * 1040 + 65 * h + 64],
                            qk_ps[:, bank * 512 + 64 * hh: bank * 512 + 64 * hh + 64])
                        if hh == 7:
                            p.inc(ins, c_se, 1)
                sched_evac[idx] = ("se", p.cnt[pfx + "c_se"])

        proj_done = dict(pe=p.cnt[pfx + "c_pe"], ve=p.cnt[pfx + "c_ve"],
                         se=p.cnt[pfx + "c_se"])

        for ql in range(4 if P1_SUB >= 5 else 0):
            qtg = c * 4 + ql
            jbmin = max(0, 8 - qtg)
            nblk = 9 - jbmin
            cl = jbmin * 128
            for h in range(H):
                buf = tilecnt % 2
                qcol = hc(h) * CH + ql * 128
                first_tile = (ql == 0 and h == 0)

                if P1_STAGE < 1:
                    tilecnt += 1
                    continue
                crev_ev = []
                for half in range(2):
                    @block.tensor
                    def _(tensor):
                        if half == 0:
                            if first_tile:
                                tensor.wait_ge(c_se, proj_done["se"])
                                tensor.wait_ge(c_ve, proj_done["ve"])
                            tensor.wait_ge(d_cw, crevs_freed[buf])
                        else:
                            tensor.wait_ge(c_se, crev_ev[0])
                        p.inc(tensor.matmul(
                            crev_ps[:, :],
                            rT[rb(h):rb(h) + 10, qcol:qcol + 128],
                            bndr[rb(h):rb(h) + 10, half * 512:(half + 1) * 512],
                            start=True, stop=True), c_pe, 1)
                    mmv = p.cnt[pfx + "c_pe"]

                    @block.scalar
                    def _(scalar):
                        scalar.wait_ge(c_pe, mmv)
                        p.inc(scalar.copy(
                            crevs[buf][:, half * 512:(half + 1) * 512],
                            crev_ps[:, :]), c_se, 1)
                    crev_ev.append(p.cnt[pfx + "c_se"])

                if P1_STAGE < 2:
                    tilecnt += 1
                    continue
                @block.gpsimd
                def _(gp):
                    gp.wait_ge(c_se, crev_ev[1])
                    gp.wait_ge(d_sh, crevd_freed[buf])
                    dstc = bass.AP(crev_d[buf].tensor, 1024,
                                   [[1024, 128], [1, 1024]])
                    p.inc(gp.dma_start(out=dstc, in_=crevs[buf][:, :]), d_cw, 16)
                cwv = p.cnt[pfx + "d_cw"]
                crevs_freed[buf] = cwv

                @block.sync
                def _(sync):
                    sync.wait_ge(d_cw, cwv)
                    kind, wv_ = scb_freed[buf]
                    sync.wait_ge(c_pe if kind == "pe" else c_ve, wv_)
                    srcs = bass.AP(crev_d[buf].tensor, 1023,
                                   [[1023, 128], [1, 1152]])
                    p.inc(sync.dma_start(out=scb[buf][:, :], in_=srcs), d_sh, 16)
                shv = p.cnt[pfx + "d_sh"]
                crevd_freed[buf] = shv

                if P1_STAGE < 3:
                    tilecnt += 1
                    continue
                segs = [(lo, hi) for lo, hi in [(cl, 512), (512, 1024),
                                                (1024, 1152)] if lo < hi]
                @block.tensor
                def _(tensor):
                    tensor.wait_ge(c_se, prev_evacadd)
                    tensor.wait_ge(d_sh, shv)
                    last_ins = None
                    for bk in range(3):
                        blo, bhi = bk * 512, min((bk + 1) * 512, 1152)
                        ops = []
                        for jb in range(max(jbmin, bk * 4), min(9, (bk + 1) * 4)):
                            kt = qtg - 8 + jb
                            ringcol = (kt * 128) % RING
                            ops.append(("qk", jb, ringcol))
                        if not ops:
                            continue
                        lo = max(cl, blo)
                        ops.append(("sc", lo, bhi))
                        if bk == 0 and jbmin == 0:
                            ops.append(("m0",))
                        if bk == 2:
                            ops.append(("m8",))
                        for i, op in enumerate(ops):
                            st_f = (i == 0)
                            sp_f = (i == len(ops) - 1)
                            if op[0] == "qk":
                                jb, ringcol = op[1], op[2]
                                last_ins = tensor.matmul(
                                    qk_ps[:, jb * 128:(jb + 1) * 128],
                                    qT[rb(h):rb(h) + 64, qcol:qcol + 128],
                                    kT[rb(h):rb(h) + 64, hc(h) * RING + ringcol:
                                       hc(h) * RING + ringcol + 128],
                                    start=st_f, stop=sp_f)
                            elif op[0] == "sc":
                                last_ins = tensor.matmul(
                                    qk_ps[:, op[1]:op[2]], idb[:, :],
                                    scb[buf][:, op[1]:op[2]], start=st_f, stop=sp_f)
                            elif op[0] == "m0":
                                last_ins = tensor.matmul(
                                    qk_ps[:, 0:128], idb[:, :], m0b[:, :],
                                    start=st_f, stop=sp_f)
                            else:
                                last_ins = tensor.matmul(
                                    qk_ps[:, 1024:1152], idb[:, :], m8b[:, :],
                                    start=st_f, stop=sp_f)
                    p.inc(last_ins, c_pe, 1)
                qkv = p.cnt[pfx + "c_pe"]
                scb_freed[buf] = ("pe", qkv)

                if P1_STAGE < 4:
                    tilecnt += 1
                    continue
                @block.scalar
                def _(scalar):
                    scalar.wait_ge(c_pe, qkv)
                    for i, (lo, hi) in enumerate(segs):
                        ins = scalar.copy(s_sb[:, lo:hi], qk_ps[:, lo:hi])
                        if i == len(segs) - 1:
                            p.inc(ins, c_se, 1)
                sev = p.cnt[pfx + "c_se"]
                prev_evacadd = sev

                @block.vector
                def _(vector):
                    vector.wait_ge(c_se, sev)
                    p.inc(vector.reduce_max(nmax[:, 0:1], s_sb[:, cl:1152],
                                            axis=AX.X, negate=True), c_ve, 1)
                eav = p.cnt[pfx + "c_ve"]

                if P1_STAGE < 5:
                    tilecnt += 1
                    continue
                @block.scalar
                def _(scalar):
                    scalar.wait_ge(c_ve, eav)
                    p.inc(scalar.activation(out=w_sb[:, cl:1152],
                                            in_=s_sb[:, cl:1152], func=AF.Exp,
                                            bias=nmax[:, 0:1]), c_se, 1)
                expv = p.cnt[pfx + "c_se"]

                if P1_STAGE < 6:
                    tilecnt += 1
                    continue
                @block.tensor
                def _(tensor):
                    tensor.wait_ge(c_se, expv)
                    tensor.wait_ge(c_ve, wT_freed)
                    for jb in range(jbmin, 9):
                        ins = tensor.transpose(
                            wtp[jb // 3][:, (jb % 3) * 128:(jb % 3) * 128 + 128],
                            w_sb[:, jb * 128:(jb + 1) * 128], idb[:, :])
                        if jb == 8:
                            p.inc(ins, c_pe, 1)
                trv = p.cnt[pfx + "c_pe"]

                if P1_STAGE < 7:
                    tilecnt += 1
                    continue
                @block.vector
                def _(vector):
                    vector.wait_ge(c_pe, trv)
                    for k in range(3):
                        lo = max(jbmin, 3 * k)
                        hi = 3 * k + 3
                        if lo >= hi:
                            continue
                        ins = vector.tensor_copy(
                            wT_sb[:, lo * 128:hi * 128],
                            wtp[k][:, (lo - 3 * k) * 128:384])
                        if k == 2:
                            p.inc(ins, c_ve, 1)
                ewv = p.cnt[pfx + "c_ve"]
                wT_freed = ewv

                if P1_STAGE < 8:
                    tilecnt += 1
                    continue
                @block.tensor
                def _(tensor):
                    tensor.wait_ge(c_ve, ewv)
                    tensor.wait_ge(c_ve, av_freed)
                    for i, jb in enumerate(range(jbmin, 9)):
                        kt = qtg - 8 + jb
                        slot = kt % 12
                        ins = tensor.matmul(
                            av_ps[0:65, 0:128],
                            va[:, slot * 1040 + 65 * h: slot * 1040 + 65 * h + 65],
                            wT_sb[:, jb * 128:(jb + 1) * 128],
                            start=(i == 0), stop=(i == nblk - 1))
                        if i == nblk - 1:
                            p.inc(ins, c_pe, 1)
                avv = p.cnt[pfx + "c_pe"]

                if P1_STAGE < 9:
                    tilecnt += 1
                    continue
                @block.vector
                def _(vector):
                    vector.wait_ge(c_pe, avv)
                    p.inc(vector.reciprocal(dnb[0:1, :], av_ps[64:65, 0:128]),
                          c_ve, 1)
                rcv = p.cnt[pfx + "c_ve"]

                if P1_STAGE < 10:
                    tilecnt += 1
                    continue
                @block.tensor
                def _(tensor):
                    tensor.wait_ge(c_ve, rcv)
                    p.inc(tensor.matmul(av_ps[0:64, 128:256], ones1[0:1, 0:64],
                                        dnb[0:1, 0:128], start=True, stop=True),
                          c_pe, 1)
                bcv = p.cnt[pfx + "c_pe"]

                @block.vector
                def _(vector):
                    vector.wait_ge(c_pe, bcv)
                    w = p.inc(vector.tensor_copy(bc_sb[:, :],
                                                 av_ps[0:64, 128:256]), c_ve, 1)
                    vector.wait_ge(c_ve, w)
                    p.inc(vector.tensor_tensor(
                        aT[rb(h):rb(h) + 64, qcol:qcol + 128],
                        av_ps[0:64, 0:128], bc_sb[:, :], op=ALU.mult), c_ve, 1)
                av_freed = p.cnt[pfx + "c_ve"]
                tilecnt += 1

        at_done = p.cnt[pfx + "c_ve"]
        projout_evac = {}
        for mc in range(8 if P1_SUB >= 4 else 0):
            bank = mc % 2

            @block.tensor
            def _(tensor):
                if mc == 0:
                    tensor.wait_ge(c_ve, at_done)
                if mc >= 2:
                    tensor.wait_ge(c_se, projout_evac[mc - 2])
                for kc in range(8):
                    ins = tensor.matmul(
                        qk_ps[:, bank * 512:(bank + 1) * 512],
                        wp[:, kc * 1024 + mc * 128: kc * 1024 + (mc + 1) * 128],
                        aT[:, kc * CH:(kc + 1) * CH],
                        start=(kc == 0), stop=(kc == 7))
                p.inc(ins, c_pe, 1)
            mmv = p.cnt[pfx + "c_pe"]

            @block.scalar
            def _(scalar):
                scalar.wait_ge(c_pe, mmv)
                if out_dma.get(mc % 2):
                    scalar.wait_ge(d_out, out_dma[mc % 2])
                p.inc(scalar.activation(out=outsb[mc % 2][:, :],
                                        in_=qk_ps[:, bank * 512:(bank + 1) * 512],
                                        func=AF.Identity, bias=bpj[:, mc:mc + 1]),
                      c_se, 1)
            projout_evac[mc] = p.cnt[pfx + "c_se"]

            @block.gpsimd
            def _(gp):
                gp.wait_ge(c_se, projout_evac[mc])
                dsto = bass.AP(attnT_d.tensor, mc * 128 * T + c * CH,
                               [[T, 128], [1, CH]])
                p.inc(gp.dma_start(out=dsto, in_=outsb[mc % 2][:, :]), d_out, 16)
            out_dma[mc % 2] = p.cnt[pfx + "d_out"]


    @block.sync
    def _(sync):
        sync.wait_ge(d_out, p.cnt[pfx + "d_out"])


ATTN_W = ["Wq", "bq", "Wk", "Wv", "Wproj", "bproj", "Wr", "br", "b_nd"]
ATTN_SHAPES = {"Wq": [D, D], "bq": [D], "Wk": [D, D], "Wv": [D, D],
               "Wproj": [D, D], "bproj": [D], "Wr": [D, 160], "br": [160],
               "b_nd": [10, 1024]}


def build_phase1():
    p = P()
    nc = p.nc
    xnT_d = nc.dram_tensor("xn0T", [D, T], BF16, kind="ExternalInput").ap()
    wdram = {n: nc.dram_tensor(n, ATTN_SHAPES[n], F32, kind="ExternalInput").ap()
             for n in ATTN_W}
    attnT_d = nc.dram_tensor("attn0T", [D, T], F32, kind="ExternalOutput").ap()
    crev_d = [nc.dram_tensor(f"crevd{i}", [130, 1024], BF16, kind="Internal").ap()
              for i in range(2)]
    with p.stack:
        with nc.Block() as block:
            emit_attn0(p, block, xnT_d, wdram, attnT_d, crev_d, pfx="a0_")
    return p.nc




CHF = 512
SUBF = 256
NCF = T // CHF
NSF = CHF // SUBF
DT_CONST = 0.1


def prep_ffn_weights(g):
    def lay(w):
        i, o = w.shape
        return np.ascontiguousarray(
            np.asarray(w, np.float32).reshape(i // 128, 128, o)
            .transpose(1, 0, 2).reshape(128, -1))

    w1cat = np.ascontiguousarray(
        np.asarray(g["expert_w1"], np.float32).transpose(1, 0, 2).reshape(2048, 512))
    w2aug = np.zeros((640, 3072), np.float32)
    w2aug[:512] = np.asarray(g["expert_w2"], np.float32).reshape(512, 3072)
    w2aug[512:520] = np.asarray(g["expert_b2"], np.float32)
    w2saug = np.zeros((128, 3072), np.float32)
    w2saug[:64] = np.asarray(g["shared_w2"], np.float32)
    w2saug[64] = np.asarray(g["shared_b2"], np.float32)

    def col(v):
        v = np.asarray(v, np.float32).reshape(-1)
        ncol = max(1, (len(v) + 127) // 128)
        out = np.zeros((128, ncol), np.float32)
        for i, x in enumerate(v):
            out[i % 128, i // 128] = x
        return out

    return {
        "rw1l": lay(g["router_w1"]), "hw1l": lay(g["halt_w1"]),
        "w1catl": lay(w1cat), "w2augl": lay(w2aug),
        "w2saugl": np.ascontiguousarray(w2saug),
        "sw1l": lay(np.asarray(g["shared_w1"], np.float32)),
        "refw1l": lay(g["refine_w1"]), "refw2l": lay(g["refine_w2"]),
        "rw2l": lay(g["router_w2"]), "hw2l": lay(g["halt_w2"]),
        "rb1c": col(g["router_b1"]), "hb1c": col(g["halt_b1"]),
        "b1catc": col(np.asarray(g["expert_b1"], np.float32).reshape(-1)),
        "sb1c": col(g["shared_b1"]),
        "refb1c": col(g["refine_b1"]), "refb2c": col(g["refine_b2"]),
        "muc": col(g["mu"]), "iwc": col(g["integration_weight"]),
        "rb2row": np.asarray(g["router_b2"], np.float32).reshape(1, 8),
        "hb2s": np.asarray(g["halt_b2"], np.float32).reshape(1, 1),
        "swts": np.asarray(g["shared_weight"], np.float32).reshape(1, 1),
    }


FFN_W_BF = ["rw1l", "hw1l", "w1catl", "w2augl", "w2saugl", "sw1l", "refw1l",
            "refw2l", "rw2l", "hw2l"]
FFN_W_F32 = ["rb1c", "hb1c", "b1catc", "sb1c", "refb1c", "refb2c", "muc",
             "iwc", "rb2row", "hb2s", "swts"]
FFN_SHAPES = {
    "rw1l": [128, 2048], "hw1l": [128, 2048], "w1catl": [128, 8192],
    "w2augl": [128, 15360], "w2saugl": [128, 3072], "sw1l": [128, 1024],
    "refw1l": [128, 16384], "refw2l": [128, 16384], "rw2l": [128, 16],
    "hw2l": [128, 2],
    "rb1c": [128, 2], "hb1c": [128, 2], "b1catc": [128, 4], "sb1c": [128, 1],
    "refb1c": [128, 16], "refb2c": [128, 8], "muc": [128, 8], "iwc": [128, 8],
    "rb2row": [1, 8], "hb2s": [1, 1], "swts": [1, 1],
}
P2_EXPALT = 0
P2_S5 = 7
P2_S4 = 7
P2_STAGE = 99


def emit_ffn0(p, block, xnT_d, wd, fT_d, pfx="f0_"):
    nc = p.nc
    wsb = {n: p.sbuf(pfx + n, FFN_SHAPES[n], BF16) for n in FFN_W_BF
           if n not in ("refw1l", "refw2l")}
    wsb["refw1l"] = wsb["refw2l"] = p.sbuf(pfx + "refw", [128, 16384], BF16)
    csb = {n: p.sbuf(pfx + n, FFN_SHAPES[n], F32) for n in FFN_W_F32}
    rb2b = p.sbuf(pfx + "rb2b", [1, 8], BF16)

    intf = p.sbuf(pfx + "intf", [128, 8 * CHF], F32)
    intc = p.sbuf(pfx + "intc", [128, 8 * CHF], BF16)
    vb = p.sbuf(pfx + "vb", [128, 8 * CHF], BF16)
    g1buf = p.sbuf(pfx + "g1buf", [128, 2 * CHF], BF16)
    r1g = g1buf
    h1g = g1buf
    hat = p.sbuf(pfx + "hat", [128, 4 * CHF], BF16)
    hkm = p.sbuf(pfx + "hkm", [128, 4 * SUBF], BF16)
    hsT = p.sbuf(pfx + "hsT", [128, CHF], BF16)
    shpool = p.sbuf(pfx + "shpool", [128, 72 * SUBF], BF16)
    oT = {k: bass.AP(shpool, (k - 1) * 24 * SUBF,
                     [[72 * SUBF, 128], [1, 24 * SUBF]]) for k in (1, 2, 3)}
    ohT = {1: p.sbuf(pfx + "oh1T", [128, CHF], BF16),
           2: p.sbuf(pfx + "oh2T", [128, CHF], BF16)}
    w1row = p.sbuf(pfx + "w1row", [1, CHF], F32)
    w2row = p.sbuf(pfx + "w2row", [1, CHF], F32)
    urow = p.sbuf(pfx + "urow", [1, CHF], F32)
    hrow = p.sbuf(pfx + "hrow", [1, CHF], F32)
    def ref1g_sl(mc):
        return bass.AP(shpool, mc * CHF, [[72 * SUBF, 128], [1, CHF]])

    def ref1g_rhs(kc):
        return bass.AP(shpool, kc * CHF, [[72 * SUBF, 128], [1, CHF]])
    prb = p.sbuf(pfx + "prb", [128, 8], F32)
    tcols = p.sbuf(pfx + "tcols", [128, 64], F32)
    t16 = p.sbuf(pfx + "t16", [128, 16], F32)
    ones8 = p.sbuf(pfx + "ones8", [128, 8], F32)
    zcol = p.sbuf(pfx + "zcol", [128, 1], F32)
    onesb = p.sbuf(pfx + "onesb", [1, 128], BF16)
    onesbf = p.sbuf(pfx + "onesbf", [1, 128], F32)
    selm = p.sbuf(pfx + "selm", [128, 4 * 128], BF16)
    idf = p.sbuf(pfx + "idf", [128, 128], F32)
    swc = p.sbuf(pfx + "swc", [128, 32], F32)
    scr = {n: p.sbuf(pfx + "scr_" + n, [128, SUBF], BF16)
           for n in ["xmu", "t1", "t2", "vr1", "vr2", "vrs", "tk1", "tk2",
                     "tks", "vacc", "sacc", "tmp", "tmp2", "dx", "q"]}
    fout = [p.sbuf(pfx + f"fout{i}", [128, CHF], F32) for i in range(2)]

    psA = p.psum(pfx + "psA", [128, 512], F32)
    psB = p.psum(pfx + "psB", [128, 512], F32)
    psbc = p.psum(pfx + "psbc", [128, 1024], F32)
    psM = [p.psum(pfx + f"psM{i}", [128, 256], F32) for i in range(2)]
    psR = p.psum(pfx + "psR", [128, 256], F32)

    c_pe = p.sem(pfx + "c_pe")
    c_ve = p.sem(pfx + "c_ve")
    c_se = p.sem(pfx + "c_se")
    vch = p.sem(pfx + "vch")
    d_wl = p.sem(pfx + "d_wl")
    d_ldx = p.sem(pfx + "d_ldx")
    d_lx2 = p.sem(pfx + "d_lx2")
    d_out = p.sem(pfx + "d_out")
    gpi = p.sem(pfx + "gpi")
    gsem = p.sem(pfx + "gsem")
    gcell = p.sbuf(pfx + "gcell", [1, 4], F32)
    GRD = (p, gcell[0:1, 0:1], gsem)

    semmap = {"se": c_se, "ve": c_ve, "pe": c_pe}
    bankstate = {0: ("se", 0), 1: ("se", 0)}
    mask_hist = {0: ("ve", 0), 1: ("ve", 0)}

    @block.gpsimd
    def _(gp):
        for n in FFN_W_BF:
            if n in ("refw1l", "refw2l"):
                continue
            p.inc(gp.dma_start(out=wsb[n].ap(), in_=wd[n]), d_wl, 16)
        for n in FFN_W_F32:
            p.inc(gp.dma_start(out=csb[n].ap(), in_=wd[n]), d_wl, 16)
        p.inc(gp.dma_start(out=rb2b.ap(), in_=wd["rb2row"]), d_wl, 16)
        gp.memset(ones8[:, :], 1.0)
        gp.memset(zcol[:, :], 0.0)
        gp.memset(onesb[:, :], 1.0)
        gp.memset(onesbf[:, :], 1.0)
        gp.memset(hsT[:, :], 0.0)
        gp.memset(ohT[1][:, :], 0.0)
        gp.memset(ohT[2][:, :], 0.0)
        gp.memset(idf[:, :], 1.0)
        gp.affine_select(idf[:, :], idf[:, :], pattern=[[-1, 128]],
                         compare_op=ALU.is_equal, fill=0.0, base=0,
                         channel_multiplier=1)
        gp.memset(selm[:, :], 0.0)
        for mc in range(4):
            gp.memset(selm[0:8, mc * 128:(mc + 1) * 128], 1.0)
            gp.affine_select(selm[0:8, mc * 128:(mc + 1) * 128],
                             selm[0:8, mc * 128:(mc + 1) * 128],
                             pattern=[[1, 128]], compare_op=ALU.is_ge,
                             fill=0.0, base=128 * mc, channel_multiplier=-64)
            gp.affine_select(selm[0:8, mc * 128:(mc + 1) * 128],
                             selm[0:8, mc * 128:(mc + 1) * 128],
                             pattern=[[-1, 128]], compare_op=ALU.is_ge,
                             fill=0.0, base=63 - 128 * mc, channel_multiplier=64)
        p.inc(gp.memset(tcols[:, :], 0.0), gpi, 1)

    @block.scalar
    def _(scalar):
        act_switch(scalar, nc, SET_SIG_ERF, GRD)
        scalar.wait_ge(gpi, 1)
        scalar.wait_ge(d_wl, p.cnt[pfx + "d_wl"])
        p.inc(scalar.activation(out=swc[0:1, 20:21], in_=csb["swts"][0:1, 0:1],
                                func=AF.Sigmoid), c_se, 1)
    sw_done = p.cnt[pfx + "c_se"]

    @block.tensor
    def _(tensor):
        tensor.wait_ge(c_se, sw_done)
        p.inc(tensor.matmul(psR[:, 0:1], onesbf[0:1, :], swc[0:1, 20:21],
                            start=True, stop=True), c_pe, 1)
    bcv = p.cnt[pfx + "c_pe"]

    @block.vector
    def _(vector):
        vector.wait_ge(c_pe, bcv)
        w = p.inc(vector.tensor_copy(swc[:, 0:1], psR[:, 0:1]), vch, 1)
        vector.wait_ge(vch, w)
        w = p.inc(vector.tensor_scalar(swc[:, 4:5], swc[:, 0:1], -1.0, 1.0,
                                       op0=ALU.mult, op1=ALU.add), vch, 1)
        vector.wait_ge(vch, w)
        w = p.inc(vector.tensor_scalar_mul(swc[:, 8:9], swc[:, 0:1], DT_CONST),
                  vch, 1)
        vector.wait_ge(vch, w)
        w = p.inc(vector.tensor_scalar_mul(swc[:, 12:13], swc[:, 4:5],
                                           DT_CONST), vch, 1)
        vector.wait_ge(vch, w)
        p.inc(vector.tensor_scalar_mul(swc[:, 16:17], swc[:, 4:5], -1.0),
              vch, 1)
    sw_ve = p.cnt[pfx + "vch"]

    for c in range(NCF):
        tcol = c * CHF
        barrier = dict(pe=p.cnt[pfx + "c_pe"], ve=p.cnt[pfx + "c_ve"],
                       se=p.cnt[pfx + "c_se"], vh=p.cnt[pfx + "vch"])

        @block.sync
        def _(sync):
            sync.wait_ge(c_pe, barrier["pe"])
            sync.wait_ge(c_ve, barrier["ve"])
            sync.wait_ge(c_se, barrier["se"])
            sync.wait_ge(vch, max(barrier["vh"], sw_ve))
            srcx = bass.AP(xnT_d.tensor, tcol, [[T, 128], [128 * T, 8], [1, CHF]])
            dstx = intc.ap().rearrange("p (a t) -> p a t", a=8)
            p.inc(sync.dma_start(out=dstx, in_=srcx), d_ldx, 16)

        @block.gpsimd
        def _(gp):
            gp.wait_ge(c_pe, barrier["pe"])
            gp.wait_ge(c_ve, barrier["ve"])
            srcx = bass.AP(xnT_d.tensor, tcol, [[T, 128], [128 * T, 8], [1, CHF]])
            dstx = intf.ap().rearrange("p (a t) -> p a t", a=8)
            p.inc(gp.dma_start(out=dstx, in_=srcx), d_lx2, 16)
        ldx_done = p.cnt[pfx + "d_ldx"]
        lx2_done = p.cnt[pfx + "d_lx2"]

        @block.vector
        def _(vector):
            vector.wait_ge(c_pe, barrier["pe"])
            vector.wait_ge(c_se, barrier["se"])
            p.inc(vector.memset(vb[:, :], 0.0), c_ve, 1)
        v0_done = p.cnt[pfx + "c_ve"]

        def mmproj(wname, n_mc, n_kc, rhs_fn, evac_fn, wait0=None, nfree=CHF):
            sched = {}
            for mc in range(n_mc):
                ps = psA if mc % 2 == 0 else psB

                @block.tensor
                def _(tensor):
                    if mc == 0 and wait0:
                        for s_, v_ in wait0:
                            tensor.wait_ge(s_, v_)
                    if mc < 2:
                        kind, wv_ = bankstate[mc % 2]
                        tensor.wait_ge(semmap[kind], wv_)
                    else:
                        kind, wv_ = sched[mc - 2]
                        tensor.wait_ge(semmap[kind], wv_)
                    for kc in range(n_kc):
                        ins = tensor.matmul(
                            ps[:, 0:nfree],
                            wsb[wname][:, kc * (n_mc * 128) + mc * 128:
                                       kc * (n_mc * 128) + (mc + 1) * 128],
                            rhs_fn(kc),
                            start=(kc == 0), stop=(kc == n_kc - 1))
                    p.inc(ins, c_pe, 1)
                mmv = p.cnt[pfx + "c_pe"]
                sched[mc] = evac_fn(mc, ps, mmv)
                bankstate[mc % 2] = sched[mc]
            return sched

        @block.scalar
        def _(scalar):
            act_switch(scalar, nc, SET_GELU, GRD)

        def ev_gelu(dst, biascol, biasidx):
            def f(mc, ps, mmv):
                @block.scalar
                def _(scalar):
                    scalar.wait_ge(c_pe, mmv)
                    p.inc(scalar.activation(
                        out=dst(mc), in_=ps[:, 0:CHF], func=AF.Gelu,
                        bias=csb[biascol][:, biasidx(mc):biasidx(mc) + 1]),
                        c_se, 1)
                return ("se", p.cnt[pfx + "c_se"])
            return f

        mmproj("rw1l", 2, 8, lambda kc: intc[:, kc * CHF:(kc + 1) * CHF],
               ev_gelu(lambda mc: r1g[:, mc * CHF:(mc + 1) * CHF], "rb1c",
                       lambda mc: 0) if False else
               ev_gelu(lambda mc: r1g[:, mc * CHF:(mc + 1) * CHF], "rb1c",
                       lambda mc: mc) if False else
               ev_gelu(lambda mc: r1g[:, mc * CHF:(mc + 1) * CHF], "rb1c",
                       lambda mc: mc),
               wait0=[(d_ldx, ldx_done), (d_lx2, lx2_done), (c_ve, v0_done)])
        r1_done = p.cnt[pfx + "c_se"]

        if P2_STAGE < 2:
            continue
        @block.scalar
        def _(scalar):
            act_switch(scalar, nc, SET_EXP_LN, GRD)

        last_t2 = None
        for tt in range(4):
            @block.tensor
            def _(tensor):
                tensor.wait_ge(c_se, r1_done)
                if last_t2 is not None:
                    tensor.wait_ge(vch, last_t2)
                for kc in range(2):
                    tensor.matmul(psR[:, 0:8],
                                  r1g[:, kc * CHF + tt * 128:
                                      kc * CHF + (tt + 1) * 128],
                                  wsb["rw2l"][:, kc * 8:(kc + 1) * 8],
                                  start=(kc == 0), stop=False)
                p.inc(tensor.matmul(psR[:, 0:8], onesb[0:1, :], rb2b[0:1, 0:8],
                                    start=False, stop=True), c_pe, 1)
            mmv = p.cnt[pfx + "c_pe"]

            @block.vector
            def _(vector):
                vector.wait_ge(c_pe, mmv)
                w = p.inc(vector.reduce_max(tcols[:, 0:1], psR[:, 0:8],
                                            axis=AX.X, negate=True), vch, 1)
                vector.wait_ge(vch, w)
                p.inc(vector.tensor_copy(prb[:, :], psR[:, 0:8]), vch, 1)
            w_rm = p.cnt[pfx + "vch"]

            @block.scalar
            def _(scalar):
                scalar.wait_ge(vch, w_rm)
                p.inc(scalar.activation(out=prb[:, :], in_=prb[:, :],
                                        func=AF.Exp, bias=tcols[:, 0:1]),
                      c_se, 1)
            w_exp = p.cnt[pfx + "c_se"]

            @block.vector
            def _(vector):
                def step(ins):
                    wv = p.inc(ins, vch, 1)
                    vector.wait_ge(vch, wv)
                vector.wait_ge(c_se, w_exp)
                step(vector.reduce_max(tcols[:, 4:5], prb[:, :], axis=AX.X))
                step(vector.tensor_scalar(t16[:, 0:8], prb[:, :],
                                          tcols[:, 4:5], None, op0=ALU.is_ge))
                step(vector.tensor_tensor_scan(t16[:, 8:16], ones8[:, :],
                                               t16[:, 0:8], zcol[:, 0:1],
                                               op0=ALU.mult, op1=ALU.add))
                step(vector.tensor_scalar(t16[:, 8:16], t16[:, 8:16], 1.0,
                                          None, op0=ALU.is_equal))
                step(vector.tensor_mul(t16[:, 0:8], t16[:, 0:8], t16[:, 8:16]))
                step(vector.tensor_scalar(t16[:, 8:16], t16[:, 0:8], -1.0, 1.0,
                                          op0=ALU.mult, op1=ALU.add))
                step(vector.tensor_mul(prb[:, :], prb[:, :], t16[:, 8:16]))
                step(vector.reduce_max(tcols[:, 8:9], prb[:, :], axis=AX.X))
                step(vector.tensor_scalar(t16[:, 8:16], prb[:, :],
                                          tcols[:, 8:9], None, op0=ALU.is_ge))
                step(vector.tensor_tensor_scan(prb[:, :], ones8[:, :],
                                               t16[:, 8:16], zcol[:, 0:1],
                                               op0=ALU.mult, op1=ALU.add))
                step(vector.tensor_scalar(prb[:, :], prb[:, :], 1.0, None,
                                          op0=ALU.is_equal))
                step(vector.tensor_mul(t16[:, 8:16], t16[:, 8:16], prb[:, :]))
                step(vector.tensor_add(tcols[:, 12:13], tcols[:, 4:5],
                                       tcols[:, 8:9]))
                step(vector.reciprocal(tcols[:, 16:17], tcols[:, 12:13]))
                step(vector.tensor_mul(tcols[:, 20:21], tcols[:, 4:5],
                                       tcols[:, 16:17]))
                step(vector.tensor_mul(tcols[:, 24:25], tcols[:, 8:9],
                                       tcols[:, 16:17]))
            w_top = p.cnt[pfx + "vch"]

            trs = [(t16[:, 0:8], ohT[1], 8, None),
                   (t16[:, 8:16], ohT[2], 8, None),
                   (tcols[:, 20:21], w1row, 1, None),
                   (tcols[:, 24:25], w2row, 1, None)]
            for ti, (src_ap, dst_t, nn, _u) in enumerate(trs):
                @block.tensor
                def _(tensor):
                    tensor.wait_ge(vch, w_top)
                    if ti > 0:
                        tensor.wait_ge(vch, p.cnt[pfx + "vch"])
                    p.inc(tensor.transpose(psR[0:nn, 128:256], src_ap,
                                           idf[:, :]), c_pe, 1)
                w_tr = p.cnt[pfx + "c_pe"]

                @block.vector
                def _(vector):
                    vector.wait_ge(c_pe, w_tr)
                    if nn == 8:
                        p.inc(vector.tensor_copy(
                            dst_t[0:8, tt * 128:(tt + 1) * 128],
                            psR[0:8, 128:256]), vch, 1)
                    else:
                        p.inc(vector.tensor_copy(
                            dst_t[0:1, tt * 128:(tt + 1) * 128],
                            psR[0:1, 128:256]), vch, 1)
            last_t2 = p.cnt[pfx + "vch"]
        router_done = p.cnt[pfx + "vch"]

        @block.vector
        def _(vector):
            vector.wait_ge(vch, router_done)
            w = p.inc(vector.tensor_add(urow[0:1, :], w1row[0:1, :],
                                        w2row[0:1, :]), vch, 1)
            vector.wait_ge(vch, w)
            p.inc(vector.tensor_scalar(urow[0:1, :], urow[0:1, :],
                                       swc[0:1, 4:5], swc[0:1, 16:17],
                                       op0=ALU.mult, op1=ALU.add), vch, 1)
        urow_done = p.cnt[pfx + "vch"]

        for it in range(2 if P2_STAGE >= 3 else 0):
            @block.scalar
            def _(scalar):
                act_switch(scalar, nc, SET_GELU, GRD)

            def rhs_ctx(kc):
                if kc < 8:
                    return intc[:, kc * CHF:(kc + 1) * CHF]
                return vb[:, (kc - 8) * CHF:(kc - 7) * CHF]

            mmproj("w1catl", 4, 16, rhs_ctx,
                   ev_gelu(lambda mc: hat[:, mc * CHF:(mc + 1) * CHF],
                           "b1catc", lambda mc: mc))

            @block.tensor
            def _(tensor):
                kind, wv_ = bankstate[0]
                tensor.wait_ge(semmap[kind], wv_)
                for kc in range(16):
                    ins = tensor.matmul(psA[0:64, 0:CHF],
                                        wsb["sw1l"][:, kc * 64:(kc + 1) * 64],
                                        rhs_ctx(kc), start=(kc == 0),
                                        stop=(kc == 15))
                p.inc(ins, c_pe, 1)
            hs_mm = p.cnt[pfx + "c_pe"]

            @block.scalar
            def _(scalar):
                scalar.wait_ge(c_pe, hs_mm)
                p.inc(scalar.activation(out=hsT[0:64, :], in_=psA[0:64, 0:CHF],
                                        func=AF.Gelu,
                                        bias=csb["sb1c"][0:64, 0:1]), c_se, 1)
            hs_done = p.cnt[pfx + "c_se"]
            bankstate[0] = ("se", hs_done)

            mmproj("hw1l", 2, 8, lambda kc: intc[:, kc * CHF:(kc + 1) * CHF],
                   ev_gelu(lambda mc: h1g[:, mc * CHF:(mc + 1) * CHF], "hb1c",
                           lambda mc: mc))
            h1_done = p.cnt[pfx + "c_se"]

            @block.tensor
            def _(tensor):
                tensor.wait_ge(c_se, h1_done)
                kind, wv_ = bankstate[1]
                tensor.wait_ge(semmap[kind], wv_)
                for kc in range(2):
                    ins = tensor.matmul(psB[0:1, 0:CHF],
                                        wsb["hw2l"][:, kc:kc + 1],
                                        h1g[:, kc * CHF:(kc + 1) * CHF],
                                        start=(kc == 0), stop=(kc == 1))
                p.inc(ins, c_pe, 1)
            h2_mm = p.cnt[pfx + "c_pe"]

            @block.scalar
            def _(scalar):
                act_switch(scalar, nc, SET_SIG_ERF, GRD)
                scalar.wait_ge(c_pe, h2_mm)
                p.inc(scalar.activation(out=hrow[0:1, :], in_=psB[0:1, 0:CHF],
                                        func=AF.Sigmoid,
                                        bias=csb["hb2s"][0:1, 0:1]), c_se, 1)
            hrow_done = p.cnt[pfx + "c_se"]
            bankstate[1] = ("se", hrow_done)

            for s in range(NSF if P2_STAGE >= 4 else 0):
                scol = s * SUBF

                octrl_done = {}
                for k in (1, 2, 3):
                    if k == 3 and not (P2_S4 & 4):
                        continue
                    if k in (1, 2):
                        if not (P2_S4 & 1):
                            continue
                        for mc4 in range(4):
                            half = mc4 % 2

                            @block.tensor
                            def _(tensor):
                                if k == 1 and mc4 == 0:
                                    tensor.wait_ge(vch, router_done)
                                kind, wv_ = mask_hist[half]
                                tensor.wait_ge(semmap[kind], wv_)
                                p.inc(tensor.matmul(
                                    psM[half][:, 0:SUBF],
                                    selm[0:8, mc4 * 128:(mc4 + 1) * 128],
                                    ohT[k][0:8, scol:scol + SUBF],
                                    start=True, stop=True), c_pe, 1)
                            mv = p.cnt[pfx + "c_pe"]

                            @block.vector
                            def _(vector):
                                vector.wait_ge(c_pe, mv)
                                p.inc(vector.tensor_tensor(
                                    hkm[:, mc4 * SUBF:(mc4 + 1) * SUBF],
                                    hat[:, mc4 * CHF + scol:
                                        mc4 * CHF + scol + SUBF],
                                    psM[half][:, 0:SUBF],
                                    op=ALU.mult), c_ve, 1)
                            mask_hist[half] = ("ve", p.cnt[pfx + "c_ve"])
                        hk_done = p.cnt[pfx + "c_ve"]

                        def rhs_ok(kc, k=k):
                            if kc < 4:
                                return hkm[:, kc * SUBF:(kc + 1) * SUBF]
                            return ohT[k][:, scol:scol + SUBF]
                        wname = "w2augl"
                        n_kc = 5
                        wait0 = [(c_ve, hk_done)]
                    else:
                        def rhs_ok(kc):
                            return hsT[:, scol:scol + SUBF]
                        wname = "w2saugl"
                        n_kc = 1
                        wait0 = [(c_se, hs_done)]

                    def ev_ctrl(k=k):
                        def f(mc, ps, mmv):
                            @block.scalar
                            def _(scalar):
                                scalar.wait_ge(c_pe, mmv)
                                p.inc(scalar.copy(
                                    oT[k][:, mc * SUBF:(mc + 1) * SUBF],
                                    ps[:, 0:SUBF]), c_se, 1)
                            return ("se", p.cnt[pfx + "c_se"])
                        return f

                    if (P2_S4 & 2) or k == 3:
                        mmproj(wname, 24, n_kc, rhs_ok, ev_ctrl(), wait0=wait0,
                               nfree=SUBF)
                    octrl_done[k] = p.cnt[pfx + "c_se"]

                if P2_STAGE < 5:
                    continue
                if P2_S5 & 1:
                    @block.scalar
                    def _(scalar):
                        act_switch(scalar, nc, SET_SIG_ERF, GRD)
                        scalar.wait_ge(c_se, octrl_done[3])
                        for k in (1, 2, 3):
                            scalar.activation(out=oT[k][:, 0:8 * SUBF],
                                              in_=oT[k][:, 0:8 * SUBF],
                                              func=AF.Sigmoid)
                            ins = scalar.activation(
                                out=oT[k][:, 16 * SUBF:24 * SUBF],
                                in_=oT[k][:, 16 * SUBF:24 * SUBF],
                                func=AF.Sigmoid)
                        p.inc(ins, c_se, 1)
                sig_done = p.cnt[pfx + "c_se"]

                if P2_S5 & 2:
                    @block.vector
                    def _(vector):
                        vector.wait_ge(c_se, octrl_done[3])
                        for k in (1, 2, 3):
                            ins = vector.tensor_scalar_min(
                                oT[k][:, 8 * SUBF:16 * SUBF],
                                oT[k][:, 8 * SUBF:16 * SUBF], 60.0)
                        p.inc(ins, c_ve, 1)
                bc_done2 = p.cnt[pfx + "c_ve"]

                if P2_S5 & 4:
                    pe_now = p.cnt[pfx + "c_pe"]
                    ve_now = p.cnt[pfx + "c_ve"]

                    @block.scalar
                    def _(scalar):
                        if P2_EXPALT != 2:
                            scalar.wait_ge(c_pe, pe_now)
                            scalar.wait_ge(c_ve, ve_now)
                            act_switch(scalar, nc, SET_EXP_LN, GRD)
                        scalar.wait_ge(c_ve, bc_done2)
                        f1 = AF.Sigmoid if P2_EXPALT == 2 else AF.Exp
                        f2 = AF.Sigmoid if P2_EXPALT == 2 else AF.Ln
                        for k in (1, 2, 3):
                            dst1 = hat[:, 0:8 * SUBF] if P2_EXPALT == 1 \
                                else oT[k][:, 8 * SUBF:16 * SUBF]
                            w1_ = p.inc(scalar.activation(
                                out=dst1,
                                in_=oT[k][:, 8 * SUBF:16 * SUBF],
                                func=f1), c_se, 1)
                            scalar.wait_ge(c_se, w1_)
                            p.inc(scalar.activation(
                                out=oT[k][:, 8 * SUBF:16 * SUBF],
                                in_=dst1, func=f2,
                                bias=1.0), c_se, 1)
                sp_done = p.cnt[pfx + "c_se"]

                if P2_STAGE < 6:
                    continue
                @block.tensor
                def _(tensor):
                    tensor.wait_ge(vch, urow_done)
                    tensor.wait_ge(c_se, hrow_done)
                    tensor.wait_ge(c_ve, octrl_done.get("combfree", 0))
                    tensor.matmul(psbc[:, 0:256], onesbf[0:1, :],
                                  w1row[0:1, scol:scol + SUBF], start=True,
                                  stop=False)
                    tensor.matmul(psbc[:, 256:512], onesbf[0:1, :],
                                  w2row[0:1, scol:scol + SUBF], start=False,
                                  stop=True)
                    tensor.matmul(psbc[:, 512:768], onesbf[0:1, :],
                                  urow[0:1, scol:scol + SUBF], start=True,
                                  stop=False)
                    p.inc(tensor.matmul(psbc[:, 768:1024], onesbf[0:1, :],
                                        hrow[0:1, scol:scol + SUBF],
                                        start=False, stop=True), c_pe, 1)
                bc2 = p.cnt[pfx + "c_pe"]

                @block.vector
                def _(vector):
                    vector.wait_ge(c_se, sig_done)
                    vector.wait_ge(c_se, sp_done)
                    vector.wait_ge(c_pe, bc2)
                    for dc in range(8):
                        i_sl = intf[:, dc * CHF + scol: dc * CHF + scol + SUBF]
                        ic_sl = intc[:, dc * CHF + scol: dc * CHF + scol + SUBF]
                        v_sl = vb[:, dc * CHF + scol: dc * CHF + scol + SUBF]
                        dsl = slice(dc * SUBF, (dc + 1) * SUBF)
                        vector.tensor_scalar(scr["xmu"][:, :], i_sl,
                                             csb["muc"][:, dc:dc + 1], None,
                                             op0=ALU.subtract)
                        for k, vr, tk in ((1, "vr1", "tk1"), (2, "vr2", "tk2"),
                                          (3, "vrs", "tks")):
                            vector.tensor_tensor(scr["t1"][:, :],
                                                 oT[k][:, dsl], v_sl,
                                                 op=ALU.mult)
                            vector.tensor_tensor(
                                scr["t2"][:, :],
                                oT[k][:, 8 * SUBF + dc * SUBF:
                                      8 * SUBF + (dc + 1) * SUBF],
                                scr["xmu"][:, :], op=ALU.mult)
                            vector.tensor_tensor(scr[vr][:, :], scr["t1"][:, :],
                                                 scr["t2"][:, :],
                                                 op=ALU.subtract)
                            vector.tensor_tensor(
                                scr[tk][:, :],
                                oT[k][:, 16 * SUBF + dc * SUBF:
                                      16 * SUBF + (dc + 1) * SUBF],
                                scr[vr][:, :], op=ALU.mult)
                        vector.tensor_tensor(scr["t1"][:, :], scr["vr1"][:, :],
                                             psbc[:, 0:256], op=ALU.mult)
                        vector.tensor_tensor(scr["t2"][:, :], scr["vr2"][:, :],
                                             psbc[:, 256:512], op=ALU.mult)
                        vector.tensor_tensor(scr["vacc"][:, :], scr["t1"][:, :],
                                             scr["t2"][:, :], op=ALU.add)
                        vector.tensor_tensor(scr["t1"][:, :], scr["tk1"][:, :],
                                             psbc[:, 0:256], op=ALU.mult)
                        vector.tensor_tensor(scr["t2"][:, :], scr["tk2"][:, :],
                                             psbc[:, 256:512], op=ALU.mult)
                        vector.tensor_tensor(scr["sacc"][:, :], scr["t1"][:, :],
                                             scr["t2"][:, :], op=ALU.add)
                        vector.tensor_tensor(scr["t1"][:, :], i_sl,
                                             psbc[:, 512:768], op=ALU.mult)
                        vector.scalar_tensor_tensor(scr["t2"][:, :],
                                                    scr["tks"][:, :],
                                                    swc[:, 8:9],
                                                    scr["t1"][:, :],
                                                    op0=ALU.mult, op1=ALU.add)
                        vector.scalar_tensor_tensor(scr["dx"][:, :],
                                                    scr["sacc"][:, :],
                                                    swc[:, 12:13],
                                                    scr["t2"][:, :],
                                                    op0=ALU.mult, op1=ALU.add)
                        vector.scalar_tensor_tensor(scr["q"][:, :],
                                                    scr["vacc"][:, :],
                                                    swc[:, 4:5], v_sl,
                                                    op0=ALU.mult,
                                                    op1=ALU.subtract)
                        vector.scalar_tensor_tensor(scr["t2"][:, :],
                                                    scr["vrs"][:, :],
                                                    swc[:, 0:1], scr["q"][:, :],
                                                    op0=ALU.mult, op1=ALU.add)
                        vector.tensor_tensor(scr["t1"][:, :], scr["dx"][:, :],
                                             psbc[:, 768:1024], op=ALU.mult)
                        vector.tensor_tensor(i_sl, i_sl, scr["t1"][:, :],
                                             op=ALU.add)
                        vector.tensor_copy(ic_sl, i_sl)
                        vector.tensor_tensor(scr["t1"][:, :], scr["t2"][:, :],
                                             psbc[:, 768:1024], op=ALU.mult)
                        ins = vector.tensor_tensor(v_sl, v_sl, scr["t1"][:, :],
                                                   op=ALU.add)
                    p.inc(ins, c_ve, 1)
                octrl_done["combfree"] = p.cnt[pfx + "c_ve"]
                octrl_done["maskfree"] = p.cnt[pfx + "c_ve"]

        if P2_STAGE < 7:
            continue
        pe_before_ref = p.cnt[pfx + "c_pe"]

        @block.gpsimd
        def _(gp):
            gp.wait_ge(c_pe, pe_before_ref)
            p.inc(gp.dma_start(out=wsb["refw1l"].ap(), in_=wd["refw1l"]),
                  d_wl, 16)
        rw1_ld = p.cnt[pfx + "d_wl"]

        @block.scalar
        def _(scalar):
            act_switch(scalar, nc, SET_GELU, GRD)
        mmproj("refw1l", 16, 8, lambda kc: intc[:, kc * CHF:(kc + 1) * CHF],
               ev_gelu(lambda mc: ref1g_sl(mc), "refb1c", lambda mc: mc),
               wait0=[(d_wl, rw1_ld),
                      (c_ve, octrl_done.get("combfree", 0))])
        ref1_done = p.cnt[pfx + "c_se"]
        ref1_pe = p.cnt[pfx + "c_pe"]

        @block.gpsimd
        def _(gp):
            gp.wait_ge(c_pe, ref1_pe)
            p.inc(gp.dma_start(out=wsb["refw2l"].ap(), in_=wd["refw2l"]),
                  d_wl, 16)
        rw2_ld = p.cnt[pfx + "d_wl"]

        out_dma = {}

        def ev_ref2(mc, ps, mmv):
            @block.vector
            def _(vector):
                vector.wait_ge(c_pe, mmv)
                if out_dma.get(mc % 2):
                    vector.wait_ge(d_out, out_dma[mc % 2])
                p.inc(vector.tensor_scalar(fout[mc % 2][:, :], ps[:, 0:CHF],
                                           csb["refb2c"][:, mc:mc + 1],
                                           csb["iwc"][:, mc:mc + 1],
                                           op0=ALU.add, op1=ALU.mult), c_ve, 1)
            w_ = p.cnt[pfx + "c_ve"]

            @block.gpsimd
            def _(gp):
                gp.wait_ge(c_ve, w_)
                p.inc(gp.dma_start(
                    out=bass.AP(fT_d.tensor, mc * 128 * T + c * CHF,
                                [[T, 128], [1, CHF]]),
                    in_=fout[mc % 2][:, :]), d_out, 16)
            out_dma[mc % 2] = p.cnt[pfx + "d_out"]
            return ("ve", w_)

        mmproj("refw2l", 8, 16, ref1g_rhs, ev_ref2,
               wait0=[(c_se, ref1_done), (d_wl, rw2_ld)])

    @block.sync
    def _(sync):
        sync.wait_ge(d_out, p.cnt[pfx + "d_out"])


def build_phase2():
    p = P()
    nc = p.nc
    xnT_d = nc.dram_tensor("xn0T", [D, T], BF16, kind="ExternalInput").ap()
    wd = {n: nc.dram_tensor(n, FFN_SHAPES[n], F32, kind="ExternalInput").ap()
          for n in FFN_W_BF + FFN_W_F32}
    fT_d = nc.dram_tensor("f0T", [D, T], F32, kind="ExternalOutput").ap()
    with p.stack:
        with nc.Block() as block:
            emit_ffn0(p, block, xnT_d, wd, fT_d, pfx="f0_")
    return p.nc


def emit_assemble(p, block, x_d, xn0_d, aT_d, fT_d, out_d, pfx):
    nc = p.nc
    xs = [p.sbuf(f"{pfx}xs{i}", [128, D], F32) for i in range(2)]
    xn0s = [p.sbuf(f"{pfx}xn0s{i}", [128, D], F32) for i in range(2)] \
        if xn0_d is not None else None
    x1 = [p.sbuf(f"{pfx}x1{i}", [128, D], F32) for i in range(2)]
    aTs = [p.sbuf(f"{pfx}aTs{i}", [128, D], F32) for i in range(2)]
    fTs = [p.sbuf(f"{pfx}fTs{i}", [128, D], F32) for i in range(2)]
    idf = p.sbuf(f"{pfx}idf", [128, 128], F32)
    pt = [p.psum(f"{pfx}pt{i}", [128, 128], F32) for i in range(4)]

    ld = p.sem(pfx + "ld")
    ldg = p.sem(pfx + "ldg")
    so = p.sem(pfx + "so")
    cv = p.sem(pfx + "cv")
    cp = p.sem(pfx + "cp")
    gpi = p.sem(pfx + "gpi")

    nld = 32 if xn0_d is not None else 16

    @block.gpsimd
    def _(gp):
        gp.memset(idf[:, :], 1.0)
        p.inc(gp.affine_select(idf[:, :], idf[:, :], pattern=[[-1, 128]],
                               compare_op=ALU.is_equal, fill=0.0, base=0,
                               channel_multiplier=1), gpi, 1)
        for rt in range(NT):
            b = rt % 2
            if rt >= 2:
                gp.wait_ge(cp, 16 * (rt - 1))
            src_a = bass.AP(aT_d.tensor, rt * 128,
                            [[T, 128], [128 * T, 8], [1, 128]])
            p.inc(gp.dma_start(out=aTs[b].ap().rearrange("p (a t) -> p a t", a=8),
                               in_=src_a), ldg, 16)
            src_f = bass.AP(fT_d.tensor, rt * 128,
                            [[T, 128], [128 * T, 8], [1, 128]])
            p.inc(gp.dma_start(out=fTs[b].ap().rearrange("p (a t) -> p a t", a=8),
                               in_=src_f), ldg, 16)

    @block.sync
    def _(sync):
        for rt in range(NT):
            b = rt % 2
            if rt >= 2:
                sync.wait_ge(cv, 5 * (rt - 1))
            p.inc(sync.dma_start(out=xs[b][:, :],
                                 in_=x_d[rt * 128:(rt + 1) * 128, :]), ld, 16)
            if xn0_d is not None:
                p.inc(sync.dma_start(out=xn0s[b][:, :],
                                     in_=xn0_d[rt * 128:(rt + 1) * 128, :]),
                      ld, 16)

    for rt in range(NT):
        b = rt % 2

        @block.tensor
        def _(tensor):
            if rt == 0:
                tensor.wait_ge(gpi, 1)
            tensor.wait_ge(ldg, 32 * (rt + 1))
            for w in range(4):
                srcT = aTs[b] if w < 2 else fTs[b]
                if rt * 4 + w >= 1:
                    tensor.wait_ge(cv, 5 * rt + w + 1 if w >= 1
                                   else 5 * (rt - 1) + 5)
                for i in range(4):
                    c8 = (w % 2) * 4 + i
                    p.inc(tensor.transpose(
                        pt[i][:, :], srcT[:, c8 * 128:(c8 + 1) * 128],
                        idf[:, :]), cp, 1)

        @block.vector
        def _(vector):
            vector.wait_ge(ld, nld * (rt + 1))
            if rt >= 2:
                vector.wait_ge(so, 16 * (rt - 1))
            if xn0_d is not None:
                ins = vector.tensor_add(x1[b][:, :], xs[b][:, :],
                                        xn0s[b][:, :])
            else:
                ins = vector.tensor_copy(x1[b][:, :], xs[b][:, :])
            p.inc(ins, cv, 1)
            for w in range(4):
                vector.wait_ge(cp, 16 * rt + 4 * (w + 1))
                for i in range(4):
                    c8 = (w % 2) * 4 + i
                    ins = vector.tensor_add(x1[b][:, c8 * 128:(c8 + 1) * 128],
                                            x1[b][:, c8 * 128:(c8 + 1) * 128],
                                            pt[i][:, :])
                    if i == 3:
                        p.inc(ins, cv, 1)

        @block.scalar
        def _(scalar):
            scalar.wait_ge(cv, 5 * (rt + 1))
            p.inc(scalar.dma_start(out=out_d[rt * 128:(rt + 1) * 128, :],
                                   in_=x1[b][:, :]), so, 16)

    @block.sync
    def _(sync):
        sync.wait_ge(so, p.cnt[pfx + "so"])


def build_phase3a():
    p = P()
    nc = p.nc
    x_d = nc.dram_tensor("x", [T, D], F32, kind="ExternalInput").ap()
    xn0_d = nc.dram_tensor("xn0", [T, D], F32, kind="ExternalInput").ap()
    aT_d = nc.dram_tensor("attn0T", [D, T], F32, kind="ExternalInput").ap()
    fT_d = nc.dram_tensor("f0T", [D, T], F32, kind="ExternalInput").ap()
    x1_d = nc.dram_tensor("x1", [T, D], F32, kind="ExternalOutput").ap()
    with p.stack:
        with nc.Block() as block:
            emit_assemble(p, block, x_d, xn0_d, aT_d, fT_d, x1_d, pfx="p3a_")
    return p.nc


def build_phase3b():
    p = P()
    nc = p.nc
    x1_d = nc.dram_tensor("x1", [T, D], F32, kind="ExternalInput").ap()
    xn_d = nc.dram_tensor("xn1", [T, D], F32, kind="ExternalOutput").ap()
    xnTb_d = nc.dram_tensor("xn1Tb", [D, T], BF16, kind="ExternalOutput").ap()
    xnTf_d = nc.dram_tensor("xn1Tf", [D, T], F32, kind="ExternalOutput").ap()
    with p.stack:
        with nc.Block() as block:
            emit_ln_transpose(p, block, x1_d, xn_d, xnTb_d, pfx="p3b_",
                              xnTf_d=xnTf_d)
    return p.nc


def build_phase6():
    p = P()
    nc = p.nc
    x1_d = nc.dram_tensor("x1", [T, D], F32, kind="ExternalInput").ap()
    aT_d = nc.dram_tensor("attn1T", [D, T], F32, kind="ExternalInput").ap()
    fT_d = nc.dram_tensor("ffn1T", [D, T], F32, kind="ExternalInput").ap()
    out_d = nc.dram_tensor("out", [T, D], F32, kind="ExternalOutput").ap()
    with p.stack:
        with nc.Block() as block:
            emit_assemble(p, block, x1_d, None, aT_d, fT_d, out_d, pfx="p6_")
    return p.nc


CH4 = 512
NC4 = T // CH4
INV2PI = 0.15915494309189535
PIC1 = 6.28125
PIC2 = 0.0019353071795862605
PIH = 1.5707963267948966
P4_DBG = 0


def prep_l1a_weights(g):
    def lay(w):
        w = np.ascontiguousarray(np.asarray(w, np.float32))
        i, o = w.shape
        return np.ascontiguousarray(
            w.reshape(i // 128, 128, o).transpose(1, 0, 2).reshape(128, -1))

    def col(v):
        v = np.asarray(v, np.float32).reshape(-1)
        out = np.zeros((128, max(1, (len(v) + 127) // 128)), np.float32)
        for i, x in enumerate(v):
            out[i % 128, i // 128] = x
        return out

    return {
        "WoT": lay(np.asarray(g["W_omega"], np.float32).T).astype(BFNP),
        "W1T": lay(np.asarray(g["W1"], np.float32).T).astype(BFNP),
        "W2T": lay(np.asarray(g["W2"], np.float32).T).astype(BFNP),
        "bomc": col(g["b_omega"]), "lsc": col(g["log_scale"]),
        "lgc": col(g["ln_gamma"]), "lbc": col(g["ln_beta"]),
        "b1c": col(g["b1"]), "b2c": col(g["b2"]),
    }


L1A_W_BF = ["WoT", "W1T", "W2T"]
L1A_W_F32 = ["bomc", "lsc", "lgc", "lbc", "b1c", "b2c"]
L1A_SHAPES = {"WoT": [128, 8192], "W1T": [128, 65536], "W2T": [128, 16384],
              "bomc": [128, 8], "lsc": [128, 8], "lgc": [128, 32],
              "lbc": [128, 32], "b1c": [128, 16], "b2c": [128, 8]}


def emit_attn1(p, block, xnTb_d, xnTf_d, wd, outT_d, pfx="a1_"):
    nc = p.nc
    wbufA = p.sbuf(pfx + "wbufA", [128, 16384], BF16)
    wbufB = p.sbuf(pfx + "wbufB", [128, 16384], BF16)
    cols = {n: p.sbuf(pfx + n, L1A_SHAPES[n], F32) for n in L1A_W_F32}
    exls = p.sbuf(pfx + "exls", [128, 8], F32)
    xnb = p.sbuf(pfx + "xnb", [128, 8 * CH4], BF16)
    xnf = p.sbuf(pfx + "xnf", [128, 8 * CH4], F32)
    omsc = p.sbuf(pfx + "omsc", [128, CH4], F32)
    phis = p.sbuf(pfx + "phis", [128, CH4], F32)
    sphi = p.sbuf(pfx + "sphi", [128, CH4], BF16)
    cphi = p.sbuf(pfx + "cphi", [128, CH4], BF16)
    ctxb = p.sbuf(pfx + "ctxb", [128, 32 * CH4], BF16)
    h1b = p.sbuf(pfx + "h1b", [128, 16 * CH4], BF16)
    mem = [p.sbuf(pfx + f"mem{i}", [128, CH4], F32) for i in range(2)]
    cumt = p.sbuf(pfx + "cumt", [128, CH4], F32)
    rsc = p.sbuf(pfx + "rsc", [128, CH4], F32)
    posv = [p.sbuf(pfx + f"posv{i}", [128, CH4], F32) for i in range(2)]
    onescan = p.sbuf(pfx + "onescan", [128, CH4], F32)
    iotc = p.sbuf(pfx + "iotc", [128, 4], F32)
    kint = p.sbuf(pfx + "kint", [128, CH4], I32)
    carr = p.sbuf(pfx + "carr", [128, 24], F32)
    lmu = p.sbuf(pfx + "lmu", [1, CH4], F32)
    lrs = p.sbuf(pfx + "lrs", [1, CH4], F32)
    ones128b = p.sbuf(pfx + "ones128b", [128, 1], BF16)
    onesbf = p.sbuf(pfx + "onesbf", [1, 128], F32)
    pihc = p.sbuf(pfx + "pihc", [128, 1], F32)
    epsc = p.sbuf(pfx + "epsc", [1, 1], F32)
    outstg = [p.sbuf(pfx + f"outstg{i}", [128, CH4], F32) for i in range(2)]
    gcell = p.sbuf(pfx + "gcell", [1, 4], F32)

    psA = p.psum(pfx + "psA", [128, 512], F32)
    psB = p.psum(pfx + "psB", [128, 512], F32)
    psS = p.psum(pfx + "psS", [128, 512], F32)
    psQ = p.psum(pfx + "psQ", [128, 512], F32)
    psbc = p.psum(pfx + "psbc", [128, 512], F32)
    psbc2 = p.psum(pfx + "psbc2", [128, 512], F32)

    c_pe = p.sem(pfx + "c_pe")
    c_ve = p.sem(pfx + "c_ve")
    c_se = p.sem(pfx + "c_se")
    vch = p.sem(pfx + "vch")
    d_wl = p.sem(pfx + "d_wl")
    d_ws = p.sem(pfx + "d_ws")
    d_ldx = p.sem(pfx + "d_ldx")
    d_out = p.sem(pfx + "d_out")
    gpi = p.sem(pfx + "gpi")
    gsem = p.sem(pfx + "gsem")
    GRD = (p, gcell[0:1, 0:1], gsem)
    semmap = {"se": c_se, "ve": c_ve, "pe": c_pe}
    bankstate = {0: ("se", 0), 1: ("se", 0)}
    busyA = 0
    busyB = 0
    out_hist = [0, 0]

    @block.gpsimd
    def _(gp):
        for n in L1A_W_F32:
            p.inc(gp.dma_start(out=cols[n].ap(), in_=wd[n]), d_wl, 16)
        gp.memset(carr[:, :], 0.0)
        gp.memset(ones128b[:, :], 1.0)
        gp.memset(onesbf[:, :], 1.0)
        gp.memset(onescan[:, :], 1.0)
        gp.memset(pihc[:, :], PIH)
        p.inc(gp.memset(epsc[:, :], LN_EPS), gpi, 1)

    @block.scalar
    def _(scalar):
        act_switch(scalar, nc, SET_EXP_LN)
        scalar.wait_ge(d_wl, p.cnt[pfx + "d_wl"])
        scalar.wait_ge(gpi, 1)
        p.inc(scalar.activation(out=exls[:, :], in_=cols["lsc"][:, :],
                                func=AF.Exp), c_se, 1)
    exls_done = p.cnt[pfx + "c_se"]

    for c in range(NC4):
        tcol = c * CH4
        barrier = dict(pe=p.cnt[pfx + "c_pe"], ve=p.cnt[pfx + "c_ve"],
                       se=p.cnt[pfx + "c_se"], vh=p.cnt[pfx + "vch"])

        @block.sync
        def _(sync):
            sync.wait_ge(c_pe, barrier["pe"])
            sync.wait_ge(c_ve, barrier["ve"])
            sync.wait_ge(c_se, barrier["se"])
            sync.wait_ge(vch, barrier["vh"])
            srcb = bass.AP(xnTb_d.tensor, tcol, [[T, 128], [128 * T, 8], [1, CH4]])
            p.inc(sync.dma_start(out=xnb.ap().rearrange("p (a t) -> p a t", a=8),
                                 in_=srcb), d_ldx, 16)
            srcf = bass.AP(xnTf_d.tensor, tcol, [[T, 128], [128 * T, 8], [1, CH4]])
            p.inc(sync.dma_start(out=xnf.ap().rearrange("p (a t) -> p a t", a=8),
                                 in_=srcf), d_ldx, 16)
        ldx_done = p.cnt[pfx + "d_ldx"]

        @block.gpsimd
        def _(gp):
            if busyA:
                gp.wait_ge(c_pe, busyA)
            p.inc(gp.dma_start(out=wbufA[:, 0:8192], in_=wd["WoT"]), d_ws, 16)
            p.inc(gp.memset(iotc[:, 0:1], float(tcol)), gpi, 1)
            if busyB:
                gp.wait_ge(c_pe, busyB)
            src = bass.AP(wd["W1T"].tensor, 0 * 512,
                          [[65536, 128], [2048, 32], [1, 512]])
            p.inc(gp.dma_start(out=wbufB.ap().rearrange("p (a t) -> p a t",
                                                        a=32), in_=src),
                  d_ws, 16)
        wot_ld = p.cnt[pfx + "d_ws"] - 16
        g_ld = {0: p.cnt[pfx + "d_ws"]}
        iota_done = p.cnt[pfx + "gpi"]

        @block.vector
        def _(vector):
            vector.wait_ge(gpi, iota_done)
            p.inc(vector.tensor_tensor_scan(posv[0][:, :], onescan[:, :],
                                            onescan[:, :], iotc[:, 0:1],
                                            op0=ALU.mult, op1=ALU.add),
                  vch, 1)
        posc_ve = p.cnt[pfx + "vch"]

        @block.scalar
        def _(scalar):
            act_switch(scalar, nc, SET_EXP_LN, GRD)
            scalar.wait_ge(vch, posc_ve)
            w = p.inc(scalar.activation(out=posv[1][:, :], in_=posv[0][:, :],
                                        func=AF.Ln), c_se, 1)
            scalar.wait_ge(c_se, w)
            p.inc(scalar.activation(out=posv[0][:, :], in_=posv[1][:, :],
                                    func=AF.Exp, scale=-0.5), c_se, 1)
        pos_se = p.cnt[pfx + "c_se"]

        @block.vector
        def _(vector):
            vector.wait_ge(c_se, pos_se)
            p.inc(vector.tensor_mul(posv[1][:, :], posv[0][:, :],
                                    posv[0][:, :]), vch, 1)
        pos_done = p.cnt[pfx + "vch"]

        @block.scalar
        def _(scalar):
            act_switch(scalar, nc, SET_SIN, GRD)

        omsc_free = 0
        dve2_done = 0
        for mc in range(8):
            ps = psA if mc % 2 == 0 else psB

            @block.tensor
            def _(tensor):
                if mc == 0:
                    tensor.wait_ge(d_ldx, ldx_done)
                    tensor.wait_ge(d_ws, wot_ld)
                kind, wv_ = bankstate[mc % 2]
                tensor.wait_ge(semmap[kind], wv_)
                for kc in range(8):
                    ins = tensor.matmul(
                        ps[:, :], wbufA[:, kc * 1024 + mc * 128:
                                        kc * 1024 + (mc + 1) * 128],
                        xnb[:, kc * CH4:(kc + 1) * CH4],
                        start=(kc == 0), stop=(kc == 7))
                p.inc(ins, c_pe, 1)
            mmv = p.cnt[pfx + "c_pe"]

            @block.scalar
            def _(scalar):
                scalar.wait_ge(c_pe, mmv)
                if mc == 0:
                    scalar.wait_ge(c_se, exls_done)
                if omsc_free:
                    scalar.wait_ge(vch, omsc_free)
                p.inc(scalar.activation(out=omsc[:, :], in_=ps[:, :],
                                        func=AF.Identity,
                                        bias=cols["bomc"][:, mc:mc + 1]),
                      c_se, 1)
            evv = p.cnt[pfx + "c_se"]
            bankstate[mc % 2] = ("se", evv)

            @block.vector
            def _(vector):
                vector.wait_ge(c_se, evv)
                if mc == 0:
                    vector.wait_ge(vch, pos_done)
                vector.tensor_scalar(omsc[:, :], omsc[:, :],
                                     exls[:, mc:mc + 1], None, op0=ALU.mult)
                vector.tensor_mul(omsc[:, :], omsc[:, :], posv[0][:, :])
                w = p.inc(vector.tensor_tensor_scan(
                    phis[:, :], onescan[:, :], omsc[:, :],
                    carr[:, mc:mc + 1], op0=ALU.mult, op1=ALU.add), vch, 1)
                vector.wait_ge(vch, w)
                vector.tensor_copy(carr[:, mc:mc + 1], phis[:, CH4 - 1:CH4])
                vector.tensor_scalar_mul(rsc[:, :], phis[:, :], INV2PI)
                vector.tensor_copy(kint[:, :], rsc[:, :])
                vector.tensor_copy(rsc[:, :], kint[:, :])
                vector.scalar_tensor_tensor(phis[:, :], rsc[:, :], -PIC1,
                                            phis[:, :], op0=ALU.mult,
                                            op1=ALU.add)
                p.inc(vector.scalar_tensor_tensor(phis[:, :], rsc[:, :],
                                                  -PIC2, phis[:, :],
                                                  op0=ALU.mult, op1=ALU.add),
                      vch, 1)
            omsc_free = p.cnt[pfx + "vch"] - 1
            rr_done = p.cnt[pfx + "vch"]

            @block.scalar
            def _(scalar):
                scalar.wait_ge(vch, rr_done)
                p.inc(scalar.activation(out=sphi[:, :], in_=phis[:, :],
                                        func=AF.Sin), c_se, 1)
                w = p.inc(scalar.activation(out=rsc[:, :], in_=phis[:, :],
                                            func=AF.Abs), c_se, 1)
                scalar.wait_ge(c_se, w)
                p.inc(scalar.activation(out=cphi[:, :], in_=rsc[:, :],
                                        func=AF.Sin, scale=-1.0,
                                        bias=pihc[:, 0:1]), c_se, 1)
            trig_done = p.cnt[pfx + "c_se"]

            @block.vector
            def _(vector):
                vector.wait_ge(c_se, trig_done)
                csl = slice(mc * CH4, (mc + 1) * CH4)
                vector.tensor_mul(cumt[:, :], xnf[:, csl], cphi[:, :])
                vector.tensor_copy(ctxb[:, 0 * 4096 + mc * CH4:
                                        0 * 4096 + (mc + 1) * CH4],
                                   cumt[:, :])
                w = p.inc(vector.tensor_tensor_scan(
                    rsc[:, :], onescan[:, :], cumt[:, :],
                    carr[:, 8 + mc:9 + mc], op0=ALU.mult, op1=ALU.add), vch, 1)
                vector.wait_ge(vch, w)
                vector.tensor_copy(carr[:, 8 + mc:9 + mc],
                                   rsc[:, CH4 - 1:CH4])
                vector.tensor_mul(mem[0][:, :], rsc[:, :], posv[1][:, :])
                vector.tensor_mul(cumt[:, :], xnf[:, csl], sphi[:, :])
                vector.tensor_copy(ctxb[:, 1 * 4096 + mc * CH4:
                                        1 * 4096 + (mc + 1) * CH4],
                                   cumt[:, :])
                w = p.inc(vector.tensor_tensor_scan(
                    rsc[:, :], onescan[:, :], cumt[:, :],
                    carr[:, 16 + mc:17 + mc], op0=ALU.mult, op1=ALU.add),
                    vch, 1)
                vector.wait_ge(vch, w)
                vector.tensor_copy(carr[:, 16 + mc:17 + mc],
                                   rsc[:, CH4 - 1:CH4])
                vector.tensor_mul(mem[1][:, :], rsc[:, :], posv[1][:, :])
                vector.tensor_mul(cumt[:, :], mem[0][:, :], cphi[:, :])
                vector.tensor_mul(rsc[:, :], mem[1][:, :], sphi[:, :])
                vector.tensor_add(ctxb[:, 2 * 4096 + mc * CH4:
                                       2 * 4096 + (mc + 1) * CH4],
                                  cumt[:, :], rsc[:, :])
                vector.tensor_mul(cumt[:, :], mem[1][:, :], cphi[:, :])
                vector.tensor_mul(rsc[:, :], mem[0][:, :], sphi[:, :])
                p.inc(vector.tensor_sub(ctxb[:, 3 * 4096 + mc * CH4:
                                             3 * 4096 + (mc + 1) * CH4],
                                        cumt[:, :], rsc[:, :]), vch, 1)
            dve2_done = p.cnt[pfx + "vch"]
        omega_mms = p.cnt[pfx + "c_pe"]
        busyA = omega_mms
        ctx_done = dve2_done

        if P4_DBG & 2 and c == 0:
            dbgw_d = nc.dram_tensor("dbgwot", [128, 8192], BF16,
                                    kind="ExternalOutput").ap()
            dbgxb_d = nc.dram_tensor("dbgxnb", [128, 8 * CH4], BF16,
                                     kind="ExternalOutput").ap()

            @block.sync
            def _(sync):
                sync.wait_ge(c_pe, omega_mms)
                p.inc(sync.dma_start(out=dbgw_d[:, :], in_=wbufA[:, 0:8192]),
                      d_out, 16)
                p.inc(sync.dma_start(out=dbgxb_d[:, :], in_=xnb[:, :]),
                      d_out, 16)

        @block.gpsimd
        def _(gp):
            gp.wait_ge(c_pe, omega_mms)
            if P4_DBG & 2 and c == 0:
                gp.wait_ge(d_out, p.cnt[pfx + "d_out"])
            src = bass.AP(wd["W1T"].tensor, 1 * 512,
                          [[65536, 128], [2048, 32], [1, 512]])
            p.inc(gp.dma_start(out=wbufA.ap().rearrange("p (a t) -> p a t",
                                                        a=32), in_=src),
                  d_ws, 16)
        g_ld[1] = p.cnt[pfx + "d_ws"]

        if P4_DBG and c == 0:
            dbgc_d = nc.dram_tensor("dbgctx", [128, 32 * CH4], BF16,
                                    kind="ExternalOutput").ap()
            dbgp_d = nc.dram_tensor("dbgpos", [128, 8 * CH4], F32,
                                    kind="ExternalOutput").ap()
            dbgx_d = nc.dram_tensor("dbgxnf", [128, 8 * CH4], F32,
                                    kind="ExternalOutput").ap()
            dbgt_d = nc.dram_tensor("dbgtrig", [128, 2 * CH4], BF16,
                                    kind="ExternalOutput").ap()

            @block.sync
            def _(sync):
                sync.wait_ge(vch, ctx_done)
                p.inc(sync.dma_start(out=dbgc_d[:, :], in_=ctxb[:, :]),
                      d_out, 16)
                p.inc(sync.dma_start(out=dbgp_d[:, 0:CH4], in_=posv[0][:, :]),
                      d_out, 16)
                p.inc(sync.dma_start(out=dbgp_d[:, CH4:2 * CH4],
                                     in_=posv[1][:, :]), d_out, 16)
                p.inc(sync.dma_start(out=dbgp_d[:, 2 * CH4:3 * CH4],
                                     in_=phis[:, :]), d_out, 16)
                p.inc(sync.dma_start(out=dbgp_d[:, 3 * CH4:4 * CH4],
                                     in_=omsc[:, :]), d_out, 16)
                p.inc(sync.dma_start(out=dbgp_d[:, 4 * CH4:5 * CH4],
                                     in_=mem[0][:, :]), d_out, 16)
                p.inc(sync.dma_start(out=dbgp_d[:, 5 * CH4:6 * CH4],
                                     in_=mem[1][:, :]), d_out, 16)
                p.inc(sync.dma_start(out=dbgx_d[:, :], in_=xnf[:, :]),
                      d_out, 16)
                p.inc(sync.dma_start(out=dbgt_d[:, 0:CH4], in_=sphi[:, :]),
                      d_out, 16)
                p.inc(sync.dma_start(out=dbgt_d[:, CH4:2 * CH4],
                                     in_=cphi[:, :]), d_out, 16)

        @block.scalar
        def _(scalar):
            act_switch(scalar, nc, SET_EXP_LN, GRD)
        sq_mm = {}
        for cc in range(32):
            sqb = h1b[:, (cc % 2) * CH4:(cc % 2 + 1) * CH4]

            @block.scalar
            def _(scalar):
                if cc == 0:
                    scalar.wait_ge(vch, ctx_done)
                if cc >= 2:
                    scalar.wait_ge(c_pe, sq_mm[cc - 2])
                p.inc(scalar.activation(out=sqb,
                                        in_=ctxb[:, cc * CH4:(cc + 1) * CH4],
                                        func=AF.Square), c_se, 1)
            sqv = p.cnt[pfx + "c_se"]

            @block.tensor
            def _(tensor):
                if cc == 0:
                    tensor.wait_ge(vch, ctx_done)
                tensor.matmul(psS[0:1, :], ones128b[:, 0:1],
                              ctxb[:, cc * CH4:(cc + 1) * CH4],
                              start=(cc == 0), stop=(cc == 31))
                tensor.wait_ge(c_se, sqv)
                p.inc(tensor.matmul(psQ[0:1, :], ones128b[:, 0:1], sqb,
                                    start=(cc == 0), stop=(cc == 31)),
                      c_pe, 1)
            sq_mm[cc] = p.cnt[pfx + "c_pe"]
        sums_done = p.cnt[pfx + "c_pe"]

        @block.vector
        def _(vector):
            vector.wait_ge(c_pe, sums_done)
            w = p.inc(vector.tensor_scalar_mul(lmu[0:1, :], psS[0:1, :],
                                               1.0 / 4096.0), vch, 1)
            vector.wait_ge(vch, w)
            w = p.inc(vector.tensor_mul(cumt[0:1, :], lmu[0:1, :],
                                        lmu[0:1, :]), vch, 1)
            vector.wait_ge(vch, w)
            p.inc(vector.scalar_tensor_tensor(rsc[0:1, :], psQ[0:1, :],
                                              1.0 / 4096.0, cumt[0:1, :],
                                              op0=ALU.mult,
                                              op1=ALU.subtract), vch, 1)
        var_done = p.cnt[pfx + "vch"]

        @block.scalar
        def _(scalar):
            scalar.wait_ge(vch, var_done)
            w = p.inc(scalar.activation(out=cumt[0:1, :], in_=rsc[0:1, :],
                                        func=AF.Ln, bias=epsc[0:1, 0:1]),
                      c_se, 1)
            scalar.wait_ge(c_se, w)
            p.inc(scalar.activation(out=lrs[0:1, :], in_=cumt[0:1, :],
                                    func=AF.Exp, scale=-0.5), c_se, 1)
        rstd_done = p.cnt[pfx + "c_se"]

        @block.tensor
        def _(tensor):
            tensor.wait_ge(c_se, rstd_done)
            tensor.matmul(psbc[:, :], onesbf[0:1, :], lmu[0:1, :],
                          start=True, stop=True)
            p.inc(tensor.matmul(psbc2[:, :], onesbf[0:1, :], lrs[0:1, :],
                                start=True, stop=True), c_pe, 1)
        bc_done = p.cnt[pfx + "c_pe"]

        @block.vector
        def _(vector):
            vector.wait_ge(c_pe, bc_done)
            for cc in range(32):
                vector.tensor_sub(cumt[:, :], ctxb[:, cc * CH4:(cc + 1) * CH4],
                                  psbc[:, :])
                vector.tensor_mul(cumt[:, :], cumt[:, :], psbc2[:, :])
                ins = vector.tensor_scalar(ctxb[:, cc * CH4:(cc + 1) * CH4],
                                           cumt[:, :],
                                           cols["lgc"][:, cc:cc + 1],
                                           cols["lbc"][:, cc:cc + 1],
                                           op0=ALU.mult, op1=ALU.add)
            p.inc(ins, c_ve, 1)
        ln_done = p.cnt[pfx + "c_ve"]

        @block.scalar
        def _(scalar):
            act_switch(scalar, nc, SET_GELU, GRD)
        gmm_done = {}
        for g4 in range(4):
            wb = wbufB if g4 % 2 == 0 else wbufA
            if g4 >= 2:
                @block.gpsimd
                def _(gp):
                    gp.wait_ge(c_pe, gmm_done[g4 - 2])
                    src = bass.AP(wd["W1T"].tensor, g4 * 512,
                                  [[65536, 128], [2048, 32], [1, 512]])
                    p.inc(gp.dma_start(
                        out=wb.ap().rearrange("p (a t) -> p a t", a=32),
                        in_=src), d_ws, 16)
                g_ld[g4] = p.cnt[pfx + "d_ws"]

            for ml in range(4):
                mc = g4 * 4 + ml
                ps = psA if mc % 2 == 0 else psB

                @block.tensor
                def _(tensor):
                    if ml == 0:
                        tensor.wait_ge(d_ws, g_ld[g4])
                        if g4 == 0:
                            tensor.wait_ge(c_ve, ln_done)
                    kind, wv_ = bankstate[mc % 2]
                    tensor.wait_ge(semmap[kind], wv_)
                    for kc in range(32):
                        ins = tensor.matmul(
                            ps[:, :], wb[:, kc * 512 + ml * 128:
                                         kc * 512 + (ml + 1) * 128],
                            ctxb[:, kc * CH4:(kc + 1) * CH4],
                            start=(kc == 0), stop=(kc == 31))
                    p.inc(ins, c_pe, 1)
                mmv = p.cnt[pfx + "c_pe"]

                @block.scalar
                def _(scalar):
                    scalar.wait_ge(c_pe, mmv)
                    p.inc(scalar.activation(
                        out=h1b[:, mc * CH4:(mc + 1) * CH4], in_=ps[:, :],
                        func=AF.Gelu, bias=cols["b1c"][:, mc:mc + 1]),
                        c_se, 1)
                bankstate[mc % 2] = ("se", p.cnt[pfx + "c_se"])
            gmm_done[g4] = p.cnt[pfx + "c_pe"]
        busyA = gmm_done[3]
        h1_done = p.cnt[pfx + "c_se"]

        @block.gpsimd
        def _(gp):
            gp.wait_ge(c_pe, gmm_done[2])
            p.inc(gp.dma_start(out=wbufB.ap(), in_=wd["W2T"]), d_ws, 16)
        w2_ld = p.cnt[pfx + "d_ws"]

        for mc in range(8):
            ps = psA if mc % 2 == 0 else psB

            @block.tensor
            def _(tensor):
                if mc == 0:
                    tensor.wait_ge(d_ws, w2_ld)
                    tensor.wait_ge(c_se, h1_done)
                kind, wv_ = bankstate[mc % 2]
                tensor.wait_ge(semmap[kind], wv_)
                for kc in range(16):
                    ins = tensor.matmul(
                        ps[:, :], wbufB[:, kc * 1024 + mc * 128:
                                        kc * 1024 + (mc + 1) * 128],
                        h1b[:, kc * CH4:(kc + 1) * CH4],
                        start=(kc == 0), stop=(kc == 15))
                p.inc(ins, c_pe, 1)
            mmv = p.cnt[pfx + "c_pe"]

            @block.vector
            def _(vector):
                vector.wait_ge(c_pe, mmv)
                if out_hist[mc % 2]:
                    vector.wait_ge(d_out, out_hist[mc % 2])
                p.inc(vector.scalar_tensor_tensor(
                    outstg[mc % 2][:, :], ps[:, :], cols["b2c"][:, mc:mc + 1],
                    xnf[:, mc * CH4:(mc + 1) * CH4], op0=ALU.add,
                    op1=ALU.add), c_ve, 1)
            evv = p.cnt[pfx + "c_ve"]
            bankstate[mc % 2] = ("ve", evv)

            @block.gpsimd
            def _(gp):
                gp.wait_ge(c_ve, evv)
                p.inc(gp.dma_start(
                    out=bass.AP(outT_d.tensor, mc * 128 * T + tcol,
                                [[T, 128], [1, CH4]]),
                    in_=outstg[mc % 2][:, :]), d_out, 16)
            out_hist[mc % 2] = p.cnt[pfx + "d_out"]
        busyB = p.cnt[pfx + "c_pe"]

    @block.sync
    def _(sync):
        sync.wait_ge(d_out, p.cnt[pfx + "d_out"])


def build_phase4():
    p = P()
    nc = p.nc
    xnTb_d = nc.dram_tensor("xn1Tb", [D, T], BF16, kind="ExternalInput").ap()
    xnTf_d = nc.dram_tensor("xn1Tf", [D, T], F32, kind="ExternalInput").ap()
    wd = {}
    for n in L1A_W_BF:
        wd[n] = nc.dram_tensor(n, L1A_SHAPES[n], BF16, kind="ExternalInput").ap()
    for n in L1A_W_F32:
        wd[n] = nc.dram_tensor(n, L1A_SHAPES[n], F32, kind="ExternalInput").ap()
    outT_d = nc.dram_tensor("attn1T", [D, T], F32, kind="ExternalOutput").ap()
    with p.stack:
        with nc.Block() as block:
            emit_attn1(p, block, xnTb_d, xnTf_d, wd, outT_d, pfx="a1_")
    return p.nc


def prep_l1f_weights(g):
    def lay(w):
        w = np.ascontiguousarray(np.asarray(w, np.float32))
        i, o = w.shape
        return np.ascontiguousarray(
            w.reshape(i // 128, 128, o).transpose(1, 0, 2).reshape(128, -1))

    def col(v):
        v = np.asarray(v, np.float32).reshape(-1)
        out = np.zeros((128, max(1, (len(v) + 127) // 128)), np.float32)
        for i, x in enumerate(v):
            out[i % 128, i // 128] = x
        return out

    return {"WrT": lay(np.asarray(g["W_r"], np.float32).T).astype(BFNP),
            "WkT": lay(np.asarray(g["W_k"], np.float32).T).astype(BFNP),
            "WvT": lay(np.asarray(g["W_v"], np.float32).T).astype(BFNP),
            "murc": col(g["mu_r"]), "mukc": col(g["mu_k"])}


L1F_W_BF = ["WrT", "WkT", "WvT"]
L1F_W_F32 = ["murc", "mukc"]
L1F_SHAPES = {"WrT": [128, 8192], "WkT": [128, 8192], "WvT": [128, 8192],
              "murc": [128, 8], "mukc": [128, 8]}


def emit_ffn1(p, block, xnTb_d, wd, outT_d, pfx="f1_"):
    nc = p.nc
    wsb = {n: p.sbuf(pfx + n, L1F_SHAPES[n], BF16) for n in L1F_W_BF}
    csb = {n: p.sbuf(pfx + n, L1F_SHAPES[n], F32) for n in L1F_W_F32}
    onemur = p.sbuf(pfx + "onemur", [128, 8], F32)
    onemuk = p.sbuf(pfx + "onemuk", [128, 8], F32)
    xg = p.sbuf(pfx + "xg", [128, 8 * (T + 1)], BF16)
    xrb = p.sbuf(pfx + "xrb", [128, 8 * 512], BF16)
    xkb = p.sbuf(pfx + "xkb", [128, 8 * 512], BF16)
    rT = p.sbuf(pfx + "rT", [128, 8 * 512], BF16)
    kact = p.sbuf(pfx + "kact", [128, 8 * 512], BF16)
    t1 = p.sbuf(pfx + "t1", [128, 512], F32)
    t2 = p.sbuf(pfx + "t2", [128, 512], F32)
    outstg = [p.sbuf(pfx + f"outstg{i}", [128, 512], F32) for i in range(2)]

    psA = p.psum(pfx + "psA", [128, 512], F32)
    psB = p.psum(pfx + "psB", [128, 512], F32)

    c_pe = p.sem(pfx + "c_pe")
    c_ve = p.sem(pfx + "c_ve")
    c_se = p.sem(pfx + "c_se")
    vch = p.sem(pfx + "vch")
    d_wl = p.sem(pfx + "d_wl")
    d_out = p.sem(pfx + "d_out")
    semmap = {"se": c_se, "ve": c_ve, "pe": c_pe}
    bankstate = {0: ("se", 0), 1: ("se", 0)}
    out_hist = [0, 0]

    @block.gpsimd
    def _(gp):
        for n in L1F_W_BF:
            p.inc(gp.dma_start(out=wsb[n].ap(), in_=wd[n]), d_wl, 16)
        for n in L1F_W_F32:
            p.inc(gp.dma_start(out=csb[n].ap(), in_=wd[n]), d_wl, 16)
        for dc in range(8):
            gp.memset(xg[:, dc * (T + 1):dc * (T + 1) + 1], 0.0)
            p.inc(gp.dma_start(
                out=xg[:, dc * (T + 1) + 1:(dc + 1) * (T + 1)],
                in_=xnTb_d[dc * 128:(dc + 1) * 128, :]), d_wl, 16)
    wl_done = p.cnt[pfx + "d_wl"]

    @block.scalar
    def _(scalar):
        act_switch(scalar, nc, SET_SIG_ERF)

    @block.vector
    def _(vector):
        vector.wait_ge(d_wl, wl_done)
        vector.tensor_scalar(onemur[:, :], csb["murc"][:, :], -1.0, 1.0,
                             op0=ALU.mult, op1=ALU.add)
        p.inc(vector.tensor_scalar(onemuk[:, :], csb["mukc"][:, :], -1.0, 1.0,
                                   op0=ALU.mult, op1=ALU.add), vch, 1)
    mu_done = p.cnt[pfx + "vch"]

    for c in range(NC4):
        tcol = c * CH4
        barrier = dict(pe=p.cnt[pfx + "c_pe"], ve=p.cnt[pfx + "c_ve"],
                       se=p.cnt[pfx + "c_se"], vh=p.cnt[pfx + "vch"])

        @block.vector
        def _(vector):
            vector.wait_ge(vch, max(barrier["vh"], mu_done))
            vector.wait_ge(c_pe, barrier["pe"])
            vector.wait_ge(c_se, barrier["se"])
            for dc in range(8):
                cur = xg[:, dc * (T + 1) + 1 + tcol:
                         dc * (T + 1) + 1 + tcol + CH4]
                prv = xg[:, dc * (T + 1) + tcol: dc * (T + 1) + tcol + CH4]
                vector.tensor_scalar(t1[:, :], cur, csb["murc"][:, dc:dc + 1],
                                     None, op0=ALU.mult)
                vector.scalar_tensor_tensor(xrb[:, dc * CH4:(dc + 1) * CH4],
                                            prv, onemur[:, dc:dc + 1],
                                            t1[:, :], op0=ALU.mult,
                                            op1=ALU.add)
                vector.tensor_scalar(t1[:, :], cur, csb["mukc"][:, dc:dc + 1],
                                     None, op0=ALU.mult)
                ins = vector.scalar_tensor_tensor(
                    xkb[:, dc * CH4:(dc + 1) * CH4], prv,
                    onemuk[:, dc:dc + 1], t1[:, :], op0=ALU.mult, op1=ALU.add)
            p.inc(ins, vch, 1)
        prep_done = p.cnt[pfx + "vch"]

        for mc in range(8):
            ps = psA if mc % 2 == 0 else psB

            @block.tensor
            def _(tensor):
                if mc == 0:
                    tensor.wait_ge(vch, prep_done)
                kind, wv_ = bankstate[mc % 2]
                tensor.wait_ge(semmap[kind], wv_)
                if mc == 0 and c == 0:
                    tensor.wait_ge(d_wl, wl_done)
                for kc in range(8):
                    ins = tensor.matmul(
                        ps[:, :], wsb["WrT"][:, kc * 1024 + mc * 128:
                                             kc * 1024 + (mc + 1) * 128],
                        xrb[:, kc * CH4:(kc + 1) * CH4],
                        start=(kc == 0), stop=(kc == 7))
                p.inc(ins, c_pe, 1)
            mmv = p.cnt[pfx + "c_pe"]

            @block.scalar
            def _(scalar):
                scalar.wait_ge(c_pe, mmv)
                p.inc(scalar.activation(out=rT[:, mc * CH4:(mc + 1) * CH4],
                                        in_=ps[:, :], func=AF.Sigmoid),
                      c_se, 1)
            bankstate[mc % 2] = ("se", p.cnt[pfx + "c_se"])

        for mc in range(8):
            ps = psA if mc % 2 == 0 else psB

            @block.tensor
            def _(tensor):
                kind, wv_ = bankstate[mc % 2]
                tensor.wait_ge(semmap[kind], wv_)
                for kc in range(8):
                    ins = tensor.matmul(
                        ps[:, :], wsb["WkT"][:, kc * 1024 + mc * 128:
                                             kc * 1024 + (mc + 1) * 128],
                        xkb[:, kc * CH4:(kc + 1) * CH4],
                        start=(kc == 0), stop=(kc == 7))
                p.inc(ins, c_pe, 1)
            mmv = p.cnt[pfx + "c_pe"]

            @block.vector
            def _(vector):
                vector.wait_ge(c_pe, mmv)
                vector.tensor_scalar_max(t2[:, :], ps[:, :], 0.0)
                p.inc(vector.tensor_mul(kact[:, mc * CH4:(mc + 1) * CH4],
                                        t2[:, :], t2[:, :]), c_ve, 1)
            bankstate[mc % 2] = ("ve", p.cnt[pfx + "c_ve"])
        kact_done = p.cnt[pfx + "c_ve"]

        for mc in range(8):
            ps = psA if mc % 2 == 0 else psB

            @block.tensor
            def _(tensor):
                if mc == 0:
                    tensor.wait_ge(c_ve, kact_done)
                kind, wv_ = bankstate[mc % 2]
                tensor.wait_ge(semmap[kind], wv_)
                for kc in range(8):
                    ins = tensor.matmul(
                        ps[:, :], wsb["WvT"][:, kc * 1024 + mc * 128:
                                             kc * 1024 + (mc + 1) * 128],
                        kact[:, kc * CH4:(kc + 1) * CH4],
                        start=(kc == 0), stop=(kc == 7))
                p.inc(ins, c_pe, 1)
            mmv = p.cnt[pfx + "c_pe"]

            @block.vector
            def _(vector):
                vector.wait_ge(c_pe, mmv)
                if out_hist[mc % 2]:
                    vector.wait_ge(d_out, out_hist[mc % 2])
                p.inc(vector.tensor_mul(outstg[mc % 2][:, :], ps[:, :],
                                        rT[:, mc * CH4:(mc + 1) * CH4]),
                      c_ve, 1)
            evv = p.cnt[pfx + "c_ve"]
            bankstate[mc % 2] = ("ve", evv)

            @block.gpsimd
            def _(gp):
                gp.wait_ge(c_ve, evv)
                p.inc(gp.dma_start(
                    out=bass.AP(outT_d.tensor, mc * 128 * T + tcol,
                                [[T, 128], [1, CH4]]),
                    in_=outstg[mc % 2][:, :]), d_out, 16)
            out_hist[mc % 2] = p.cnt[pfx + "d_out"]

    @block.sync
    def _(sync):
        sync.wait_ge(d_out, p.cnt[pfx + "d_out"])


def build_phase5():
    p = P()
    nc = p.nc
    xnTb_d = nc.dram_tensor("xn1Tb", [D, T], BF16, kind="ExternalInput").ap()
    wd = {}
    for n in L1F_W_BF:
        wd[n] = nc.dram_tensor(n, L1F_SHAPES[n], BF16, kind="ExternalInput").ap()
    for n in L1F_W_F32:
        wd[n] = nc.dram_tensor(n, L1F_SHAPES[n], F32, kind="ExternalInput").ap()
    outT_d = nc.dram_tensor("ffn1T", [D, T], F32, kind="ExternalOutput").ap()
    with p.stack:
        with nc.Block() as block:
            emit_ffn1(p, block, xnTb_d, wd, outT_d, pfx="f1_")
    return p.nc


def build_fused():
    p = P()
    nc = p.nc
    x_d = nc.dram_tensor("x", [T, D], F32, kind="ExternalInput").ap()
    a0w = {n: nc.dram_tensor(n, ATTN_SHAPES[n], F32, kind="ExternalInput").ap()
           for n in ATTN_W}
    f0w = {n: nc.dram_tensor(n, FFN_SHAPES[n], F32, kind="ExternalInput").ap()
           for n in FFN_W_BF + FFN_W_F32}
    a1w = {n: nc.dram_tensor(n, L1A_SHAPES[n],
                             BF16 if n in L1A_W_BF else F32,
                             kind="ExternalInput").ap()
           for n in L1A_W_BF + L1A_W_F32}
    f1w = {n: nc.dram_tensor(n, L1F_SHAPES[n],
                             BF16 if n in L1F_W_BF else F32,
                             kind="ExternalInput").ap()
           for n in L1F_W_BF + L1F_W_F32}
    xn0_d = nc.dram_tensor("xn0", [T, D], F32, kind="Internal").ap()
    xn0T_d = nc.dram_tensor("xn0T", [D, T], BF16, kind="Internal").ap()
    attn0T_d = nc.dram_tensor("attn0T", [D, T], F32, kind="Internal").ap()
    f0T_d = nc.dram_tensor("f0T", [D, T], F32, kind="Internal").ap()
    crev_d = [nc.dram_tensor(f"crevd{i}", [130, 1024], BF16,
                             kind="Internal").ap() for i in range(2)]
    x1_d = nc.dram_tensor("x1", [T, D], F32, kind="Internal").ap()
    xn1_d = nc.dram_tensor("xn1", [T, D], F32, kind="Internal").ap()
    xn1Tb_d = nc.dram_tensor("xn1Tb", [D, T], BF16, kind="Internal").ap()
    xn1Tf_d = nc.dram_tensor("xn1Tf", [D, T], F32, kind="Internal").ap()
    attn1T_d = nc.dram_tensor("attn1T", [D, T], F32, kind="Internal").ap()
    ffn1T_d = nc.dram_tensor("ffn1T", [D, T], F32, kind="Internal").ap()
    out_d = nc.dram_tensor("out", [T, D], F32, kind="ExternalOutput").ap()

    with p.stack:
        with nc.Block() as block:
            bar = {"i": 0}

            def barrier():
                new = p.sems[bar["i"]:]
                bar["i"] = len(p.sems)
                waits = [(s, p.cnt[s.name]) for s in new if p.cnt[s.name] > 0]

                @block.tensor
                def _(tensor):
                    for s, v in waits:
                        tensor.wait_ge(s, v)

                @block.vector
                def _(vector):
                    for s, v in waits:
                        vector.wait_ge(s, v)

                @block.scalar
                def _(scalar):
                    for s, v in waits:
                        scalar.wait_ge(s, v)

                @block.sync
                def _(sync):
                    for s, v in waits:
                        sync.wait_ge(s, v)

                @block.gpsimd
                def _(gp):
                    for s, v in waits:
                        gp.wait_ge(s, v)

            with p.phase():
                emit_ln_transpose(p, block, x_d, xn0_d, xn0T_d, pfx="p0_")
            barrier()
            with p.phase():
                emit_attn0(p, block, xn0T_d, a0w, attn0T_d, crev_d, pfx="a0_")
            barrier()
            with p.phase():
                emit_ffn0(p, block, xn0T_d, f0w, f0T_d, pfx="f0_")
            barrier()
            with p.phase():
                emit_assemble(p, block, x_d, xn0_d, attn0T_d, f0T_d, x1_d,
                              pfx="p3a_")
            barrier()
            with p.phase():
                emit_ln_transpose(p, block, x1_d, xn1_d, xn1Tb_d, pfx="p3b_",
                                  xnTf_d=xn1Tf_d)
            barrier()
            with p.phase():
                emit_attn1(p, block, xn1Tb_d, xn1Tf_d, a1w, attn1T_d,
                           pfx="a1_")
            barrier()
            with p.phase():
                emit_ffn1(p, block, xn1Tb_d, f1w, ffn1T_d, pfx="f1_")
            barrier()
            with p.phase():
                emit_assemble(p, block, x1_d, None, attn1T_d, ffn1T_d, out_d,
                              pfx="p6_")
    return p.nc


def kernel(**inputs):
    X = np.asarray(inputs["L0_attn_X"], np.float32)
    attn_w = {n: np.ascontiguousarray(np.asarray(inputs["L0_attn_" + n],
                                                 np.float32))
              for n in ATTN_W}
    ffn_wd = prep_ffn_weights(
        {k[len("L0_ffn_"):]: np.asarray(v) for k, v in inputs.items()
         if k.startswith("L0_ffn_")})
    l1a_wd = prep_l1a_weights(
        {k[len("L1_attn_"):]: np.asarray(v) for k, v in inputs.items()
         if k.startswith("L1_attn_")})
    l1f_wd = prep_l1f_weights(
        {k[len("L1_ffn_"):]: np.asarray(v) for k, v in inputs.items()
         if k.startswith("L1_ffn_")})
    cores = list(range(8))
    wmap = dict(attn_w)
    wmap.update(ffn_wd)
    wmap.update(l1a_wd)
    wmap.update(l1f_wd)
    res = run_bass_kernel_spmd(
        build_fused(),
        [dict(wmap, x=np.ascontiguousarray(X[i])) for i in cores],
        core_ids=cores).results
    return np.stack([np.asarray(r["out"], np.float32) for r in res])


def kernel_multi(**inputs):
    X = np.asarray(inputs["L0_attn_X"], np.float32)
    attn_w = {n: np.ascontiguousarray(np.asarray(inputs["L0_attn_" + n],
                                                 np.float32))
              for n in ATTN_W}
    ffn_wd = prep_ffn_weights(
        {k[len("L0_ffn_"):]: np.asarray(v) for k, v in inputs.items()
         if k.startswith("L0_ffn_")})
    l1a_wd = prep_l1a_weights(
        {k[len("L1_attn_"):]: np.asarray(v) for k, v in inputs.items()
         if k.startswith("L1_attn_")})
    l1f_wd = prep_l1f_weights(
        {k[len("L1_ffn_"):]: np.asarray(v) for k, v in inputs.items()
         if k.startswith("L1_ffn_")})
    cores = list(range(8))

    res = run_bass_kernel_spmd(
        build_phase0(), [{"x": np.ascontiguousarray(X[i])} for i in cores],
        core_ids=cores).results
    xn0 = [r["xn0"] for r in res]
    xn0T = [r["xn0T"] for r in res]

    res = run_bass_kernel_spmd(
        build_phase1(), [dict(attn_w, xn0T=xn0T[i]) for i in cores],
        core_ids=cores).results
    attn0T = [r["attn0T"] for r in res]

    res = run_bass_kernel_spmd(
        build_phase2(), [dict(ffn_wd, xn0T=xn0T[i]) for i in cores],
        core_ids=cores).results
    f0T = [r["f0T"] for r in res]

    res = run_bass_kernel_spmd(
        build_phase3a(),
        [{"x": np.ascontiguousarray(X[i]), "xn0": xn0[i],
          "attn0T": attn0T[i], "f0T": f0T[i]} for i in cores],
        core_ids=cores).results
    x1 = [r["x1"] for r in res]

    res = run_bass_kernel_spmd(
        build_phase3b(), [{"x1": x1[i]} for i in cores],
        core_ids=cores).results
    xn1Tb = [r["xn1Tb"] for r in res]
    xn1Tf = [r["xn1Tf"] for r in res]

    res = run_bass_kernel_spmd(
        build_phase4(),
        [dict(l1a_wd, xn1Tb=xn1Tb[i], xn1Tf=xn1Tf[i]) for i in cores],
        core_ids=cores).results
    attn1T = [r["attn1T"] for r in res]

    res = run_bass_kernel_spmd(
        build_phase5(), [dict(l1f_wd, xn1Tb=xn1Tb[i]) for i in cores],
        core_ids=cores).results
    ffn1T = [r["ffn1T"] for r in res]

    res = run_bass_kernel_spmd(
        build_phase6(),
        [{"x1": x1[i], "attn1T": attn1T[i], "ffn1T": ffn1T[i]}
         for i in cores],
        core_ids=cores).results
    return np.stack([np.asarray(r["out"], np.float32) for r in res])


# version 04 (final)
# speedup vs baseline: 1.1288x; 1.1288x over previous
import contextlib
import os
import numpy as np
import ml_dtypes
import concourse.bass as bass
import concourse.mybir as mybir
from concourse.bass_utils import run_bass_kernel_spmd

BFNP = ml_dtypes.bfloat16

F32 = mybir.dt.float32
BF16 = mybir.dt.bfloat16
I32 = mybir.dt.int32
AF = mybir.ActivationFunctionType
ALU = mybir.AluOpType
AX = mybir.AxisListType

T = 4096
D = 1024
NT = T // 128
ND = D // 128
LN_EPS = 1e-5
ACT_NOP = 8000
SIM = os.environ.get("BASS_SIM_FP64") == "1" or os.environ.get("KERNEL_SIM") == "1"

SET_EXP_LN = 6
SET_SIG_ERF = 2
SET_SIN = 9
SET_GELU = 10


def act_switch(scalar, nc, set_id, guard=None):
    if SIM:
        return
    if guard is not None:
        p, gcell, gsem = guard
        w = p.inc(scalar.copy(gcell, gcell), gsem, 1)
        scalar.wait_ge(gsem, w)
    scalar.add_instruction(
        mybir.InstLoadActFuncSet(name=nc.get_next_instruction_name(),
                                 act_func_set_id=set_id, ins=[], outs=[]))
    scalar.nop(cycle_cnt=ACT_NOP)


class P:

    def __init__(self):
        self.nc = bass.Bass("TRN2", target_bir_lowering=False,
                            detect_race_conditions=False)
        self.stack = contextlib.ExitStack()
        self.cnt = {}
        self.cur = None
        self.sems = []

    def sem(self, name):
        h = self.stack.enter_context(self.nc.semaphore(name))
        self.cnt[name] = 0
        self.sems.append(h)
        return h

    def sbuf(self, name, shape, dtype):
        return (self.cur or self.stack).enter_context(
            self.nc.sbuf_tensor(name, shape, dtype))

    def psum(self, name, shape, dtype):
        return (self.cur or self.stack).enter_context(
            self.nc.psum_tensor(name, shape, dtype))

    @contextlib.contextmanager
    def phase(self):
        prev = self.cur
        with contextlib.ExitStack() as st:
            self.cur = st
            try:
                yield
            finally:
                self.cur = prev

    def inc(self, inst, sem, n=1):
        inst.then_inc(sem, n)
        self.cnt[sem.name] += n
        return self.cnt[sem.name]


def emit_ln_transpose(p, block, x_d, xn_d, xnT_d, pfx="", dbg=None,
                      xnTf_d=None):
    nc = p.nc
    xs = [p.sbuf(f"{pfx}xs{i}", [128, D], F32) for i in range(2)]
    xn = [p.sbuf(f"{pfx}xn{i}", [128, D], F32) for i in range(2)]
    stage = [p.sbuf(f"{pfx}stage{i}", [128, D], BF16) for i in range(2)]
    stagef = [p.sbuf(f"{pfx}stagef{i}", [128, D], F32) for i in range(2)] \
        if xnTf_d is not None else None
    sq = p.sbuf(f"{pfx}sq", [128, D], F32)
    stats = p.sbuf(f"{pfx}stats", [128, 32], F32)
    ident = p.sbuf(f"{pfx}ident", [128, 128], F32)
    consts = p.sbuf(f"{pfx}consts", [128, 2], F32)
    pt = [p.psum(f"{pfx}pt{i}", [128, 128], F32) for i in range(2)]

    ld = p.sem(pfx + "ld")
    sox = p.sem(pfx + "sox")
    sost = p.sem(pfx + "sost")
    csq = p.sem(pfx + "csq")
    vs = p.sem(pfx + "vs")
    sr = p.sem(pfx + "sr")
    xnv = p.sem(pfx + "xnv")
    tp = p.sem(pfx + "tp")
    ev = p.sem(pfx + "ev")
    gpi = p.sem(pfx + "gpi")
    vch = p.sem(pfx + "vch")

    @block.gpsimd
    def _(gp):
        gp.memset(ident[:, :], 1.0)
        gp.affine_select(ident[:, :], ident[:, :], pattern=[[-1, 128]],
                         compare_op=ALU.is_equal, fill=0.0, base=0,
                         channel_multiplier=1)
        p.inc(gp.memset(consts[:, 0:1], LN_EPS), gpi, 1)

    @block.sync
    def _(sync):
        for rt in range(NT):
            b = rt % 2
            if rt >= 2:
                sync.wait_ge(csq, rt - 1)
                sync.wait_ge(xnv, rt - 1)
            p.inc(sync.dma_start(out=xs[b][:, :],
                                 in_=x_d[rt * 128:(rt + 1) * 128, :]), ld, 16)

    @block.scalar
    def _(scalar):
        act_switch(scalar, nc, SET_EXP_LN)
        scalar.wait_ge(gpi, 1)
        for rt in range(NT):
            b = rt % 2
            scalar.wait_ge(ld, 16 * (rt + 1))
            p.inc(scalar.activation(out=sq[:, :], in_=xs[b][:, :],
                                    func=AF.Square, accum_out=stats[:, 4:5]),
                  csq, 1)
            scalar.wait_ge(vs, rt + 1)
            lnv = p.inc(scalar.activation(out=stats[:, 20:21],
                                          in_=stats[:, 12:13],
                                          func=AF.Ln, bias=consts[:, 0:1]),
                        sr, 1)
            scalar.wait_ge(sr, lnv)
            p.inc(scalar.activation(out=stats[:, 16:17], in_=stats[:, 20:21],
                                    func=AF.Exp, scale=-0.5), sr, 1)
            scalar.wait_ge(xnv, rt + 1)
            p.inc(scalar.dma_start(out=xn_d[rt * 128:(rt + 1) * 128, :],
                                   in_=xn[b][:, :]), sox, 16)
            if dbg is not None and rt == 0:
                p.inc(scalar.dma_start(out=dbg, in_=stats[:, :]), sox, 16)

    @block.vector
    def _(vector):
        nev = 16 if xnTf_d is not None else 8
        for rt in range(NT):
            b = rt % 2
            vector.wait_ge(ld, 16 * (rt + 1))
            w = p.inc(vector.reduce_sum(stats[:, 0:1], xs[b][:, :], axis=AX.X),
                      vch, 1)
            vector.wait_ge(vch, w)
            w = p.inc(vector.tensor_scalar_mul(stats[:, 8:9], stats[:, 0:1],
                                               1.0 / D), vch, 1)
            vector.wait_ge(vch, w)
            w = p.inc(vector.tensor_mul(stats[:, 24:25], stats[:, 8:9],
                                        stats[:, 8:9]), vch, 1)
            vector.wait_ge(vch, w)
            vector.wait_ge(csq, rt + 1)
            p.inc(vector.scalar_tensor_tensor(stats[:, 12:13], stats[:, 4:5],
                                              1.0 / D, stats[:, 24:25],
                                              op0=ALU.mult, op1=ALU.subtract),
                  vs, 1)
            vector.wait_ge(sr, 2 * (rt + 1))
            if rt >= 2:
                vector.wait_ge(sox, 16 * (rt - 1))
                vector.wait_ge(tp, 8 * (rt - 1))
            p.inc(vector.tensor_scalar(xn[b][:, :], xs[b][:, :], stats[:, 8:9],
                                       stats[:, 16:17], op0=ALU.subtract,
                                       op1=ALU.mult), xnv, 1)
            for cc in range(8):
                vector.wait_ge(tp, 8 * rt + cc + 1)
                if cc == 0 and rt >= 2:
                    vector.wait_ge(sost, (32 if xnTf_d is not None else 16) * (rt - 1))
                ins = vector.tensor_copy(stage[b][:, cc * 128:(cc + 1) * 128],
                                         pt[cc % 2][:, :])
                if xnTf_d is not None:
                    ins = vector.tensor_copy(
                        stagef[b][:, cc * 128:(cc + 1) * 128], pt[cc % 2][:, :])
                p.inc(ins, ev, 1)

    @block.tensor
    def _(tensor):
        tensor.wait_ge(gpi, 1)
        for rt in range(NT):
            b = rt % 2
            tensor.wait_ge(xnv, rt + 1)
            for cc in range(8):
                w = 8 * rt + cc - 1
                if w > 0:
                    tensor.wait_ge(ev, w)
                p.inc(tensor.transpose(pt[cc % 2][:, :],
                                       xn[b][:, cc * 128:(cc + 1) * 128],
                                       ident[:, :]), tp, 1)

    @block.gpsimd
    def _(gp):
        for rt in range(NT):
            b = rt % 2
            gp.wait_ge(ev, 8 * (rt + 1))
            dst = bass.AP(xnT_d.tensor, rt * 128,
                          [[T, 128], [128 * T, 8], [1, 128]])
            p.inc(gp.dma_start(out=dst,
                               in_=stage[b].ap().rearrange("p (c t) -> p c t",
                                                           c=8)), sost, 16)
            if xnTf_d is not None:
                dstf = bass.AP(xnTf_d.tensor, rt * 128,
                               [[T, 128], [128 * T, 8], [1, 128]])
                p.inc(gp.dma_start(out=dstf,
                                   in_=stagef[b].ap().rearrange(
                                       "p (c t) -> p c t", c=8)), sost, 16)

    @block.sync
    def _(sync):
        sync.wait_ge(sox, p.cnt[pfx + "sox"])
        sync.wait_ge(sost, p.cnt[pfx + "sost"])


DEBUG_STATS = False


def build_phase0():
    p = P()
    nc = p.nc
    x_d = nc.dram_tensor("x", [T, D], F32, kind="ExternalInput").ap()
    xn_d = nc.dram_tensor("xn0", [T, D], F32, kind="ExternalOutput").ap()
    xnT_d = nc.dram_tensor("xn0T", [D, T], BF16, kind="ExternalOutput").ap()
    dbg = nc.dram_tensor("dbg", [128, 32], F32, kind="ExternalOutput").ap() \
        if DEBUG_STATS else None
    with p.stack:
        with nc.Block() as block:
            emit_ln_transpose(p, block, x_d, xn_d, xnT_d, pfx="p0_", dbg=dbg)
    return p.nc


H = 16
CH = 512
NCH = T // CH
RING = 1536
P1_STAGE = 99
P1_SUB = 99


def emit_attn0(p, block, xnT_d, wdram, attnT_d, crev_d, pfx="a0_"):
    nc = p.nc
    wq = p.sbuf(pfx + "wq", [128, 8192], BF16)
    wk = p.sbuf(pfx + "wk", [128, 8192], BF16)
    wv = p.sbuf(pfx + "wv", [128, 8192], BF16)
    wp = p.sbuf(pfx + "wp", [128, 8192], BF16)
    wr = p.sbuf(pfx + "wr", [128, 8192], BF16)
    bq8 = p.sbuf(pfx + "bq8", [128, 8], F32)
    bpj = p.sbuf(pfx + "bpj", [128, 8], F32)
    brp = p.sbuf(pfx + "brp", [128, 8], F32)
    bndf = p.sbuf(pfx + "bndf", [16, 1024], F32)
    bndr = p.sbuf(pfx + "bndr", [128, 1024], BF16)
    xch = p.sbuf(pfx + "xch", [128, 8 * CH], BF16)
    qT = p.sbuf(pfx + "qT", [128, 8 * CH], BF16)
    rT = p.sbuf(pfx + "rT", [128, 8 * CH], BF16)
    kT = p.sbuf(pfx + "kT", [128, 8 * RING], BF16)
    va = p.sbuf(pfx + "va", [128, 12 * 1040], BF16)
    aT = p.sbuf(pfx + "aT", [128, 8 * CH], BF16)
    crevs = [p.sbuf(pfx + f"crevs{i}", [128, 1024], BF16) for i in range(2)]
    scb = [p.sbuf(pfx + f"sc{i}", [128, 1152], BF16) for i in range(2)]
    s_sb = p.sbuf(pfx + "s_sb", [128, 1152], F32)
    w_sb = p.sbuf(pfx + "w_sb", [128, 1152], BF16)
    wT_sb = p.sbuf(pfx + "wT_sb", [128, 1152], BF16)
    m0 = p.sbuf(pfx + "m0", [128, 128], F32)
    m8 = p.sbuf(pfx + "m8", [128, 128], F32)
    m0b = p.sbuf(pfx + "m0b", [128, 128], BF16)
    m8b = p.sbuf(pfx + "m8b", [128, 128], BF16)
    idf = p.sbuf(pfx + "idf", [128, 128], F32)
    idb = p.sbuf(pfx + "idb", [128, 128], BF16)
    zrow = p.sbuf(pfx + "zrow", [1, 1024], BF16)
    nmax = p.sbuf(pfx + "nmax", [128, 12], F32)
    dnb = p.sbuf(pfx + "dnb", [1, 128], F32)
    bc_sb = p.sbuf(pfx + "bc_sb", [64, 128], F32)
    ones1 = p.sbuf(pfx + "ones1", [1, 64], F32)
    outsb = [p.sbuf(pfx + f"outsb{i}", [128, 512], F32) for i in range(2)]

    crev_ps = p.psum(pfx + "crev_ps", [128, 512], F32)
    qk_ps = p.psum(pfx + "qk_ps", [128, 1536], F32)
    wtp = [p.psum(pfx + f"wtp{i}", [128, 384], BF16) for i in range(3)]
    av_ps = p.psum(pfx + "av_ps", [128, 256], F32)

    c_pe = p.sem(pfx + "c_pe")
    c_ve = p.sem(pfx + "c_ve")
    c_se = p.sem(pfx + "c_se")
    d_wl = p.sem(pfx + "d_wl")
    d_ldx = p.sem(pfx + "d_ldx")
    d_cw = p.sem(pfx + "d_cw")
    d_sh = p.sem(pfx + "d_sh")
    d_out = p.sem(pfx + "d_out")
    gpi = p.sem(pfx + "gpi")

    def rb(h):
        return 64 * (h % 2)

    def hc(h):
        return h // 2

    @block.gpsimd
    def _(gp):
        for wsb, name in [(wq, "Wq"), (wk, "Wk"), (wv, "Wv"), (wp, "Wproj")]:
            wd = wdram[name]
            srcw = bass.AP(wd.tensor, 0,
                           [[1024, 128], [128 * 1024, 8], [128, 8], [1, 128]])
            dst = wsb.ap().rearrange("p (a b t) -> p a b t", a=8, b=8)
            p.inc(gp.dma_start(out=dst, in_=srcw), d_wl, 16)
        gp.memset(wr[:, :], 0.0)
        wrd = wdram["Wr"]
        for kc in range(8):
            dst = bass.AP(wr, kc * 1024, [[8192, 128], [64, 16], [1, 10]])
            srcw = bass.AP(wrd.tensor, kc * 128 * 160,
                           [[160, 128], [10, 16], [1, 10]])
            p.inc(gp.dma_start(out=dst, in_=srcw), d_wl, 16)
        for dname, dstt in [("bq", bq8), ("bproj", bpj)]:
            bd = wdram[dname]
            for mc in range(8):
                p.inc(gp.dma_start(out=dstt[:, mc:mc + 1],
                                   in_=bass.AP(bd.tensor, mc * 128,
                                               [[1, 128], [1, 1]])), d_wl, 16)
        gp.memset(brp[:, :], 0.0)
        brd = wdram["br"]
        for h in range(H):
            p.inc(gp.dma_start(out=brp[rb(h):rb(h) + 10, hc(h):hc(h) + 1],
                               in_=bass.AP(brd.tensor, 10 * h, [[1, 10], [1, 1]])),
                  d_wl, 16)
        p.inc(gp.dma_start(out=bndf[0:10, :], in_=wdram["b_nd"]), d_wl, 16)
        gp.memset(m0[:, :], 0.0)
        gp.affine_select(m0[:, :], m0[:, :], pattern=[[1, 128]],
                         compare_op=ALU.is_ge, fill=-1e9, base=-1,
                         channel_multiplier=-1)
        gp.memset(m8[:, :], 0.0)
        gp.affine_select(m8[:, :], m8[:, :], pattern=[[-1, 128]],
                         compare_op=ALU.is_ge, fill=-1e9, base=0,
                         channel_multiplier=1)
        gp.tensor_copy(m0b[:, :], m0[:, :])
        gp.tensor_copy(m8b[:, :], m8[:, :])
        gp.memset(idf[:, :], 1.0)
        gp.affine_select(idf[:, :], idf[:, :], pattern=[[-1, 128]],
                         compare_op=ALU.is_equal, fill=0.0, base=0,
                         channel_multiplier=1)
        gp.tensor_copy(idb[:, :], idf[:, :])
        gp.memset(ones1[:, :], 1.0)
        gp.memset(zrow[:, :], 0.0)
        for s in range(12):
            for h in range(H):
                gp.memset(va[:, s * 1040 + 65 * h + 64: s * 1040 + 65 * h + 65],
                          1.0)
        for buf in range(2):
            p.inc(gp.dma_start(out=bass.AP(crev_d[buf].tensor, 0,
                                           [[1024, 1], [1, 1024]]),
                               in_=zrow[0:1, :]), d_wl, 16)
            p.inc(gp.dma_start(out=bass.AP(crev_d[buf].tensor, 129 * 1024,
                                           [[1024, 1], [1, 1024]]),
                               in_=zrow[0:1, :]), d_wl, 16)
        p.inc(gp.memset(nmax[:, 8:9], 0.0), gpi, 1)

    @block.vector
    def _(vector):
        vector.wait_ge(d_wl, p.cnt[pfx + "d_wl"])
        vector.wait_ge(gpi, 1)
        p.inc(vector.tensor_scalar_mul(bq8[:, :], bq8[:, :], 0.125), c_ve, 1)
        rev = bass.AP(bndf, 1023, [[1024, 10], [-1, 1024]])
        p.inc(vector.tensor_copy(bndr[0:10, :], rev), c_ve, 1)
        p.inc(vector.tensor_copy(bndr[64:74, :], rev), c_ve, 1)

    @block.scalar
    def _(scalar):
        act_switch(scalar, nc, SET_EXP_LN)
        scalar.wait_ge(c_ve, p.cnt[pfx + "c_ve"])
        scalar.wait_ge(gpi, 1)

    crevs_freed = [0, 0]
    crevd_freed = [0, 0]
    scb_freed = [("ve", 0), ("ve", 0)]
    tilecnt = 0
    prev_evacadd = 0
    wT_freed = 0
    av_freed = 0
    prev_trv = 0
    pendB = [None]

    def emit_partB(jbmin, nblk, h, qtg, qcol, expv):
        nonlocal wT_freed, av_freed, prev_trv
        if P1_STAGE < 6:
            return

        @block.tensor
        def _(tensor):
            tensor.wait_ge(c_se, expv)
            tensor.wait_ge(c_ve, wT_freed)
            for jb in range(jbmin, 9):
                ins = tensor.transpose(
                    wtp[jb // 3][:, (jb % 3) * 128:(jb % 3) * 128 + 128],
                    w_sb[:, jb * 128:(jb + 1) * 128], idb[:, :])
                if jb == 8:
                    p.inc(ins, c_pe, 1)
        trv = p.cnt[pfx + "c_pe"]
        prev_trv = trv

        if P1_STAGE < 7:
            return

        @block.vector
        def _(vector):
            vector.wait_ge(c_pe, trv)
            for k in range(3):
                lo = max(jbmin, 3 * k)
                hi = 3 * k + 3
                if lo >= hi:
                    continue
                ins = vector.tensor_copy(
                    wT_sb[:, lo * 128:hi * 128],
                    wtp[k][:, (lo - 3 * k) * 128:384])
                if k == 2:
                    p.inc(ins, c_ve, 1)
        ewv = p.cnt[pfx + "c_ve"]
        wT_freed = ewv

        if P1_STAGE < 8:
            return

        @block.tensor
        def _(tensor):
            tensor.wait_ge(c_ve, ewv)
            tensor.wait_ge(c_ve, av_freed)
            for i, jb in enumerate(range(jbmin, 9)):
                kt = qtg - 8 + jb
                slot = kt % 12
                ins = tensor.matmul(
                    av_ps[0:65, 0:128],
                    va[:, slot * 1040 + 65 * h: slot * 1040 + 65 * h + 65],
                    wT_sb[:, jb * 128:(jb + 1) * 128],
                    start=(i == 0), stop=(i == nblk - 1))
                if i == nblk - 1:
                    p.inc(ins, c_pe, 1)
        avv = p.cnt[pfx + "c_pe"]

        if P1_STAGE < 9:
            return

        @block.vector
        def _(vector):
            vector.wait_ge(c_pe, avv)
            p.inc(vector.reciprocal(dnb[0:1, :], av_ps[64:65, 0:128]),
                  c_ve, 1)
        rcv = p.cnt[pfx + "c_ve"]

        if P1_STAGE < 10:
            return

        @block.tensor
        def _(tensor):
            tensor.wait_ge(c_ve, rcv)
            p.inc(tensor.matmul(av_ps[0:64, 128:256], ones1[0:1, 0:64],
                                dnb[0:1, 0:128], start=True, stop=True),
                  c_pe, 1)
        bcv = p.cnt[pfx + "c_pe"]

        @block.vector
        def _(vector):
            vector.wait_ge(c_pe, bcv)
            w = p.inc(vector.tensor_copy(bc_sb[:, :],
                                         av_ps[0:64, 128:256]), c_ve, 1)
            vector.wait_ge(c_ve, w)
            p.inc(vector.tensor_tensor(
                aT[rb(h):rb(h) + 64, qcol:qcol + 128],
                av_ps[0:64, 0:128], bc_sb[:, :], op=ALU.mult), c_ve, 1)
        av_freed = p.cnt[pfx + "c_ve"]
    out_dma = {}

    for c in range(NCH):
        barrier = dict(pe=p.cnt[pfx + "c_pe"], ve=p.cnt[pfx + "c_ve"],
                       se=p.cnt[pfx + "c_se"])

        if P1_SUB < 1:
            continue

        @block.sync
        def _(sync):
            sync.wait_ge(c_pe, barrier["pe"])
            sync.wait_ge(c_ve, barrier["ve"])
            sync.wait_ge(c_se, barrier["se"])
            srcx = bass.AP(xnT_d.tensor, c * CH, [[T, 128], [128 * T, 8], [1, CH]])
            dstx = xch.ap().rearrange("p (a t) -> p a t", a=8)
            p.inc(sync.dma_start(out=dstx, in_=srcx), d_ldx, 16)
        ldx_done = p.cnt[pfx + "d_ldx"]

        kslot = c % 3
        sched_evac = {}

        def proj(dst_fn, wsb, evac_kind, bias_sb=None, scale=1.0, first=False):
            for mc in range(8):
                bank = mc % 2

                @block.tensor
                def _(tensor):
                    if first and mc == 0:
                        tensor.wait_ge(d_ldx, ldx_done)
                        tensor.wait_ge(c_ve, barrier["ve"])
                        tensor.wait_ge(c_se, barrier["se"])
                        tensor.wait_ge(gpi, 1)
                    if mc >= 2:
                        kind, w = sched_evac[mc - 2]
                        tensor.wait_ge(c_ve if kind == "dve" else c_se, w)
                    for kc in range(8):
                        ins = tensor.matmul(
                            qk_ps[:, bank * 512:(bank + 1) * 512],
                            wsb[:, kc * 1024 + mc * 128: kc * 1024 + (mc + 1) * 128],
                            xch[:, kc * CH:(kc + 1) * CH],
                            start=(kc == 0), stop=(kc == 7))
                    p.inc(ins, c_pe, 1)
                mmv = p.cnt[pfx + "c_pe"]

                @block.scalar
                def _(scalar):
                    scalar.wait_ge(c_pe, mmv)
                    if evac_kind == "se_copy":
                        p.inc(scalar.copy(dst_fn(mc),
                                          qk_ps[:, bank * 512:(bank + 1) * 512]),
                              c_se, 1)
                    else:
                        p.inc(scalar.activation(
                            out=dst_fn(mc),
                            in_=qk_ps[:, bank * 512:(bank + 1) * 512],
                            func=AF.Identity, bias=bias_sb[:, mc:mc + 1], scale=scale),
                            c_se, 1)
                sched_evac[mc] = ("se", p.cnt[pfx + "c_se"])

        if P1_SUB >= 2:
            proj(lambda mc: kT[:, mc * RING + kslot * 512: mc * RING + (kslot + 1) * 512],
                 wk, "se_copy", first=True)
        if P1_SUB >= 3:
            sched_evac = {}
            proj(lambda mc: qT[:, mc * CH:(mc + 1) * CH], wq, "se", bq8, 0.125)
            sched_evac = {}
            proj(lambda mc: rT[:, mc * CH:(mc + 1) * CH], wr, "se", brp, 1.0)

        sched_evac = {}
        for tt in range(4 if P1_SUB >= 3 else 0):
            slot = (4 * c + tt) % 12
            for half in range(2):
                idx = 2 * tt + half
                bank = idx % 2

                @block.tensor
                def _(tensor):
                    if idx >= 2:
                        kind, w = sched_evac[idx - 2]
                        tensor.wait_ge(c_ve if kind == "dve" else c_se, w)
                    for kc in range(8):
                        ins = tensor.matmul(
                            qk_ps[:, bank * 512:(bank + 1) * 512],
                            xch[:, kc * CH + tt * 128: kc * CH + (tt + 1) * 128],
                            wv[:, kc * 1024 + half * 512: kc * 1024 + (half + 1) * 512],
                            start=(kc == 0), stop=(kc == 7))
                    p.inc(ins, c_pe, 1)
                mmv = p.cnt[pfx + "c_pe"]

                @block.scalar
                def _(scalar):
                    scalar.wait_ge(c_pe, mmv)
                    for hh in range(8):
                        h = half * 8 + hh
                        ins = scalar.copy(
                            va[:, slot * 1040 + 65 * h: slot * 1040 + 65 * h + 64],
                            qk_ps[:, bank * 512 + 64 * hh: bank * 512 + 64 * hh + 64])
                        if hh == 7:
                            p.inc(ins, c_se, 1)
                sched_evac[idx] = ("se", p.cnt[pfx + "c_se"])

        proj_done = dict(pe=p.cnt[pfx + "c_pe"], ve=p.cnt[pfx + "c_ve"],
                         se=p.cnt[pfx + "c_se"])

        for ql in range(4 if P1_SUB >= 5 else 0):
            qtg = c * 4 + ql
            jbmin = max(0, 8 - qtg)
            nblk = 9 - jbmin
            cl = jbmin * 128
            for h in range(H):
                buf = tilecnt % 2
                qcol = hc(h) * CH + ql * 128
                first_tile = (ql == 0 and h == 0)

                if P1_STAGE < 1:
                    tilecnt += 1
                    continue
                crev_ev = []
                for half in range(2):
                    @block.tensor
                    def _(tensor):
                        if half == 0:
                            if first_tile:
                                tensor.wait_ge(c_se, proj_done["se"])
                                tensor.wait_ge(c_ve, proj_done["ve"])
                            tensor.wait_ge(d_cw, crevs_freed[buf])
                        else:
                            tensor.wait_ge(c_se, crev_ev[0])
                        p.inc(tensor.matmul(
                            crev_ps[:, :],
                            rT[rb(h):rb(h) + 10, qcol:qcol + 128],
                            bndr[rb(h):rb(h) + 10, half * 512:(half + 1) * 512],
                            start=True, stop=True), c_pe, 1)
                    mmv = p.cnt[pfx + "c_pe"]

                    @block.scalar
                    def _(scalar):
                        scalar.wait_ge(c_pe, mmv)
                        p.inc(scalar.copy(
                            crevs[buf][:, half * 512:(half + 1) * 512],
                            crev_ps[:, :]), c_se, 1)
                    crev_ev.append(p.cnt[pfx + "c_se"])

                if P1_STAGE < 2:
                    tilecnt += 1
                    continue
                @block.gpsimd
                def _(gp):
                    gp.wait_ge(c_se, crev_ev[1])
                    gp.wait_ge(d_sh, crevd_freed[buf])
                    dstc = bass.AP(crev_d[buf].tensor, 1024,
                                   [[1024, 128], [1, 1024]])
                    p.inc(gp.dma_start(out=dstc, in_=crevs[buf][:, :]), d_cw, 16)
                cwv = p.cnt[pfx + "d_cw"]
                crevs_freed[buf] = cwv

                @block.sync
                def _(sync):
                    sync.wait_ge(d_cw, cwv)
                    kind, wv_ = scb_freed[buf]
                    sync.wait_ge(c_pe if kind == "pe" else c_ve, wv_)
                    srcs = bass.AP(crev_d[buf].tensor, 1023,
                                   [[1023, 128], [1, 1152]])
                    p.inc(sync.dma_start(out=scb[buf][:, :], in_=srcs), d_sh, 16)
                shv = p.cnt[pfx + "d_sh"]
                crevd_freed[buf] = shv

                if P1_STAGE < 3:
                    tilecnt += 1
                    continue
                segs = [(lo, hi) for lo, hi in [(cl, 512), (512, 1024),
                                                (1024, 1152)] if lo < hi]
                @block.tensor
                def _(tensor):
                    tensor.wait_ge(c_se, prev_evacadd)
                    tensor.wait_ge(d_sh, shv)
                    last_ins = None
                    for bk in range(3):
                        blo, bhi = bk * 512, min((bk + 1) * 512, 1152)
                        ops = []
                        for jb in range(max(jbmin, bk * 4), min(9, (bk + 1) * 4)):
                            kt = qtg - 8 + jb
                            ringcol = (kt * 128) % RING
                            ops.append(("qk", jb, ringcol))
                        if not ops:
                            continue
                        lo = max(cl, blo)
                        ops.append(("sc", lo, bhi))
                        if bk == 0 and jbmin == 0:
                            ops.append(("m0",))
                        if bk == 2:
                            ops.append(("m8",))
                        for i, op in enumerate(ops):
                            st_f = (i == 0)
                            sp_f = (i == len(ops) - 1)
                            if op[0] == "qk":
                                jb, ringcol = op[1], op[2]
                                last_ins = tensor.matmul(
                                    qk_ps[:, jb * 128:(jb + 1) * 128],
                                    qT[rb(h):rb(h) + 64, qcol:qcol + 128],
                                    kT[rb(h):rb(h) + 64, hc(h) * RING + ringcol:
                                       hc(h) * RING + ringcol + 128],
                                    start=st_f, stop=sp_f)
                            elif op[0] == "sc":
                                last_ins = tensor.matmul(
                                    qk_ps[:, op[1]:op[2]], idb[:, :],
                                    scb[buf][:, op[1]:op[2]], start=st_f, stop=sp_f)
                            elif op[0] == "m0":
                                last_ins = tensor.matmul(
                                    qk_ps[:, 0:128], idb[:, :], m0b[:, :],
                                    start=st_f, stop=sp_f)
                            else:
                                last_ins = tensor.matmul(
                                    qk_ps[:, 1024:1152], idb[:, :], m8b[:, :],
                                    start=st_f, stop=sp_f)
                    p.inc(last_ins, c_pe, 1)
                qkv = p.cnt[pfx + "c_pe"]
                scb_freed[buf] = ("pe", qkv)

                if P1_STAGE < 5:
                    tilecnt += 1
                    continue
                @block.scalar
                def _(scalar):
                    scalar.wait_ge(c_pe, qkv)
                    if prev_trv:
                        scalar.wait_ge(c_pe, prev_trv)
                    for i, (lo, hi) in enumerate(segs):
                        ins = scalar.activation(out=w_sb[:, lo:hi],
                                                in_=qk_ps[:, lo:hi],
                                                func=AF.Exp,
                                                bias=nmax[:, 8:9])
                        if i == len(segs) - 1:
                            p.inc(ins, c_se, 1)
                expv = p.cnt[pfx + "c_se"]
                prev_evacadd = expv

                if pendB[0] is not None:
                    emit_partB(*pendB[0])
                pendB[0] = (jbmin, nblk, h, qtg, qcol, expv)
                tilecnt += 1


        if pendB[0] is not None:
            emit_partB(*pendB[0])
            pendB[0] = None
        at_done = p.cnt[pfx + "c_ve"]
        projout_evac = {}
        for mc in range(8 if P1_SUB >= 4 else 0):
            bank = mc % 2

            @block.tensor
            def _(tensor):
                if mc == 0:
                    tensor.wait_ge(c_ve, at_done)
                if mc >= 2:
                    tensor.wait_ge(c_se, projout_evac[mc - 2])
                for kc in range(8):
                    ins = tensor.matmul(
                        qk_ps[:, bank * 512:(bank + 1) * 512],
                        wp[:, kc * 1024 + mc * 128: kc * 1024 + (mc + 1) * 128],
                        aT[:, kc * CH:(kc + 1) * CH],
                        start=(kc == 0), stop=(kc == 7))
                p.inc(ins, c_pe, 1)
            mmv = p.cnt[pfx + "c_pe"]

            @block.scalar
            def _(scalar):
                scalar.wait_ge(c_pe, mmv)
                if out_dma.get(mc % 2):
                    scalar.wait_ge(d_out, out_dma[mc % 2])
                p.inc(scalar.activation(out=outsb[mc % 2][:, :],
                                        in_=qk_ps[:, bank * 512:(bank + 1) * 512],
                                        func=AF.Identity, bias=bpj[:, mc:mc + 1]),
                      c_se, 1)
            projout_evac[mc] = p.cnt[pfx + "c_se"]

            @block.gpsimd
            def _(gp):
                gp.wait_ge(c_se, projout_evac[mc])
                dsto = bass.AP(attnT_d.tensor, mc * 128 * T + c * CH,
                               [[T, 128], [1, CH]])
                p.inc(gp.dma_start(out=dsto, in_=outsb[mc % 2][:, :]), d_out, 16)
            out_dma[mc % 2] = p.cnt[pfx + "d_out"]


    @block.sync
    def _(sync):
        sync.wait_ge(d_out, p.cnt[pfx + "d_out"])


ATTN_W = ["Wq", "bq", "Wk", "Wv", "Wproj", "bproj", "Wr", "br", "b_nd"]
ATTN_SHAPES = {"Wq": [D, D], "bq": [D], "Wk": [D, D], "Wv": [D, D],
               "Wproj": [D, D], "bproj": [D], "Wr": [D, 160], "br": [160],
               "b_nd": [10, 1024]}


def build_phase1():
    p = P()
    nc = p.nc
    xnT_d = nc.dram_tensor("xn0T", [D, T], BF16, kind="ExternalInput").ap()
    wdram = {n: nc.dram_tensor(n, ATTN_SHAPES[n], F32, kind="ExternalInput").ap()
             for n in ATTN_W}
    attnT_d = nc.dram_tensor("attn0T", [D, T], F32, kind="ExternalOutput").ap()
    crev_d = [nc.dram_tensor(f"crevd{i}", [130, 1024], BF16, kind="Internal").ap()
              for i in range(2)]
    with p.stack:
        with nc.Block() as block:
            emit_attn0(p, block, xnT_d, wdram, attnT_d, crev_d, pfx="a0_")
    return p.nc




CHF = 512
SUBF = 256
NCF = T // CHF
NSF = CHF // SUBF
DT_CONST = 0.1


def prep_ffn_weights(g):
    def lay(w):
        i, o = w.shape
        return np.ascontiguousarray(
            np.asarray(w, np.float32).reshape(i // 128, 128, o)
            .transpose(1, 0, 2).reshape(128, -1))

    w1cat = np.ascontiguousarray(
        np.asarray(g["expert_w1"], np.float32).transpose(1, 0, 2).reshape(2048, 512))
    w2aug = np.zeros((640, 3072), np.float32)
    w2aug[:512] = np.asarray(g["expert_w2"], np.float32).reshape(512, 3072)
    w2aug[512:520] = np.asarray(g["expert_b2"], np.float32)
    w2saug = np.zeros((128, 3072), np.float32)
    w2saug[:64] = np.asarray(g["shared_w2"], np.float32)
    w2saug[64] = np.asarray(g["shared_b2"], np.float32)

    def col(v):
        v = np.asarray(v, np.float32).reshape(-1)
        ncol = max(1, (len(v) + 127) // 128)
        out = np.zeros((128, ncol), np.float32)
        for i, x in enumerate(v):
            out[i % 128, i // 128] = x
        return out

    return {
        "rw1l": lay(g["router_w1"]), "hw1l": lay(g["halt_w1"]),
        "w1catl": lay(w1cat), "w2augl": lay(w2aug),
        "w2saugl": np.ascontiguousarray(w2saug),
        "sw1l": lay(np.asarray(g["shared_w1"], np.float32)),
        "refw1l": lay(g["refine_w1"]), "refw2l": lay(g["refine_w2"]),
        "rw2l": lay(g["router_w2"]), "hw2l": lay(g["halt_w2"]),
        "rb1c": col(g["router_b1"]), "hb1c": col(g["halt_b1"]),
        "b1catc": col(np.asarray(g["expert_b1"], np.float32).reshape(-1)),
        "sb1c": col(g["shared_b1"]),
        "refb1c": col(g["refine_b1"]), "refb2c": col(g["refine_b2"]),
        "muc": col(g["mu"]), "iwc": col(g["integration_weight"]),
        "rb2row": np.asarray(g["router_b2"], np.float32).reshape(1, 8),
        "hb2s": np.asarray(g["halt_b2"], np.float32).reshape(1, 1),
        "swts": np.asarray(g["shared_weight"], np.float32).reshape(1, 1),
    }


FFN_W_BF = ["rw1l", "hw1l", "w1catl", "w2augl", "w2saugl", "sw1l", "refw1l",
            "refw2l", "rw2l", "hw2l"]
FFN_W_F32 = ["rb1c", "hb1c", "b1catc", "sb1c", "refb1c", "refb2c", "muc",
             "iwc", "rb2row", "hb2s", "swts"]
FFN_SHAPES = {
    "rw1l": [128, 2048], "hw1l": [128, 2048], "w1catl": [128, 8192],
    "w2augl": [128, 15360], "w2saugl": [128, 3072], "sw1l": [128, 1024],
    "refw1l": [128, 16384], "refw2l": [128, 16384], "rw2l": [128, 16],
    "hw2l": [128, 2],
    "rb1c": [128, 2], "hb1c": [128, 2], "b1catc": [128, 4], "sb1c": [128, 1],
    "refb1c": [128, 16], "refb2c": [128, 8], "muc": [128, 8], "iwc": [128, 8],
    "rb2row": [1, 8], "hb2s": [1, 1], "swts": [1, 1],
}
P2_EXPALT = 0
P2_S5 = 7
P2_S4 = 7
P2_STAGE = 99


def emit_ffn0(p, block, xnT_d, wd, fT_d, pfx="f0_"):
    nc = p.nc
    wsb = {n: p.sbuf(pfx + n, FFN_SHAPES[n], BF16) for n in FFN_W_BF
           if n not in ("refw1l", "refw2l")}
    wsb["refw1l"] = wsb["refw2l"] = p.sbuf(pfx + "refw", [128, 16384], BF16)
    csb = {n: p.sbuf(pfx + n, FFN_SHAPES[n], F32) for n in FFN_W_F32}
    rb2b = p.sbuf(pfx + "rb2b", [1, 8], BF16)

    intf = p.sbuf(pfx + "intf", [128, 8 * CHF], F32)
    intc = p.sbuf(pfx + "intc", [128, 8 * CHF], BF16)
    vb = p.sbuf(pfx + "vb", [128, 8 * CHF], BF16)
    g1buf = p.sbuf(pfx + "g1buf", [128, 2 * CHF], BF16)
    r1g = g1buf
    h1g = g1buf
    hat = p.sbuf(pfx + "hat", [128, 4 * CHF], BF16)
    hkm = p.sbuf(pfx + "hkm", [128, 4 * SUBF], BF16)
    hsT = p.sbuf(pfx + "hsT", [128, CHF], BF16)
    shpool = p.sbuf(pfx + "shpool", [128, 72 * SUBF], BF16)
    oT = {k: bass.AP(shpool, (k - 1) * 24 * SUBF,
                     [[72 * SUBF, 128], [1, 24 * SUBF]]) for k in (1, 2, 3)}
    ohT = {1: p.sbuf(pfx + "oh1T", [128, CHF], BF16),
           2: p.sbuf(pfx + "oh2T", [128, CHF], BF16)}
    w1row = p.sbuf(pfx + "w1row", [1, CHF], F32)
    w2row = p.sbuf(pfx + "w2row", [1, CHF], F32)
    urow = p.sbuf(pfx + "urow", [1, CHF], F32)
    hrow = p.sbuf(pfx + "hrow", [1, CHF], F32)
    def ref1g_sl(mc):
        return bass.AP(shpool, mc * CHF, [[72 * SUBF, 128], [1, CHF]])

    def ref1g_rhs(kc):
        return bass.AP(shpool, kc * CHF, [[72 * SUBF, 128], [1, CHF]])
    prb = p.sbuf(pfx + "prb", [128, 8], F32)
    tcols = p.sbuf(pfx + "tcols", [128, 64], F32)
    t16 = p.sbuf(pfx + "t16", [128, 16], F32)
    ones8 = p.sbuf(pfx + "ones8", [128, 8], F32)
    zcol = p.sbuf(pfx + "zcol", [128, 1], F32)
    onesb = p.sbuf(pfx + "onesb", [1, 128], BF16)
    onesbf = p.sbuf(pfx + "onesbf", [1, 128], F32)
    selm = p.sbuf(pfx + "selm", [128, 4 * 128], BF16)
    idf = p.sbuf(pfx + "idf", [128, 128], F32)
    swc = p.sbuf(pfx + "swc", [128, 32], F32)
    scr = {n: p.sbuf(pfx + "scr_" + n, [128, SUBF], BF16)
           for n in ["xmu", "t1", "t2", "vr1", "vr2", "vrs", "tk1", "tk2",
                     "tks", "vacc", "sacc", "tmp", "tmp2", "dx", "q"]}
    fout = [p.sbuf(pfx + f"fout{i}", [128, CHF], F32) for i in range(2)]

    psA = p.psum(pfx + "psA", [128, 512], F32)
    psB = p.psum(pfx + "psB", [128, 512], F32)
    psbc = p.psum(pfx + "psbc", [128, 1024], F32)
    psM = [p.psum(pfx + f"psM{i}", [128, 256], F32) for i in range(2)]
    psR = p.psum(pfx + "psR", [128, 256], F32)

    c_pe = p.sem(pfx + "c_pe")
    c_ve = p.sem(pfx + "c_ve")
    c_se = p.sem(pfx + "c_se")
    vch = p.sem(pfx + "vch")
    d_wl = p.sem(pfx + "d_wl")
    d_ldx = p.sem(pfx + "d_ldx")
    d_lx2 = p.sem(pfx + "d_lx2")
    d_out = p.sem(pfx + "d_out")
    gpi = p.sem(pfx + "gpi")
    gsem = p.sem(pfx + "gsem")
    gcell = p.sbuf(pfx + "gcell", [1, 4], F32)
    GRD = (p, gcell[0:1, 0:1], gsem)

    semmap = {"se": c_se, "ve": c_ve, "pe": c_pe}
    bankstate = {0: ("se", 0), 1: ("se", 0)}
    mask_hist = {0: ("ve", 0), 1: ("ve", 0)}

    @block.gpsimd
    def _(gp):
        for n in FFN_W_BF:
            if n in ("refw1l", "refw2l"):
                continue
            p.inc(gp.dma_start(out=wsb[n].ap(), in_=wd[n]), d_wl, 16)
        for n in FFN_W_F32:
            p.inc(gp.dma_start(out=csb[n].ap(), in_=wd[n]), d_wl, 16)
        p.inc(gp.dma_start(out=rb2b.ap(), in_=wd["rb2row"]), d_wl, 16)
        gp.memset(ones8[:, :], 1.0)
        gp.memset(zcol[:, :], 0.0)
        gp.memset(onesb[:, :], 1.0)
        gp.memset(onesbf[:, :], 1.0)
        gp.memset(hsT[:, :], 0.0)
        gp.memset(ohT[1][:, :], 0.0)
        gp.memset(ohT[2][:, :], 0.0)
        gp.memset(idf[:, :], 1.0)
        gp.affine_select(idf[:, :], idf[:, :], pattern=[[-1, 128]],
                         compare_op=ALU.is_equal, fill=0.0, base=0,
                         channel_multiplier=1)
        gp.memset(selm[:, :], 0.0)
        for mc in range(4):
            gp.memset(selm[0:8, mc * 128:(mc + 1) * 128], 1.0)
            gp.affine_select(selm[0:8, mc * 128:(mc + 1) * 128],
                             selm[0:8, mc * 128:(mc + 1) * 128],
                             pattern=[[1, 128]], compare_op=ALU.is_ge,
                             fill=0.0, base=128 * mc, channel_multiplier=-64)
            gp.affine_select(selm[0:8, mc * 128:(mc + 1) * 128],
                             selm[0:8, mc * 128:(mc + 1) * 128],
                             pattern=[[-1, 128]], compare_op=ALU.is_ge,
                             fill=0.0, base=63 - 128 * mc, channel_multiplier=64)
        p.inc(gp.memset(tcols[:, :], 0.0), gpi, 1)

    @block.scalar
    def _(scalar):
        act_switch(scalar, nc, SET_SIG_ERF, GRD)
        scalar.wait_ge(gpi, 1)
        scalar.wait_ge(d_wl, p.cnt[pfx + "d_wl"])
        p.inc(scalar.activation(out=swc[0:1, 20:21], in_=csb["swts"][0:1, 0:1],
                                func=AF.Sigmoid), c_se, 1)
    sw_done = p.cnt[pfx + "c_se"]

    @block.tensor
    def _(tensor):
        tensor.wait_ge(c_se, sw_done)
        p.inc(tensor.matmul(psR[:, 0:1], onesbf[0:1, :], swc[0:1, 20:21],
                            start=True, stop=True), c_pe, 1)
    bcv = p.cnt[pfx + "c_pe"]

    @block.vector
    def _(vector):
        vector.wait_ge(c_pe, bcv)
        w = p.inc(vector.tensor_copy(swc[:, 0:1], psR[:, 0:1]), vch, 1)
        vector.wait_ge(vch, w)
        w = p.inc(vector.tensor_scalar(swc[:, 4:5], swc[:, 0:1], -1.0, 1.0,
                                       op0=ALU.mult, op1=ALU.add), vch, 1)
        vector.wait_ge(vch, w)
        w = p.inc(vector.tensor_scalar_mul(swc[:, 8:9], swc[:, 0:1], DT_CONST),
                  vch, 1)
        vector.wait_ge(vch, w)
        w = p.inc(vector.tensor_scalar_mul(swc[:, 12:13], swc[:, 4:5],
                                           DT_CONST), vch, 1)
        vector.wait_ge(vch, w)
        p.inc(vector.tensor_scalar_mul(swc[:, 16:17], swc[:, 4:5], -1.0),
              vch, 1)
    sw_ve = p.cnt[pfx + "vch"]

    for c in range(NCF):
        tcol = c * CHF
        barrier = dict(pe=p.cnt[pfx + "c_pe"], ve=p.cnt[pfx + "c_ve"],
                       se=p.cnt[pfx + "c_se"], vh=p.cnt[pfx + "vch"])

        @block.sync
        def _(sync):
            sync.wait_ge(c_pe, barrier["pe"])
            sync.wait_ge(c_ve, barrier["ve"])
            sync.wait_ge(c_se, barrier["se"])
            sync.wait_ge(vch, max(barrier["vh"], sw_ve))
            srcx = bass.AP(xnT_d.tensor, tcol, [[T, 128], [128 * T, 8], [1, CHF]])
            dstx = intc.ap().rearrange("p (a t) -> p a t", a=8)
            p.inc(sync.dma_start(out=dstx, in_=srcx), d_ldx, 16)

        @block.gpsimd
        def _(gp):
            gp.wait_ge(c_pe, barrier["pe"])
            gp.wait_ge(c_ve, barrier["ve"])
            srcx = bass.AP(xnT_d.tensor, tcol, [[T, 128], [128 * T, 8], [1, CHF]])
            dstx = intf.ap().rearrange("p (a t) -> p a t", a=8)
            p.inc(gp.dma_start(out=dstx, in_=srcx), d_lx2, 16)
        ldx_done = p.cnt[pfx + "d_ldx"]
        lx2_done = p.cnt[pfx + "d_lx2"]

        @block.vector
        def _(vector):
            vector.wait_ge(c_pe, barrier["pe"])
            vector.wait_ge(c_se, barrier["se"])
            p.inc(vector.memset(vb[:, :], 0.0), c_ve, 1)
        v0_done = p.cnt[pfx + "c_ve"]

        def mmproj(wname, n_mc, n_kc, rhs_fn, evac_fn, wait0=None, nfree=CHF):
            sched = {}
            for mc in range(n_mc):
                ps = psA if mc % 2 == 0 else psB

                @block.tensor
                def _(tensor):
                    if mc == 0 and wait0:
                        for s_, v_ in wait0:
                            tensor.wait_ge(s_, v_)
                    if mc < 2:
                        kind, wv_ = bankstate[mc % 2]
                        tensor.wait_ge(semmap[kind], wv_)
                    else:
                        kind, wv_ = sched[mc - 2]
                        tensor.wait_ge(semmap[kind], wv_)
                    for kc in range(n_kc):
                        ins = tensor.matmul(
                            ps[:, 0:nfree],
                            wsb[wname][:, kc * (n_mc * 128) + mc * 128:
                                       kc * (n_mc * 128) + (mc + 1) * 128],
                            rhs_fn(kc),
                            start=(kc == 0), stop=(kc == n_kc - 1))
                    p.inc(ins, c_pe, 1)
                mmv = p.cnt[pfx + "c_pe"]
                sched[mc] = evac_fn(mc, ps, mmv)
                bankstate[mc % 2] = sched[mc]
            return sched

        @block.scalar
        def _(scalar):
            act_switch(scalar, nc, SET_GELU, GRD)

        def ev_gelu(dst, biascol, biasidx):
            def f(mc, ps, mmv):
                @block.scalar
                def _(scalar):
                    scalar.wait_ge(c_pe, mmv)
                    p.inc(scalar.activation(
                        out=dst(mc), in_=ps[:, 0:CHF], func=AF.Gelu,
                        bias=csb[biascol][:, biasidx(mc):biasidx(mc) + 1]),
                        c_se, 1)
                return ("se", p.cnt[pfx + "c_se"])
            return f

        mmproj("rw1l", 2, 8, lambda kc: intc[:, kc * CHF:(kc + 1) * CHF],
               ev_gelu(lambda mc: r1g[:, mc * CHF:(mc + 1) * CHF], "rb1c",
                       lambda mc: 0) if False else
               ev_gelu(lambda mc: r1g[:, mc * CHF:(mc + 1) * CHF], "rb1c",
                       lambda mc: mc) if False else
               ev_gelu(lambda mc: r1g[:, mc * CHF:(mc + 1) * CHF], "rb1c",
                       lambda mc: mc),
               wait0=[(d_ldx, ldx_done), (d_lx2, lx2_done), (c_ve, v0_done)])
        r1_done = p.cnt[pfx + "c_se"]

        if P2_STAGE < 2:
            continue
        @block.scalar
        def _(scalar):
            act_switch(scalar, nc, SET_EXP_LN, GRD)

        last_t2 = None
        for tt in range(4):
            @block.tensor
            def _(tensor):
                tensor.wait_ge(c_se, r1_done)
                if last_t2 is not None:
                    tensor.wait_ge(vch, last_t2)
                for kc in range(2):
                    tensor.matmul(psR[:, 0:8],
                                  r1g[:, kc * CHF + tt * 128:
                                      kc * CHF + (tt + 1) * 128],
                                  wsb["rw2l"][:, kc * 8:(kc + 1) * 8],
                                  start=(kc == 0), stop=False)
                p.inc(tensor.matmul(psR[:, 0:8], onesb[0:1, :], rb2b[0:1, 0:8],
                                    start=False, stop=True), c_pe, 1)
            mmv = p.cnt[pfx + "c_pe"]

            @block.vector
            def _(vector):
                vector.wait_ge(c_pe, mmv)
                w = p.inc(vector.reduce_max(tcols[:, 0:1], psR[:, 0:8],
                                            axis=AX.X, negate=True), vch, 1)
                vector.wait_ge(vch, w)
                p.inc(vector.tensor_copy(prb[:, :], psR[:, 0:8]), vch, 1)
            w_rm = p.cnt[pfx + "vch"]

            @block.scalar
            def _(scalar):
                scalar.wait_ge(vch, w_rm)
                p.inc(scalar.activation(out=prb[:, :], in_=prb[:, :],
                                        func=AF.Exp, bias=tcols[:, 0:1]),
                      c_se, 1)
            w_exp = p.cnt[pfx + "c_se"]

            @block.vector
            def _(vector):
                def step(ins):
                    wv = p.inc(ins, vch, 1)
                    vector.wait_ge(vch, wv)
                vector.wait_ge(c_se, w_exp)
                step(vector.reduce_max(tcols[:, 4:5], prb[:, :], axis=AX.X))
                step(vector.tensor_scalar(t16[:, 0:8], prb[:, :],
                                          tcols[:, 4:5], None, op0=ALU.is_ge))
                step(vector.tensor_tensor_scan(t16[:, 8:16], ones8[:, :],
                                               t16[:, 0:8], zcol[:, 0:1],
                                               op0=ALU.mult, op1=ALU.add))
                step(vector.tensor_scalar(t16[:, 8:16], t16[:, 8:16], 1.0,
                                          None, op0=ALU.is_equal))
                step(vector.tensor_mul(t16[:, 0:8], t16[:, 0:8], t16[:, 8:16]))
                step(vector.tensor_scalar(t16[:, 8:16], t16[:, 0:8], -1.0, 1.0,
                                          op0=ALU.mult, op1=ALU.add))
                step(vector.tensor_mul(prb[:, :], prb[:, :], t16[:, 8:16]))
                step(vector.reduce_max(tcols[:, 8:9], prb[:, :], axis=AX.X))
                step(vector.tensor_scalar(t16[:, 8:16], prb[:, :],
                                          tcols[:, 8:9], None, op0=ALU.is_ge))
                step(vector.tensor_tensor_scan(prb[:, :], ones8[:, :],
                                               t16[:, 8:16], zcol[:, 0:1],
                                               op0=ALU.mult, op1=ALU.add))
                step(vector.tensor_scalar(prb[:, :], prb[:, :], 1.0, None,
                                          op0=ALU.is_equal))
                step(vector.tensor_mul(t16[:, 8:16], t16[:, 8:16], prb[:, :]))
                step(vector.tensor_add(tcols[:, 12:13], tcols[:, 4:5],
                                       tcols[:, 8:9]))
                step(vector.reciprocal(tcols[:, 16:17], tcols[:, 12:13]))
                step(vector.tensor_mul(tcols[:, 20:21], tcols[:, 4:5],
                                       tcols[:, 16:17]))
                step(vector.tensor_mul(tcols[:, 24:25], tcols[:, 8:9],
                                       tcols[:, 16:17]))
            w_top = p.cnt[pfx + "vch"]

            trs = [(t16[:, 0:8], ohT[1], 8, None),
                   (t16[:, 8:16], ohT[2], 8, None),
                   (tcols[:, 20:21], w1row, 1, None),
                   (tcols[:, 24:25], w2row, 1, None)]
            for ti, (src_ap, dst_t, nn, _u) in enumerate(trs):
                @block.tensor
                def _(tensor):
                    tensor.wait_ge(vch, w_top)
                    if ti > 0:
                        tensor.wait_ge(vch, p.cnt[pfx + "vch"])
                    p.inc(tensor.transpose(psR[0:nn, 128:256], src_ap,
                                           idf[:, :]), c_pe, 1)
                w_tr = p.cnt[pfx + "c_pe"]

                @block.vector
                def _(vector):
                    vector.wait_ge(c_pe, w_tr)
                    if nn == 8:
                        p.inc(vector.tensor_copy(
                            dst_t[0:8, tt * 128:(tt + 1) * 128],
                            psR[0:8, 128:256]), vch, 1)
                    else:
                        p.inc(vector.tensor_copy(
                            dst_t[0:1, tt * 128:(tt + 1) * 128],
                            psR[0:1, 128:256]), vch, 1)
            last_t2 = p.cnt[pfx + "vch"]
        router_done = p.cnt[pfx + "vch"]

        @block.vector
        def _(vector):
            vector.wait_ge(vch, router_done)
            w = p.inc(vector.tensor_add(urow[0:1, :], w1row[0:1, :],
                                        w2row[0:1, :]), vch, 1)
            vector.wait_ge(vch, w)
            p.inc(vector.tensor_scalar(urow[0:1, :], urow[0:1, :],
                                       swc[0:1, 4:5], swc[0:1, 16:17],
                                       op0=ALU.mult, op1=ALU.add), vch, 1)
        urow_done = p.cnt[pfx + "vch"]

        for it in range(2 if P2_STAGE >= 3 else 0):
            @block.scalar
            def _(scalar):
                act_switch(scalar, nc, SET_GELU, GRD)

            def rhs_ctx(kc):
                if kc < 8:
                    return intc[:, kc * CHF:(kc + 1) * CHF]
                return vb[:, (kc - 8) * CHF:(kc - 7) * CHF]

            mmproj("w1catl", 4, 16, rhs_ctx,
                   ev_gelu(lambda mc: hat[:, mc * CHF:(mc + 1) * CHF],
                           "b1catc", lambda mc: mc))

            @block.tensor
            def _(tensor):
                kind, wv_ = bankstate[0]
                tensor.wait_ge(semmap[kind], wv_)
                for kc in range(16):
                    ins = tensor.matmul(psA[0:64, 0:CHF],
                                        wsb["sw1l"][:, kc * 64:(kc + 1) * 64],
                                        rhs_ctx(kc), start=(kc == 0),
                                        stop=(kc == 15))
                p.inc(ins, c_pe, 1)
            hs_mm = p.cnt[pfx + "c_pe"]

            @block.scalar
            def _(scalar):
                scalar.wait_ge(c_pe, hs_mm)
                p.inc(scalar.activation(out=hsT[0:64, :], in_=psA[0:64, 0:CHF],
                                        func=AF.Gelu,
                                        bias=csb["sb1c"][0:64, 0:1]), c_se, 1)
            hs_done = p.cnt[pfx + "c_se"]
            bankstate[0] = ("se", hs_done)

            mmproj("hw1l", 2, 8, lambda kc: intc[:, kc * CHF:(kc + 1) * CHF],
                   ev_gelu(lambda mc: h1g[:, mc * CHF:(mc + 1) * CHF], "hb1c",
                           lambda mc: mc))
            h1_done = p.cnt[pfx + "c_se"]

            @block.tensor
            def _(tensor):
                tensor.wait_ge(c_se, h1_done)
                kind, wv_ = bankstate[1]
                tensor.wait_ge(semmap[kind], wv_)
                for kc in range(2):
                    ins = tensor.matmul(psB[0:1, 0:CHF],
                                        wsb["hw2l"][:, kc:kc + 1],
                                        h1g[:, kc * CHF:(kc + 1) * CHF],
                                        start=(kc == 0), stop=(kc == 1))
                p.inc(ins, c_pe, 1)
            h2_mm = p.cnt[pfx + "c_pe"]

            @block.scalar
            def _(scalar):
                act_switch(scalar, nc, SET_SIG_ERF, GRD)
                scalar.wait_ge(c_pe, h2_mm)
                p.inc(scalar.activation(out=hrow[0:1, :], in_=psB[0:1, 0:CHF],
                                        func=AF.Sigmoid,
                                        bias=csb["hb2s"][0:1, 0:1]), c_se, 1)
            hrow_done = p.cnt[pfx + "c_se"]
            bankstate[1] = ("se", hrow_done)

            for s in range(NSF if P2_STAGE >= 4 else 0):
                scol = s * SUBF

                octrl_done = {}
                for k in (1, 2, 3):
                    if k == 3 and not (P2_S4 & 4):
                        continue
                    if k in (1, 2):
                        if not (P2_S4 & 1):
                            continue
                        for mc4 in range(4):
                            half = mc4 % 2

                            @block.tensor
                            def _(tensor):
                                if k == 1 and mc4 == 0:
                                    tensor.wait_ge(vch, router_done)
                                kind, wv_ = mask_hist[half]
                                tensor.wait_ge(semmap[kind], wv_)
                                p.inc(tensor.matmul(
                                    psM[half][:, 0:SUBF],
                                    selm[0:8, mc4 * 128:(mc4 + 1) * 128],
                                    ohT[k][0:8, scol:scol + SUBF],
                                    start=True, stop=True), c_pe, 1)
                            mv = p.cnt[pfx + "c_pe"]

                            @block.vector
                            def _(vector):
                                vector.wait_ge(c_pe, mv)
                                p.inc(vector.tensor_tensor(
                                    hkm[:, mc4 * SUBF:(mc4 + 1) * SUBF],
                                    hat[:, mc4 * CHF + scol:
                                        mc4 * CHF + scol + SUBF],
                                    psM[half][:, 0:SUBF],
                                    op=ALU.mult), c_ve, 1)
                            mask_hist[half] = ("ve", p.cnt[pfx + "c_ve"])
                        hk_done = p.cnt[pfx + "c_ve"]

                        def rhs_ok(kc, k=k):
                            if kc < 4:
                                return hkm[:, kc * SUBF:(kc + 1) * SUBF]
                            return ohT[k][:, scol:scol + SUBF]
                        wname = "w2augl"
                        n_kc = 5
                        wait0 = [(c_ve, hk_done)]
                    else:
                        def rhs_ok(kc):
                            return hsT[:, scol:scol + SUBF]
                        wname = "w2saugl"
                        n_kc = 1
                        wait0 = [(c_se, hs_done)]

                    def ev_ctrl(k=k):
                        def f(mc, ps, mmv):
                            @block.scalar
                            def _(scalar):
                                scalar.wait_ge(c_pe, mmv)
                                p.inc(scalar.copy(
                                    oT[k][:, mc * SUBF:(mc + 1) * SUBF],
                                    ps[:, 0:SUBF]), c_se, 1)
                            return ("se", p.cnt[pfx + "c_se"])
                        return f

                    if (P2_S4 & 2) or k == 3:
                        mmproj(wname, 24, n_kc, rhs_ok, ev_ctrl(), wait0=wait0,
                               nfree=SUBF)
                    octrl_done[k] = p.cnt[pfx + "c_se"]

                if P2_STAGE < 5:
                    continue
                if P2_S5 & 1:
                    @block.scalar
                    def _(scalar):
                        act_switch(scalar, nc, SET_SIG_ERF, GRD)
                        scalar.wait_ge(c_se, octrl_done[3])
                        for k in (1, 2, 3):
                            scalar.activation(out=oT[k][:, 0:8 * SUBF],
                                              in_=oT[k][:, 0:8 * SUBF],
                                              func=AF.Sigmoid)
                            ins = scalar.activation(
                                out=oT[k][:, 16 * SUBF:24 * SUBF],
                                in_=oT[k][:, 16 * SUBF:24 * SUBF],
                                func=AF.Sigmoid)
                        p.inc(ins, c_se, 1)
                sig_done = p.cnt[pfx + "c_se"]

                if P2_S5 & 2:
                    @block.vector
                    def _(vector):
                        vector.wait_ge(c_se, octrl_done[3])
                        for k in (1, 2, 3):
                            ins = vector.tensor_scalar_min(
                                oT[k][:, 8 * SUBF:16 * SUBF],
                                oT[k][:, 8 * SUBF:16 * SUBF], 60.0)
                        p.inc(ins, c_ve, 1)
                bc_done2 = p.cnt[pfx + "c_ve"]

                if P2_S5 & 4:
                    pe_now = p.cnt[pfx + "c_pe"]
                    ve_now = p.cnt[pfx + "c_ve"]

                    @block.scalar
                    def _(scalar):
                        if P2_EXPALT != 2:
                            scalar.wait_ge(c_pe, pe_now)
                            scalar.wait_ge(c_ve, ve_now)
                            act_switch(scalar, nc, SET_EXP_LN, GRD)
                        scalar.wait_ge(c_ve, bc_done2)
                        f1 = AF.Sigmoid if P2_EXPALT == 2 else AF.Exp
                        f2 = AF.Sigmoid if P2_EXPALT == 2 else AF.Ln
                        for k in (1, 2, 3):
                            dst1 = hat[:, 0:8 * SUBF] if P2_EXPALT == 1 \
                                else oT[k][:, 8 * SUBF:16 * SUBF]
                            w1_ = p.inc(scalar.activation(
                                out=dst1,
                                in_=oT[k][:, 8 * SUBF:16 * SUBF],
                                func=f1), c_se, 1)
                            scalar.wait_ge(c_se, w1_)
                            p.inc(scalar.activation(
                                out=oT[k][:, 8 * SUBF:16 * SUBF],
                                in_=dst1, func=f2,
                                bias=1.0), c_se, 1)
                sp_done = p.cnt[pfx + "c_se"]

                if P2_STAGE < 6:
                    continue
                @block.tensor
                def _(tensor):
                    tensor.wait_ge(vch, urow_done)
                    tensor.wait_ge(c_se, hrow_done)
                    tensor.wait_ge(c_ve, octrl_done.get("combfree", 0))
                    tensor.matmul(psbc[:, 0:256], onesbf[0:1, :],
                                  w1row[0:1, scol:scol + SUBF], start=True,
                                  stop=False)
                    tensor.matmul(psbc[:, 256:512], onesbf[0:1, :],
                                  w2row[0:1, scol:scol + SUBF], start=False,
                                  stop=True)
                    tensor.matmul(psbc[:, 512:768], onesbf[0:1, :],
                                  urow[0:1, scol:scol + SUBF], start=True,
                                  stop=False)
                    p.inc(tensor.matmul(psbc[:, 768:1024], onesbf[0:1, :],
                                        hrow[0:1, scol:scol + SUBF],
                                        start=False, stop=True), c_pe, 1)
                bc2 = p.cnt[pfx + "c_pe"]

                @block.vector
                def _(vector):
                    vector.wait_ge(c_se, sig_done)
                    vector.wait_ge(c_se, sp_done)
                    vector.wait_ge(c_pe, bc2)
                    for dc in range(8):
                        i_sl = intf[:, dc * CHF + scol: dc * CHF + scol + SUBF]
                        ic_sl = intc[:, dc * CHF + scol: dc * CHF + scol + SUBF]
                        v_sl = vb[:, dc * CHF + scol: dc * CHF + scol + SUBF]
                        dsl = slice(dc * SUBF, (dc + 1) * SUBF)
                        vector.tensor_scalar(scr["xmu"][:, :], i_sl,
                                             csb["muc"][:, dc:dc + 1], None,
                                             op0=ALU.subtract)
                        for k, vr, tk in ((1, "vr1", "tk1"), (2, "vr2", "tk2"),
                                          (3, "vrs", "tks")):
                            vector.tensor_tensor(scr["t1"][:, :],
                                                 oT[k][:, dsl], v_sl,
                                                 op=ALU.mult)
                            vector.tensor_tensor(
                                scr["t2"][:, :],
                                oT[k][:, 8 * SUBF + dc * SUBF:
                                      8 * SUBF + (dc + 1) * SUBF],
                                scr["xmu"][:, :], op=ALU.mult)
                            vector.tensor_tensor(scr[vr][:, :], scr["t1"][:, :],
                                                 scr["t2"][:, :],
                                                 op=ALU.subtract)
                            vector.tensor_tensor(
                                scr[tk][:, :],
                                oT[k][:, 16 * SUBF + dc * SUBF:
                                      16 * SUBF + (dc + 1) * SUBF],
                                scr[vr][:, :], op=ALU.mult)
                        vector.tensor_tensor(scr["t1"][:, :], scr["vr1"][:, :],
                                             psbc[:, 0:256], op=ALU.mult)
                        vector.tensor_tensor(scr["t2"][:, :], scr["vr2"][:, :],
                                             psbc[:, 256:512], op=ALU.mult)
                        vector.tensor_tensor(scr["vacc"][:, :], scr["t1"][:, :],
                                             scr["t2"][:, :], op=ALU.add)
                        vector.tensor_tensor(scr["t1"][:, :], scr["tk1"][:, :],
                                             psbc[:, 0:256], op=ALU.mult)
                        vector.tensor_tensor(scr["t2"][:, :], scr["tk2"][:, :],
                                             psbc[:, 256:512], op=ALU.mult)
                        vector.tensor_tensor(scr["sacc"][:, :], scr["t1"][:, :],
                                             scr["t2"][:, :], op=ALU.add)
                        vector.tensor_tensor(scr["t1"][:, :], i_sl,
                                             psbc[:, 512:768], op=ALU.mult)
                        vector.scalar_tensor_tensor(scr["t2"][:, :],
                                                    scr["tks"][:, :],
                                                    swc[:, 8:9],
                                                    scr["t1"][:, :],
                                                    op0=ALU.mult, op1=ALU.add)
                        vector.scalar_tensor_tensor(scr["dx"][:, :],
                                                    scr["sacc"][:, :],
                                                    swc[:, 12:13],
                                                    scr["t2"][:, :],
                                                    op0=ALU.mult, op1=ALU.add)
                        vector.scalar_tensor_tensor(scr["q"][:, :],
                                                    scr["vacc"][:, :],
                                                    swc[:, 4:5], v_sl,
                                                    op0=ALU.mult,
                                                    op1=ALU.subtract)
                        vector.scalar_tensor_tensor(scr["t2"][:, :],
                                                    scr["vrs"][:, :],
                                                    swc[:, 0:1], scr["q"][:, :],
                                                    op0=ALU.mult, op1=ALU.add)
                        vector.tensor_tensor(scr["t1"][:, :], scr["dx"][:, :],
                                             psbc[:, 768:1024], op=ALU.mult)
                        vector.tensor_tensor(i_sl, i_sl, scr["t1"][:, :],
                                             op=ALU.add)
                        vector.tensor_copy(ic_sl, i_sl)
                        vector.tensor_tensor(scr["t1"][:, :], scr["t2"][:, :],
                                             psbc[:, 768:1024], op=ALU.mult)
                        ins = vector.tensor_tensor(v_sl, v_sl, scr["t1"][:, :],
                                                   op=ALU.add)
                    p.inc(ins, c_ve, 1)
                octrl_done["combfree"] = p.cnt[pfx + "c_ve"]
                octrl_done["maskfree"] = p.cnt[pfx + "c_ve"]

        if P2_STAGE < 7:
            continue
        pe_before_ref = p.cnt[pfx + "c_pe"]

        @block.gpsimd
        def _(gp):
            gp.wait_ge(c_pe, pe_before_ref)
            p.inc(gp.dma_start(out=wsb["refw1l"].ap(), in_=wd["refw1l"]),
                  d_wl, 16)
        rw1_ld = p.cnt[pfx + "d_wl"]

        @block.scalar
        def _(scalar):
            act_switch(scalar, nc, SET_GELU, GRD)
        mmproj("refw1l", 16, 8, lambda kc: intc[:, kc * CHF:(kc + 1) * CHF],
               ev_gelu(lambda mc: ref1g_sl(mc), "refb1c", lambda mc: mc),
               wait0=[(d_wl, rw1_ld),
                      (c_ve, octrl_done.get("combfree", 0))])
        ref1_done = p.cnt[pfx + "c_se"]
        ref1_pe = p.cnt[pfx + "c_pe"]

        @block.gpsimd
        def _(gp):
            gp.wait_ge(c_pe, ref1_pe)
            p.inc(gp.dma_start(out=wsb["refw2l"].ap(), in_=wd["refw2l"]),
                  d_wl, 16)
        rw2_ld = p.cnt[pfx + "d_wl"]

        out_dma = {}

        def ev_ref2(mc, ps, mmv):
            @block.vector
            def _(vector):
                vector.wait_ge(c_pe, mmv)
                if out_dma.get(mc % 2):
                    vector.wait_ge(d_out, out_dma[mc % 2])
                p.inc(vector.tensor_scalar(fout[mc % 2][:, :], ps[:, 0:CHF],
                                           csb["refb2c"][:, mc:mc + 1],
                                           csb["iwc"][:, mc:mc + 1],
                                           op0=ALU.add, op1=ALU.mult), c_ve, 1)
            w_ = p.cnt[pfx + "c_ve"]

            @block.gpsimd
            def _(gp):
                gp.wait_ge(c_ve, w_)
                p.inc(gp.dma_start(
                    out=bass.AP(fT_d.tensor, mc * 128 * T + c * CHF,
                                [[T, 128], [1, CHF]]),
                    in_=fout[mc % 2][:, :]), d_out, 16)
            out_dma[mc % 2] = p.cnt[pfx + "d_out"]
            return ("ve", w_)

        mmproj("refw2l", 8, 16, ref1g_rhs, ev_ref2,
               wait0=[(c_se, ref1_done), (d_wl, rw2_ld)])

    @block.sync
    def _(sync):
        sync.wait_ge(d_out, p.cnt[pfx + "d_out"])


def build_phase2():
    p = P()
    nc = p.nc
    xnT_d = nc.dram_tensor("xn0T", [D, T], BF16, kind="ExternalInput").ap()
    wd = {n: nc.dram_tensor(n, FFN_SHAPES[n], F32, kind="ExternalInput").ap()
          for n in FFN_W_BF + FFN_W_F32}
    fT_d = nc.dram_tensor("f0T", [D, T], F32, kind="ExternalOutput").ap()
    with p.stack:
        with nc.Block() as block:
            emit_ffn0(p, block, xnT_d, wd, fT_d, pfx="f0_")
    return p.nc


def emit_assemble(p, block, x_d, xn0_d, aT_d, fT_d, out_d, pfx):
    nc = p.nc
    xs = [p.sbuf(f"{pfx}xs{i}", [128, D], F32) for i in range(2)]
    xn0s = [p.sbuf(f"{pfx}xn0s{i}", [128, D], F32) for i in range(2)] \
        if xn0_d is not None else None
    x1 = [p.sbuf(f"{pfx}x1{i}", [128, D], F32) for i in range(2)]
    aTs = [p.sbuf(f"{pfx}aTs{i}", [128, D], F32) for i in range(2)]
    fTs = [p.sbuf(f"{pfx}fTs{i}", [128, D], F32) for i in range(2)]
    idf = p.sbuf(f"{pfx}idf", [128, 128], F32)
    pt = [p.psum(f"{pfx}pt{i}", [128, 128], F32) for i in range(4)]

    ld = p.sem(pfx + "ld")
    ldg = p.sem(pfx + "ldg")
    so = p.sem(pfx + "so")
    cv = p.sem(pfx + "cv")
    cp = p.sem(pfx + "cp")
    gpi = p.sem(pfx + "gpi")

    nld = 32 if xn0_d is not None else 16

    @block.gpsimd
    def _(gp):
        gp.memset(idf[:, :], 1.0)
        p.inc(gp.affine_select(idf[:, :], idf[:, :], pattern=[[-1, 128]],
                               compare_op=ALU.is_equal, fill=0.0, base=0,
                               channel_multiplier=1), gpi, 1)
        for rt in range(NT):
            b = rt % 2
            if rt >= 2:
                gp.wait_ge(cp, 16 * (rt - 1))
            src_a = bass.AP(aT_d.tensor, rt * 128,
                            [[T, 128], [128 * T, 8], [1, 128]])
            p.inc(gp.dma_start(out=aTs[b].ap().rearrange("p (a t) -> p a t", a=8),
                               in_=src_a), ldg, 16)
            src_f = bass.AP(fT_d.tensor, rt * 128,
                            [[T, 128], [128 * T, 8], [1, 128]])
            p.inc(gp.dma_start(out=fTs[b].ap().rearrange("p (a t) -> p a t", a=8),
                               in_=src_f), ldg, 16)

    @block.sync
    def _(sync):
        for rt in range(NT):
            b = rt % 2
            if rt >= 2:
                sync.wait_ge(cv, 5 * (rt - 1))
            p.inc(sync.dma_start(out=xs[b][:, :],
                                 in_=x_d[rt * 128:(rt + 1) * 128, :]), ld, 16)
            if xn0_d is not None:
                p.inc(sync.dma_start(out=xn0s[b][:, :],
                                     in_=xn0_d[rt * 128:(rt + 1) * 128, :]),
                      ld, 16)

    for rt in range(NT):
        b = rt % 2

        @block.tensor
        def _(tensor):
            if rt == 0:
                tensor.wait_ge(gpi, 1)
            tensor.wait_ge(ldg, 32 * (rt + 1))
            for w in range(4):
                srcT = aTs[b] if w < 2 else fTs[b]
                if rt * 4 + w >= 1:
                    tensor.wait_ge(cv, 5 * rt + w + 1 if w >= 1
                                   else 5 * (rt - 1) + 5)
                for i in range(4):
                    c8 = (w % 2) * 4 + i
                    p.inc(tensor.transpose(
                        pt[i][:, :], srcT[:, c8 * 128:(c8 + 1) * 128],
                        idf[:, :]), cp, 1)

        @block.vector
        def _(vector):
            vector.wait_ge(ld, nld * (rt + 1))
            if rt >= 2:
                vector.wait_ge(so, 16 * (rt - 1))
            if xn0_d is not None:
                ins = vector.tensor_add(x1[b][:, :], xs[b][:, :],
                                        xn0s[b][:, :])
            else:
                ins = vector.tensor_copy(x1[b][:, :], xs[b][:, :])
            p.inc(ins, cv, 1)
            for w in range(4):
                vector.wait_ge(cp, 16 * rt + 4 * (w + 1))
                for i in range(4):
                    c8 = (w % 2) * 4 + i
                    ins = vector.tensor_add(x1[b][:, c8 * 128:(c8 + 1) * 128],
                                            x1[b][:, c8 * 128:(c8 + 1) * 128],
                                            pt[i][:, :])
                    if i == 3:
                        p.inc(ins, cv, 1)

        @block.scalar
        def _(scalar):
            scalar.wait_ge(cv, 5 * (rt + 1))
            p.inc(scalar.dma_start(out=out_d[rt * 128:(rt + 1) * 128, :],
                                   in_=x1[b][:, :]), so, 16)

    @block.sync
    def _(sync):
        sync.wait_ge(so, p.cnt[pfx + "so"])


def build_phase3a():
    p = P()
    nc = p.nc
    x_d = nc.dram_tensor("x", [T, D], F32, kind="ExternalInput").ap()
    xn0_d = nc.dram_tensor("xn0", [T, D], F32, kind="ExternalInput").ap()
    aT_d = nc.dram_tensor("attn0T", [D, T], F32, kind="ExternalInput").ap()
    fT_d = nc.dram_tensor("f0T", [D, T], F32, kind="ExternalInput").ap()
    x1_d = nc.dram_tensor("x1", [T, D], F32, kind="ExternalOutput").ap()
    with p.stack:
        with nc.Block() as block:
            emit_assemble(p, block, x_d, xn0_d, aT_d, fT_d, x1_d, pfx="p3a_")
    return p.nc


def build_phase3b():
    p = P()
    nc = p.nc
    x1_d = nc.dram_tensor("x1", [T, D], F32, kind="ExternalInput").ap()
    xn_d = nc.dram_tensor("xn1", [T, D], F32, kind="ExternalOutput").ap()
    xnTb_d = nc.dram_tensor("xn1Tb", [D, T], BF16, kind="ExternalOutput").ap()
    xnTf_d = nc.dram_tensor("xn1Tf", [D, T], F32, kind="ExternalOutput").ap()
    with p.stack:
        with nc.Block() as block:
            emit_ln_transpose(p, block, x1_d, xn_d, xnTb_d, pfx="p3b_",
                              xnTf_d=xnTf_d)
    return p.nc


def build_phase6():
    p = P()
    nc = p.nc
    x1_d = nc.dram_tensor("x1", [T, D], F32, kind="ExternalInput").ap()
    aT_d = nc.dram_tensor("attn1T", [D, T], F32, kind="ExternalInput").ap()
    fT_d = nc.dram_tensor("ffn1T", [D, T], F32, kind="ExternalInput").ap()
    out_d = nc.dram_tensor("out", [T, D], F32, kind="ExternalOutput").ap()
    with p.stack:
        with nc.Block() as block:
            emit_assemble(p, block, x1_d, None, aT_d, fT_d, out_d, pfx="p6_")
    return p.nc


CH4 = 512
NC4 = T // CH4
INV2PI = 0.15915494309189535
PIC1 = 6.28125
PIC2 = 0.0019353071795862605
PIH = 1.5707963267948966
P4_DBG = 0


def prep_l1a_weights(g):
    def lay(w):
        w = np.ascontiguousarray(np.asarray(w, np.float32))
        i, o = w.shape
        return np.ascontiguousarray(
            w.reshape(i // 128, 128, o).transpose(1, 0, 2).reshape(128, -1))

    def col(v):
        v = np.asarray(v, np.float32).reshape(-1)
        out = np.zeros((128, max(1, (len(v) + 127) // 128)), np.float32)
        for i, x in enumerate(v):
            out[i % 128, i // 128] = x
        return out

    return {
        "WoT": lay(np.asarray(g["W_omega"], np.float32).T).astype(BFNP),
        "W1T": lay(np.asarray(g["W1"], np.float32).T).astype(BFNP),
        "W2T": lay(np.asarray(g["W2"], np.float32).T).astype(BFNP),
        "bomc": col(g["b_omega"]), "lsc": col(g["log_scale"]),
        "lgc": col(g["ln_gamma"]), "lbc": col(g["ln_beta"]),
        "b1c": col(g["b1"]), "b2c": col(g["b2"]),
    }


L1A_W_BF = ["WoT", "W1T", "W2T"]
L1A_W_F32 = ["bomc", "lsc", "lgc", "lbc", "b1c", "b2c"]
L1A_SHAPES = {"WoT": [128, 8192], "W1T": [128, 65536], "W2T": [128, 16384],
              "bomc": [128, 8], "lsc": [128, 8], "lgc": [128, 32],
              "lbc": [128, 32], "b1c": [128, 16], "b2c": [128, 8]}


def emit_attn1(p, block, xnTb_d, xnTf_d, wd, outT_d, pfx="a1_"):
    nc = p.nc
    wbufA = p.sbuf(pfx + "wbufA", [128, 16384], BF16)
    wbufB = p.sbuf(pfx + "wbufB", [128, 16384], BF16)
    cols = {n: p.sbuf(pfx + n, L1A_SHAPES[n], F32) for n in L1A_W_F32}
    exls = p.sbuf(pfx + "exls", [128, 8], F32)
    xnb = p.sbuf(pfx + "xnb", [128, 8 * CH4], BF16)
    xnf = p.sbuf(pfx + "xnf", [128, 8 * CH4], F32)
    omsc = p.sbuf(pfx + "omsc", [128, CH4], F32)
    phis = p.sbuf(pfx + "phis", [128, CH4], F32)
    sphi = p.sbuf(pfx + "sphi", [128, CH4], BF16)
    cphi = p.sbuf(pfx + "cphi", [128, CH4], BF16)
    ctxb = p.sbuf(pfx + "ctxb", [128, 32 * CH4], BF16)
    h1b = p.sbuf(pfx + "h1b", [128, 16 * CH4], BF16)
    mem = [p.sbuf(pfx + f"mem{i}", [128, CH4], F32) for i in range(2)]
    cumt = p.sbuf(pfx + "cumt", [128, CH4], F32)
    rsc = p.sbuf(pfx + "rsc", [128, CH4], F32)
    posv = [p.sbuf(pfx + f"posv{i}", [128, CH4], F32) for i in range(2)]
    onescan = p.sbuf(pfx + "onescan", [128, CH4], F32)
    iotc = p.sbuf(pfx + "iotc", [128, 4], F32)
    kint = p.sbuf(pfx + "kint", [128, CH4], I32)
    carr = p.sbuf(pfx + "carr", [128, 24], F32)
    lmu = p.sbuf(pfx + "lmu", [1, CH4], F32)
    lrs = p.sbuf(pfx + "lrs", [1, CH4], F32)
    ones128b = p.sbuf(pfx + "ones128b", [128, 1], BF16)
    onesbf = p.sbuf(pfx + "onesbf", [1, 128], F32)
    pihc = p.sbuf(pfx + "pihc", [128, 1], F32)
    epsc = p.sbuf(pfx + "epsc", [1, 1], F32)
    outstg = [p.sbuf(pfx + f"outstg{i}", [128, CH4], F32) for i in range(2)]
    gcell = p.sbuf(pfx + "gcell", [1, 4], F32)

    psA = p.psum(pfx + "psA", [128, 512], F32)
    psB = p.psum(pfx + "psB", [128, 512], F32)
    psS = p.psum(pfx + "psS", [128, 512], F32)
    psQ = p.psum(pfx + "psQ", [128, 512], F32)
    psbc = p.psum(pfx + "psbc", [128, 512], F32)
    psbc2 = p.psum(pfx + "psbc2", [128, 512], F32)

    c_pe = p.sem(pfx + "c_pe")
    c_ve = p.sem(pfx + "c_ve")
    c_se = p.sem(pfx + "c_se")
    vch = p.sem(pfx + "vch")
    d_wl = p.sem(pfx + "d_wl")
    d_ws = p.sem(pfx + "d_ws")
    d_ldx = p.sem(pfx + "d_ldx")
    d_out = p.sem(pfx + "d_out")
    gpi = p.sem(pfx + "gpi")
    gsem = p.sem(pfx + "gsem")
    GRD = (p, gcell[0:1, 0:1], gsem)
    semmap = {"se": c_se, "ve": c_ve, "pe": c_pe}
    bankstate = {0: ("se", 0), 1: ("se", 0)}
    busyA = 0
    busyB = 0
    out_hist = [0, 0]

    @block.gpsimd
    def _(gp):
        for n in L1A_W_F32:
            p.inc(gp.dma_start(out=cols[n].ap(), in_=wd[n]), d_wl, 16)
        gp.memset(carr[:, :], 0.0)
        gp.memset(ones128b[:, :], 1.0)
        gp.memset(onesbf[:, :], 1.0)
        gp.memset(onescan[:, :], 1.0)
        gp.memset(pihc[:, :], PIH)
        p.inc(gp.memset(epsc[:, :], LN_EPS), gpi, 1)

    @block.scalar
    def _(scalar):
        act_switch(scalar, nc, SET_EXP_LN)
        scalar.wait_ge(d_wl, p.cnt[pfx + "d_wl"])
        scalar.wait_ge(gpi, 1)
        p.inc(scalar.activation(out=exls[:, :], in_=cols["lsc"][:, :],
                                func=AF.Exp), c_se, 1)
    exls_done = p.cnt[pfx + "c_se"]

    for c in range(NC4):
        tcol = c * CH4
        barrier = dict(pe=p.cnt[pfx + "c_pe"], ve=p.cnt[pfx + "c_ve"],
                       se=p.cnt[pfx + "c_se"], vh=p.cnt[pfx + "vch"])

        @block.sync
        def _(sync):
            sync.wait_ge(c_pe, barrier["pe"])
            sync.wait_ge(c_ve, barrier["ve"])
            sync.wait_ge(c_se, barrier["se"])
            sync.wait_ge(vch, barrier["vh"])
            srcb = bass.AP(xnTb_d.tensor, tcol, [[T, 128], [128 * T, 8], [1, CH4]])
            p.inc(sync.dma_start(out=xnb.ap().rearrange("p (a t) -> p a t", a=8),
                                 in_=srcb), d_ldx, 16)
            srcf = bass.AP(xnTf_d.tensor, tcol, [[T, 128], [128 * T, 8], [1, CH4]])
            p.inc(sync.dma_start(out=xnf.ap().rearrange("p (a t) -> p a t", a=8),
                                 in_=srcf), d_ldx, 16)
        ldx_done = p.cnt[pfx + "d_ldx"]

        @block.gpsimd
        def _(gp):
            if busyA:
                gp.wait_ge(c_pe, busyA)
            p.inc(gp.dma_start(out=wbufA[:, 0:8192], in_=wd["WoT"]), d_ws, 16)
            p.inc(gp.memset(iotc[:, 0:1], float(tcol)), gpi, 1)
            if busyB:
                gp.wait_ge(c_pe, busyB)
            src = bass.AP(wd["W1T"].tensor, 0 * 512,
                          [[65536, 128], [2048, 32], [1, 512]])
            p.inc(gp.dma_start(out=wbufB.ap().rearrange("p (a t) -> p a t",
                                                        a=32), in_=src),
                  d_ws, 16)
        wot_ld = p.cnt[pfx + "d_ws"] - 16
        g_ld = {0: p.cnt[pfx + "d_ws"]}
        iota_done = p.cnt[pfx + "gpi"]

        @block.vector
        def _(vector):
            vector.wait_ge(gpi, iota_done)
            p.inc(vector.tensor_tensor_scan(posv[0][:, :], onescan[:, :],
                                            onescan[:, :], iotc[:, 0:1],
                                            op0=ALU.mult, op1=ALU.add),
                  vch, 1)
        posc_ve = p.cnt[pfx + "vch"]

        @block.scalar
        def _(scalar):
            act_switch(scalar, nc, SET_EXP_LN, GRD)
            scalar.wait_ge(vch, posc_ve)
            w = p.inc(scalar.activation(out=posv[1][:, :], in_=posv[0][:, :],
                                        func=AF.Ln), c_se, 1)
            scalar.wait_ge(c_se, w)
            p.inc(scalar.activation(out=posv[0][:, :], in_=posv[1][:, :],
                                    func=AF.Exp, scale=-0.5), c_se, 1)
        pos_se = p.cnt[pfx + "c_se"]

        @block.vector
        def _(vector):
            vector.wait_ge(c_se, pos_se)
            p.inc(vector.tensor_mul(posv[1][:, :], posv[0][:, :],
                                    posv[0][:, :]), vch, 1)
        pos_done = p.cnt[pfx + "vch"]

        @block.scalar
        def _(scalar):
            act_switch(scalar, nc, SET_SIN, GRD)

        omsc_free = 0
        dve2_done = 0
        for mc in range(8):
            ps = psA if mc % 2 == 0 else psB

            @block.tensor
            def _(tensor):
                if mc == 0:
                    tensor.wait_ge(d_ldx, ldx_done)
                    tensor.wait_ge(d_ws, wot_ld)
                kind, wv_ = bankstate[mc % 2]
                tensor.wait_ge(semmap[kind], wv_)
                for kc in range(8):
                    ins = tensor.matmul(
                        ps[:, :], wbufA[:, kc * 1024 + mc * 128:
                                        kc * 1024 + (mc + 1) * 128],
                        xnb[:, kc * CH4:(kc + 1) * CH4],
                        start=(kc == 0), stop=(kc == 7))
                p.inc(ins, c_pe, 1)
            mmv = p.cnt[pfx + "c_pe"]

            @block.scalar
            def _(scalar):
                scalar.wait_ge(c_pe, mmv)
                if mc == 0:
                    scalar.wait_ge(c_se, exls_done)
                if omsc_free:
                    scalar.wait_ge(vch, omsc_free)
                p.inc(scalar.activation(out=omsc[:, :], in_=ps[:, :],
                                        func=AF.Identity,
                                        bias=cols["bomc"][:, mc:mc + 1]),
                      c_se, 1)
            evv = p.cnt[pfx + "c_se"]
            bankstate[mc % 2] = ("se", evv)

            @block.vector
            def _(vector):
                vector.wait_ge(c_se, evv)
                if mc == 0:
                    vector.wait_ge(vch, pos_done)
                vector.tensor_scalar(omsc[:, :], omsc[:, :],
                                     exls[:, mc:mc + 1], None, op0=ALU.mult)
                vector.tensor_mul(omsc[:, :], omsc[:, :], posv[0][:, :])
                w = p.inc(vector.tensor_tensor_scan(
                    phis[:, :], onescan[:, :], omsc[:, :],
                    carr[:, mc:mc + 1], op0=ALU.mult, op1=ALU.add), vch, 1)
                vector.wait_ge(vch, w)
                vector.tensor_copy(carr[:, mc:mc + 1], phis[:, CH4 - 1:CH4])
                vector.tensor_scalar_mul(rsc[:, :], phis[:, :], INV2PI)
                vector.tensor_copy(kint[:, :], rsc[:, :])
                vector.tensor_copy(rsc[:, :], kint[:, :])
                vector.scalar_tensor_tensor(phis[:, :], rsc[:, :], -PIC1,
                                            phis[:, :], op0=ALU.mult,
                                            op1=ALU.add)
                p.inc(vector.scalar_tensor_tensor(phis[:, :], rsc[:, :],
                                                  -PIC2, phis[:, :],
                                                  op0=ALU.mult, op1=ALU.add),
                      vch, 1)
            omsc_free = p.cnt[pfx + "vch"] - 1
            rr_done = p.cnt[pfx + "vch"]

            @block.scalar
            def _(scalar):
                scalar.wait_ge(vch, rr_done)
                p.inc(scalar.activation(out=sphi[:, :], in_=phis[:, :],
                                        func=AF.Sin), c_se, 1)
                w = p.inc(scalar.activation(out=rsc[:, :], in_=phis[:, :],
                                            func=AF.Abs), c_se, 1)
                scalar.wait_ge(c_se, w)
                p.inc(scalar.activation(out=cphi[:, :], in_=rsc[:, :],
                                        func=AF.Sin, scale=-1.0,
                                        bias=pihc[:, 0:1]), c_se, 1)
            trig_done = p.cnt[pfx + "c_se"]

            @block.vector
            def _(vector):
                vector.wait_ge(c_se, trig_done)
                csl = slice(mc * CH4, (mc + 1) * CH4)
                vector.tensor_mul(cumt[:, :], xnf[:, csl], cphi[:, :])
                vector.tensor_copy(ctxb[:, 0 * 4096 + mc * CH4:
                                        0 * 4096 + (mc + 1) * CH4],
                                   cumt[:, :])
                w = p.inc(vector.tensor_tensor_scan(
                    rsc[:, :], onescan[:, :], cumt[:, :],
                    carr[:, 8 + mc:9 + mc], op0=ALU.mult, op1=ALU.add), vch, 1)
                vector.wait_ge(vch, w)
                vector.tensor_copy(carr[:, 8 + mc:9 + mc],
                                   rsc[:, CH4 - 1:CH4])
                vector.tensor_mul(mem[0][:, :], rsc[:, :], posv[1][:, :])
                vector.tensor_mul(cumt[:, :], xnf[:, csl], sphi[:, :])
                vector.tensor_copy(ctxb[:, 1 * 4096 + mc * CH4:
                                        1 * 4096 + (mc + 1) * CH4],
                                   cumt[:, :])
                w = p.inc(vector.tensor_tensor_scan(
                    rsc[:, :], onescan[:, :], cumt[:, :],
                    carr[:, 16 + mc:17 + mc], op0=ALU.mult, op1=ALU.add),
                    vch, 1)
                vector.wait_ge(vch, w)
                vector.tensor_copy(carr[:, 16 + mc:17 + mc],
                                   rsc[:, CH4 - 1:CH4])
                vector.tensor_mul(mem[1][:, :], rsc[:, :], posv[1][:, :])
                vector.tensor_mul(cumt[:, :], mem[0][:, :], cphi[:, :])
                vector.tensor_mul(rsc[:, :], mem[1][:, :], sphi[:, :])
                vector.tensor_add(ctxb[:, 2 * 4096 + mc * CH4:
                                       2 * 4096 + (mc + 1) * CH4],
                                  cumt[:, :], rsc[:, :])
                vector.tensor_mul(cumt[:, :], mem[1][:, :], cphi[:, :])
                vector.tensor_mul(rsc[:, :], mem[0][:, :], sphi[:, :])
                p.inc(vector.tensor_sub(ctxb[:, 3 * 4096 + mc * CH4:
                                             3 * 4096 + (mc + 1) * CH4],
                                        cumt[:, :], rsc[:, :]), vch, 1)
            dve2_done = p.cnt[pfx + "vch"]
        omega_mms = p.cnt[pfx + "c_pe"]
        busyA = omega_mms
        ctx_done = dve2_done

        if P4_DBG & 2 and c == 0:
            dbgw_d = nc.dram_tensor("dbgwot", [128, 8192], BF16,
                                    kind="ExternalOutput").ap()
            dbgxb_d = nc.dram_tensor("dbgxnb", [128, 8 * CH4], BF16,
                                     kind="ExternalOutput").ap()

            @block.sync
            def _(sync):
                sync.wait_ge(c_pe, omega_mms)
                p.inc(sync.dma_start(out=dbgw_d[:, :], in_=wbufA[:, 0:8192]),
                      d_out, 16)
                p.inc(sync.dma_start(out=dbgxb_d[:, :], in_=xnb[:, :]),
                      d_out, 16)

        @block.gpsimd
        def _(gp):
            gp.wait_ge(c_pe, omega_mms)
            if P4_DBG & 2 and c == 0:
                gp.wait_ge(d_out, p.cnt[pfx + "d_out"])
            src = bass.AP(wd["W1T"].tensor, 1 * 512,
                          [[65536, 128], [2048, 32], [1, 512]])
            p.inc(gp.dma_start(out=wbufA.ap().rearrange("p (a t) -> p a t",
                                                        a=32), in_=src),
                  d_ws, 16)
        g_ld[1] = p.cnt[pfx + "d_ws"]

        if P4_DBG and c == 0:
            dbgc_d = nc.dram_tensor("dbgctx", [128, 32 * CH4], BF16,
                                    kind="ExternalOutput").ap()
            dbgp_d = nc.dram_tensor("dbgpos", [128, 8 * CH4], F32,
                                    kind="ExternalOutput").ap()
            dbgx_d = nc.dram_tensor("dbgxnf", [128, 8 * CH4], F32,
                                    kind="ExternalOutput").ap()
            dbgt_d = nc.dram_tensor("dbgtrig", [128, 2 * CH4], BF16,
                                    kind="ExternalOutput").ap()

            @block.sync
            def _(sync):
                sync.wait_ge(vch, ctx_done)
                p.inc(sync.dma_start(out=dbgc_d[:, :], in_=ctxb[:, :]),
                      d_out, 16)
                p.inc(sync.dma_start(out=dbgp_d[:, 0:CH4], in_=posv[0][:, :]),
                      d_out, 16)
                p.inc(sync.dma_start(out=dbgp_d[:, CH4:2 * CH4],
                                     in_=posv[1][:, :]), d_out, 16)
                p.inc(sync.dma_start(out=dbgp_d[:, 2 * CH4:3 * CH4],
                                     in_=phis[:, :]), d_out, 16)
                p.inc(sync.dma_start(out=dbgp_d[:, 3 * CH4:4 * CH4],
                                     in_=omsc[:, :]), d_out, 16)
                p.inc(sync.dma_start(out=dbgp_d[:, 4 * CH4:5 * CH4],
                                     in_=mem[0][:, :]), d_out, 16)
                p.inc(sync.dma_start(out=dbgp_d[:, 5 * CH4:6 * CH4],
                                     in_=mem[1][:, :]), d_out, 16)
                p.inc(sync.dma_start(out=dbgx_d[:, :], in_=xnf[:, :]),
                      d_out, 16)
                p.inc(sync.dma_start(out=dbgt_d[:, 0:CH4], in_=sphi[:, :]),
                      d_out, 16)
                p.inc(sync.dma_start(out=dbgt_d[:, CH4:2 * CH4],
                                     in_=cphi[:, :]), d_out, 16)

        @block.scalar
        def _(scalar):
            act_switch(scalar, nc, SET_EXP_LN, GRD)
        sq_mm = {}
        for cc in range(32):
            sqb = h1b[:, (cc % 2) * CH4:(cc % 2 + 1) * CH4]

            @block.scalar
            def _(scalar):
                if cc == 0:
                    scalar.wait_ge(vch, ctx_done)
                if cc >= 2:
                    scalar.wait_ge(c_pe, sq_mm[cc - 2])
                p.inc(scalar.activation(out=sqb,
                                        in_=ctxb[:, cc * CH4:(cc + 1) * CH4],
                                        func=AF.Square), c_se, 1)
            sqv = p.cnt[pfx + "c_se"]

            @block.tensor
            def _(tensor):
                if cc == 0:
                    tensor.wait_ge(vch, ctx_done)
                tensor.matmul(psS[0:1, :], ones128b[:, 0:1],
                              ctxb[:, cc * CH4:(cc + 1) * CH4],
                              start=(cc == 0), stop=(cc == 31))
                tensor.wait_ge(c_se, sqv)
                p.inc(tensor.matmul(psQ[0:1, :], ones128b[:, 0:1], sqb,
                                    start=(cc == 0), stop=(cc == 31)),
                      c_pe, 1)
            sq_mm[cc] = p.cnt[pfx + "c_pe"]
        sums_done = p.cnt[pfx + "c_pe"]

        @block.vector
        def _(vector):
            vector.wait_ge(c_pe, sums_done)
            w = p.inc(vector.tensor_scalar_mul(lmu[0:1, :], psS[0:1, :],
                                               1.0 / 4096.0), vch, 1)
            vector.wait_ge(vch, w)
            w = p.inc(vector.tensor_mul(cumt[0:1, :], lmu[0:1, :],
                                        lmu[0:1, :]), vch, 1)
            vector.wait_ge(vch, w)
            p.inc(vector.scalar_tensor_tensor(rsc[0:1, :], psQ[0:1, :],
                                              1.0 / 4096.0, cumt[0:1, :],
                                              op0=ALU.mult,
                                              op1=ALU.subtract), vch, 1)
        var_done = p.cnt[pfx + "vch"]

        @block.scalar
        def _(scalar):
            scalar.wait_ge(vch, var_done)
            w = p.inc(scalar.activation(out=cumt[0:1, :], in_=rsc[0:1, :],
                                        func=AF.Ln, bias=epsc[0:1, 0:1]),
                      c_se, 1)
            scalar.wait_ge(c_se, w)
            p.inc(scalar.activation(out=lrs[0:1, :], in_=cumt[0:1, :],
                                    func=AF.Exp, scale=-0.5), c_se, 1)
        rstd_done = p.cnt[pfx + "c_se"]

        @block.tensor
        def _(tensor):
            tensor.wait_ge(c_se, rstd_done)
            tensor.matmul(psbc[:, :], onesbf[0:1, :], lmu[0:1, :],
                          start=True, stop=True)
            p.inc(tensor.matmul(psbc2[:, :], onesbf[0:1, :], lrs[0:1, :],
                                start=True, stop=True), c_pe, 1)
        bc_done = p.cnt[pfx + "c_pe"]

        @block.vector
        def _(vector):
            vector.wait_ge(c_pe, bc_done)
            for cc in range(32):
                vector.tensor_sub(cumt[:, :], ctxb[:, cc * CH4:(cc + 1) * CH4],
                                  psbc[:, :])
                vector.tensor_mul(cumt[:, :], cumt[:, :], psbc2[:, :])
                ins = vector.tensor_scalar(ctxb[:, cc * CH4:(cc + 1) * CH4],
                                           cumt[:, :],
                                           cols["lgc"][:, cc:cc + 1],
                                           cols["lbc"][:, cc:cc + 1],
                                           op0=ALU.mult, op1=ALU.add)
            p.inc(ins, c_ve, 1)
        ln_done = p.cnt[pfx + "c_ve"]

        @block.scalar
        def _(scalar):
            act_switch(scalar, nc, SET_GELU, GRD)
        gmm_done = {}
        for g4 in range(4):
            wb = wbufB if g4 % 2 == 0 else wbufA
            if g4 >= 2:
                @block.gpsimd
                def _(gp):
                    gp.wait_ge(c_pe, gmm_done[g4 - 2])
                    src = bass.AP(wd["W1T"].tensor, g4 * 512,
                                  [[65536, 128], [2048, 32], [1, 512]])
                    p.inc(gp.dma_start(
                        out=wb.ap().rearrange("p (a t) -> p a t", a=32),
                        in_=src), d_ws, 16)
                g_ld[g4] = p.cnt[pfx + "d_ws"]

            for ml in range(4):
                mc = g4 * 4 + ml
                ps = psA if mc % 2 == 0 else psB

                @block.tensor
                def _(tensor):
                    if ml == 0:
                        tensor.wait_ge(d_ws, g_ld[g4])
                        if g4 == 0:
                            tensor.wait_ge(c_ve, ln_done)
                    kind, wv_ = bankstate[mc % 2]
                    tensor.wait_ge(semmap[kind], wv_)
                    for kc in range(32):
                        ins = tensor.matmul(
                            ps[:, :], wb[:, kc * 512 + ml * 128:
                                         kc * 512 + (ml + 1) * 128],
                            ctxb[:, kc * CH4:(kc + 1) * CH4],
                            start=(kc == 0), stop=(kc == 31))
                    p.inc(ins, c_pe, 1)
                mmv = p.cnt[pfx + "c_pe"]

                @block.scalar
                def _(scalar):
                    scalar.wait_ge(c_pe, mmv)
                    p.inc(scalar.activation(
                        out=h1b[:, mc * CH4:(mc + 1) * CH4], in_=ps[:, :],
                        func=AF.Gelu, bias=cols["b1c"][:, mc:mc + 1]),
                        c_se, 1)
                bankstate[mc % 2] = ("se", p.cnt[pfx + "c_se"])
            gmm_done[g4] = p.cnt[pfx + "c_pe"]
        busyA = gmm_done[3]
        h1_done = p.cnt[pfx + "c_se"]

        @block.gpsimd
        def _(gp):
            gp.wait_ge(c_pe, gmm_done[2])
            p.inc(gp.dma_start(out=wbufB.ap(), in_=wd["W2T"]), d_ws, 16)
        w2_ld = p.cnt[pfx + "d_ws"]

        for mc in range(8):
            ps = psA if mc % 2 == 0 else psB

            @block.tensor
            def _(tensor):
                if mc == 0:
                    tensor.wait_ge(d_ws, w2_ld)
                    tensor.wait_ge(c_se, h1_done)
                kind, wv_ = bankstate[mc % 2]
                tensor.wait_ge(semmap[kind], wv_)
                for kc in range(16):
                    ins = tensor.matmul(
                        ps[:, :], wbufB[:, kc * 1024 + mc * 128:
                                        kc * 1024 + (mc + 1) * 128],
                        h1b[:, kc * CH4:(kc + 1) * CH4],
                        start=(kc == 0), stop=(kc == 15))
                p.inc(ins, c_pe, 1)
            mmv = p.cnt[pfx + "c_pe"]

            @block.vector
            def _(vector):
                vector.wait_ge(c_pe, mmv)
                if out_hist[mc % 2]:
                    vector.wait_ge(d_out, out_hist[mc % 2])
                p.inc(vector.scalar_tensor_tensor(
                    outstg[mc % 2][:, :], ps[:, :], cols["b2c"][:, mc:mc + 1],
                    xnf[:, mc * CH4:(mc + 1) * CH4], op0=ALU.add,
                    op1=ALU.add), c_ve, 1)
            evv = p.cnt[pfx + "c_ve"]
            bankstate[mc % 2] = ("ve", evv)

            @block.gpsimd
            def _(gp):
                gp.wait_ge(c_ve, evv)
                p.inc(gp.dma_start(
                    out=bass.AP(outT_d.tensor, mc * 128 * T + tcol,
                                [[T, 128], [1, CH4]]),
                    in_=outstg[mc % 2][:, :]), d_out, 16)
            out_hist[mc % 2] = p.cnt[pfx + "d_out"]
        busyB = p.cnt[pfx + "c_pe"]

    @block.sync
    def _(sync):
        sync.wait_ge(d_out, p.cnt[pfx + "d_out"])


def build_phase4():
    p = P()
    nc = p.nc
    xnTb_d = nc.dram_tensor("xn1Tb", [D, T], BF16, kind="ExternalInput").ap()
    xnTf_d = nc.dram_tensor("xn1Tf", [D, T], F32, kind="ExternalInput").ap()
    wd = {}
    for n in L1A_W_BF:
        wd[n] = nc.dram_tensor(n, L1A_SHAPES[n], BF16, kind="ExternalInput").ap()
    for n in L1A_W_F32:
        wd[n] = nc.dram_tensor(n, L1A_SHAPES[n], F32, kind="ExternalInput").ap()
    outT_d = nc.dram_tensor("attn1T", [D, T], F32, kind="ExternalOutput").ap()
    with p.stack:
        with nc.Block() as block:
            emit_attn1(p, block, xnTb_d, xnTf_d, wd, outT_d, pfx="a1_")
    return p.nc


def prep_l1f_weights(g):
    def lay(w):
        w = np.ascontiguousarray(np.asarray(w, np.float32))
        i, o = w.shape
        return np.ascontiguousarray(
            w.reshape(i // 128, 128, o).transpose(1, 0, 2).reshape(128, -1))

    def col(v):
        v = np.asarray(v, np.float32).reshape(-1)
        out = np.zeros((128, max(1, (len(v) + 127) // 128)), np.float32)
        for i, x in enumerate(v):
            out[i % 128, i // 128] = x
        return out

    return {"WrT": lay(np.asarray(g["W_r"], np.float32).T).astype(BFNP),
            "WkT": lay(np.asarray(g["W_k"], np.float32).T).astype(BFNP),
            "WvT": lay(np.asarray(g["W_v"], np.float32).T).astype(BFNP),
            "murc": col(g["mu_r"]), "mukc": col(g["mu_k"])}


L1F_W_BF = ["WrT", "WkT", "WvT"]
L1F_W_F32 = ["murc", "mukc"]
L1F_SHAPES = {"WrT": [128, 8192], "WkT": [128, 8192], "WvT": [128, 8192],
              "murc": [128, 8], "mukc": [128, 8]}


def emit_ffn1(p, block, xnTb_d, wd, outT_d, pfx="f1_"):
    nc = p.nc
    wsb = {n: p.sbuf(pfx + n, L1F_SHAPES[n], BF16) for n in L1F_W_BF}
    csb = {n: p.sbuf(pfx + n, L1F_SHAPES[n], F32) for n in L1F_W_F32}
    onemur = p.sbuf(pfx + "onemur", [128, 8], F32)
    onemuk = p.sbuf(pfx + "onemuk", [128, 8], F32)
    xg = p.sbuf(pfx + "xg", [128, 8 * (T + 1)], BF16)
    xrb = p.sbuf(pfx + "xrb", [128, 8 * 512], BF16)
    xkb = p.sbuf(pfx + "xkb", [128, 8 * 512], BF16)
    rT = p.sbuf(pfx + "rT", [128, 8 * 512], BF16)
    kact = p.sbuf(pfx + "kact", [128, 8 * 512], BF16)
    t1 = p.sbuf(pfx + "t1", [128, 512], F32)
    t2 = p.sbuf(pfx + "t2", [128, 512], F32)
    outstg = [p.sbuf(pfx + f"outstg{i}", [128, 512], F32) for i in range(2)]

    psA = p.psum(pfx + "psA", [128, 512], F32)
    psB = p.psum(pfx + "psB", [128, 512], F32)

    c_pe = p.sem(pfx + "c_pe")
    c_ve = p.sem(pfx + "c_ve")
    c_se = p.sem(pfx + "c_se")
    vch = p.sem(pfx + "vch")
    d_wl = p.sem(pfx + "d_wl")
    d_out = p.sem(pfx + "d_out")
    semmap = {"se": c_se, "ve": c_ve, "pe": c_pe}
    bankstate = {0: ("se", 0), 1: ("se", 0)}
    out_hist = [0, 0]

    @block.gpsimd
    def _(gp):
        for n in L1F_W_BF:
            p.inc(gp.dma_start(out=wsb[n].ap(), in_=wd[n]), d_wl, 16)
        for n in L1F_W_F32:
            p.inc(gp.dma_start(out=csb[n].ap(), in_=wd[n]), d_wl, 16)
        for dc in range(8):
            gp.memset(xg[:, dc * (T + 1):dc * (T + 1) + 1], 0.0)
            p.inc(gp.dma_start(
                out=xg[:, dc * (T + 1) + 1:(dc + 1) * (T + 1)],
                in_=xnTb_d[dc * 128:(dc + 1) * 128, :]), d_wl, 16)
    wl_done = p.cnt[pfx + "d_wl"]

    @block.scalar
    def _(scalar):
        act_switch(scalar, nc, SET_SIG_ERF)

    @block.vector
    def _(vector):
        vector.wait_ge(d_wl, wl_done)
        vector.tensor_scalar(onemur[:, :], csb["murc"][:, :], -1.0, 1.0,
                             op0=ALU.mult, op1=ALU.add)
        p.inc(vector.tensor_scalar(onemuk[:, :], csb["mukc"][:, :], -1.0, 1.0,
                                   op0=ALU.mult, op1=ALU.add), vch, 1)
    mu_done = p.cnt[pfx + "vch"]

    for c in range(NC4):
        tcol = c * CH4
        barrier = dict(pe=p.cnt[pfx + "c_pe"], ve=p.cnt[pfx + "c_ve"],
                       se=p.cnt[pfx + "c_se"], vh=p.cnt[pfx + "vch"])

        @block.vector
        def _(vector):
            vector.wait_ge(vch, max(barrier["vh"], mu_done))
            vector.wait_ge(c_pe, barrier["pe"])
            vector.wait_ge(c_se, barrier["se"])
            for dc in range(8):
                cur = xg[:, dc * (T + 1) + 1 + tcol:
                         dc * (T + 1) + 1 + tcol + CH4]
                prv = xg[:, dc * (T + 1) + tcol: dc * (T + 1) + tcol + CH4]
                vector.tensor_scalar(t1[:, :], cur, csb["murc"][:, dc:dc + 1],
                                     None, op0=ALU.mult)
                vector.scalar_tensor_tensor(xrb[:, dc * CH4:(dc + 1) * CH4],
                                            prv, onemur[:, dc:dc + 1],
                                            t1[:, :], op0=ALU.mult,
                                            op1=ALU.add)
                vector.tensor_scalar(t1[:, :], cur, csb["mukc"][:, dc:dc + 1],
                                     None, op0=ALU.mult)
                ins = vector.scalar_tensor_tensor(
                    xkb[:, dc * CH4:(dc + 1) * CH4], prv,
                    onemuk[:, dc:dc + 1], t1[:, :], op0=ALU.mult, op1=ALU.add)
            p.inc(ins, vch, 1)
        prep_done = p.cnt[pfx + "vch"]

        for mc in range(8):
            ps = psA if mc % 2 == 0 else psB

            @block.tensor
            def _(tensor):
                if mc == 0:
                    tensor.wait_ge(vch, prep_done)
                kind, wv_ = bankstate[mc % 2]
                tensor.wait_ge(semmap[kind], wv_)
                if mc == 0 and c == 0:
                    tensor.wait_ge(d_wl, wl_done)
                for kc in range(8):
                    ins = tensor.matmul(
                        ps[:, :], wsb["WrT"][:, kc * 1024 + mc * 128:
                                             kc * 1024 + (mc + 1) * 128],
                        xrb[:, kc * CH4:(kc + 1) * CH4],
                        start=(kc == 0), stop=(kc == 7))
                p.inc(ins, c_pe, 1)
            mmv = p.cnt[pfx + "c_pe"]

            @block.scalar
            def _(scalar):
                scalar.wait_ge(c_pe, mmv)
                p.inc(scalar.activation(out=rT[:, mc * CH4:(mc + 1) * CH4],
                                        in_=ps[:, :], func=AF.Sigmoid),
                      c_se, 1)
            bankstate[mc % 2] = ("se", p.cnt[pfx + "c_se"])

        for mc in range(8):
            ps = psA if mc % 2 == 0 else psB

            @block.tensor
            def _(tensor):
                kind, wv_ = bankstate[mc % 2]
                tensor.wait_ge(semmap[kind], wv_)
                for kc in range(8):
                    ins = tensor.matmul(
                        ps[:, :], wsb["WkT"][:, kc * 1024 + mc * 128:
                                             kc * 1024 + (mc + 1) * 128],
                        xkb[:, kc * CH4:(kc + 1) * CH4],
                        start=(kc == 0), stop=(kc == 7))
                p.inc(ins, c_pe, 1)
            mmv = p.cnt[pfx + "c_pe"]

            @block.vector
            def _(vector):
                vector.wait_ge(c_pe, mmv)
                vector.tensor_scalar_max(t2[:, :], ps[:, :], 0.0)
                p.inc(vector.tensor_mul(kact[:, mc * CH4:(mc + 1) * CH4],
                                        t2[:, :], t2[:, :]), c_ve, 1)
            bankstate[mc % 2] = ("ve", p.cnt[pfx + "c_ve"])
        kact_done = p.cnt[pfx + "c_ve"]

        for mc in range(8):
            ps = psA if mc % 2 == 0 else psB

            @block.tensor
            def _(tensor):
                if mc == 0:
                    tensor.wait_ge(c_ve, kact_done)
                kind, wv_ = bankstate[mc % 2]
                tensor.wait_ge(semmap[kind], wv_)
                for kc in range(8):
                    ins = tensor.matmul(
                        ps[:, :], wsb["WvT"][:, kc * 1024 + mc * 128:
                                             kc * 1024 + (mc + 1) * 128],
                        kact[:, kc * CH4:(kc + 1) * CH4],
                        start=(kc == 0), stop=(kc == 7))
                p.inc(ins, c_pe, 1)
            mmv = p.cnt[pfx + "c_pe"]

            @block.vector
            def _(vector):
                vector.wait_ge(c_pe, mmv)
                if out_hist[mc % 2]:
                    vector.wait_ge(d_out, out_hist[mc % 2])
                p.inc(vector.tensor_mul(outstg[mc % 2][:, :], ps[:, :],
                                        rT[:, mc * CH4:(mc + 1) * CH4]),
                      c_ve, 1)
            evv = p.cnt[pfx + "c_ve"]
            bankstate[mc % 2] = ("ve", evv)

            @block.gpsimd
            def _(gp):
                gp.wait_ge(c_ve, evv)
                p.inc(gp.dma_start(
                    out=bass.AP(outT_d.tensor, mc * 128 * T + tcol,
                                [[T, 128], [1, CH4]]),
                    in_=outstg[mc % 2][:, :]), d_out, 16)
            out_hist[mc % 2] = p.cnt[pfx + "d_out"]

    @block.sync
    def _(sync):
        sync.wait_ge(d_out, p.cnt[pfx + "d_out"])


def build_phase5():
    p = P()
    nc = p.nc
    xnTb_d = nc.dram_tensor("xn1Tb", [D, T], BF16, kind="ExternalInput").ap()
    wd = {}
    for n in L1F_W_BF:
        wd[n] = nc.dram_tensor(n, L1F_SHAPES[n], BF16, kind="ExternalInput").ap()
    for n in L1F_W_F32:
        wd[n] = nc.dram_tensor(n, L1F_SHAPES[n], F32, kind="ExternalInput").ap()
    outT_d = nc.dram_tensor("ffn1T", [D, T], F32, kind="ExternalOutput").ap()
    with p.stack:
        with nc.Block() as block:
            emit_ffn1(p, block, xnTb_d, wd, outT_d, pfx="f1_")
    return p.nc


def build_fused():
    p = P()
    nc = p.nc
    x_d = nc.dram_tensor("x", [T, D], F32, kind="ExternalInput").ap()
    a0w = {n: nc.dram_tensor(n, ATTN_SHAPES[n], F32, kind="ExternalInput").ap()
           for n in ATTN_W}
    f0w = {n: nc.dram_tensor(n, FFN_SHAPES[n], F32, kind="ExternalInput").ap()
           for n in FFN_W_BF + FFN_W_F32}
    a1w = {n: nc.dram_tensor(n, L1A_SHAPES[n],
                             BF16 if n in L1A_W_BF else F32,
                             kind="ExternalInput").ap()
           for n in L1A_W_BF + L1A_W_F32}
    f1w = {n: nc.dram_tensor(n, L1F_SHAPES[n],
                             BF16 if n in L1F_W_BF else F32,
                             kind="ExternalInput").ap()
           for n in L1F_W_BF + L1F_W_F32}
    xn0_d = nc.dram_tensor("xn0", [T, D], F32, kind="Internal").ap()
    xn0T_d = nc.dram_tensor("xn0T", [D, T], BF16, kind="Internal").ap()
    attn0T_d = nc.dram_tensor("attn0T", [D, T], F32, kind="Internal").ap()
    f0T_d = nc.dram_tensor("f0T", [D, T], F32, kind="Internal").ap()
    crev_d = [nc.dram_tensor(f"crevd{i}", [130, 1024], BF16,
                             kind="Internal").ap() for i in range(2)]
    x1_d = nc.dram_tensor("x1", [T, D], F32, kind="Internal").ap()
    xn1_d = nc.dram_tensor("xn1", [T, D], F32, kind="Internal").ap()
    xn1Tb_d = nc.dram_tensor("xn1Tb", [D, T], BF16, kind="Internal").ap()
    xn1Tf_d = nc.dram_tensor("xn1Tf", [D, T], F32, kind="Internal").ap()
    attn1T_d = nc.dram_tensor("attn1T", [D, T], F32, kind="Internal").ap()
    ffn1T_d = nc.dram_tensor("ffn1T", [D, T], F32, kind="Internal").ap()
    out_d = nc.dram_tensor("out", [T, D], F32, kind="ExternalOutput").ap()

    with p.stack:
        with nc.Block() as block:
            bar = {"i": 0}

            def barrier():
                new = p.sems[bar["i"]:]
                bar["i"] = len(p.sems)
                waits = [(s, p.cnt[s.name]) for s in new if p.cnt[s.name] > 0]

                @block.tensor
                def _(tensor):
                    for s, v in waits:
                        tensor.wait_ge(s, v)

                @block.vector
                def _(vector):
                    for s, v in waits:
                        vector.wait_ge(s, v)

                @block.scalar
                def _(scalar):
                    for s, v in waits:
                        scalar.wait_ge(s, v)

                @block.sync
                def _(sync):
                    for s, v in waits:
                        sync.wait_ge(s, v)

                @block.gpsimd
                def _(gp):
                    for s, v in waits:
                        gp.wait_ge(s, v)

            with p.phase(), nc.named_scope("p0"):
                emit_ln_transpose(p, block, x_d, xn0_d, xn0T_d, pfx="p0_")
            barrier()
            with p.phase(), nc.named_scope("p1attn0"):
                emit_attn0(p, block, xn0T_d, a0w, attn0T_d, crev_d, pfx="a0_")
            barrier()
            with p.phase(), nc.named_scope("p2ffn0"):
                emit_ffn0(p, block, xn0T_d, f0w, f0T_d, pfx="f0_")
            barrier()
            with p.phase(), nc.named_scope("p3a"):
                emit_assemble(p, block, x_d, xn0_d, attn0T_d, f0T_d, x1_d,
                              pfx="p3a_")
            barrier()
            with p.phase(), nc.named_scope("p3b"):
                emit_ln_transpose(p, block, x1_d, xn1_d, xn1Tb_d, pfx="p3b_",
                                  xnTf_d=xn1Tf_d)
            barrier()
            with p.phase(), nc.named_scope("p4attn1"):
                emit_attn1(p, block, xn1Tb_d, xn1Tf_d, a1w, attn1T_d,
                           pfx="a1_")
            barrier()
            with p.phase(), nc.named_scope("p5ffn1"):
                emit_ffn1(p, block, xn1Tb_d, f1w, ffn1T_d, pfx="f1_")
            barrier()
            with p.phase(), nc.named_scope("p6"):
                emit_assemble(p, block, x1_d, None, attn1T_d, ffn1T_d, out_d,
                              pfx="p6_")
    return p.nc


def kernel(**inputs):
    X = np.asarray(inputs["L0_attn_X"], np.float32)
    attn_w = {n: np.ascontiguousarray(np.asarray(inputs["L0_attn_" + n],
                                                 np.float32))
              for n in ATTN_W}
    ffn_wd = prep_ffn_weights(
        {k[len("L0_ffn_"):]: np.asarray(v) for k, v in inputs.items()
         if k.startswith("L0_ffn_")})
    l1a_wd = prep_l1a_weights(
        {k[len("L1_attn_"):]: np.asarray(v) for k, v in inputs.items()
         if k.startswith("L1_attn_")})
    l1f_wd = prep_l1f_weights(
        {k[len("L1_ffn_"):]: np.asarray(v) for k, v in inputs.items()
         if k.startswith("L1_ffn_")})
    cores = list(range(8))
    wmap = dict(attn_w)
    wmap.update(ffn_wd)
    wmap.update(l1a_wd)
    wmap.update(l1f_wd)
    res = run_bass_kernel_spmd(
        build_fused(),
        [dict(wmap, x=np.ascontiguousarray(X[i])) for i in cores],
        core_ids=cores).results
    return np.stack([np.asarray(r["out"], np.float32) for r in res])


def kernel_multi(**inputs):
    X = np.asarray(inputs["L0_attn_X"], np.float32)
    attn_w = {n: np.ascontiguousarray(np.asarray(inputs["L0_attn_" + n],
                                                 np.float32))
              for n in ATTN_W}
    ffn_wd = prep_ffn_weights(
        {k[len("L0_ffn_"):]: np.asarray(v) for k, v in inputs.items()
         if k.startswith("L0_ffn_")})
    l1a_wd = prep_l1a_weights(
        {k[len("L1_attn_"):]: np.asarray(v) for k, v in inputs.items()
         if k.startswith("L1_attn_")})
    l1f_wd = prep_l1f_weights(
        {k[len("L1_ffn_"):]: np.asarray(v) for k, v in inputs.items()
         if k.startswith("L1_ffn_")})
    cores = list(range(8))

    res = run_bass_kernel_spmd(
        build_phase0(), [{"x": np.ascontiguousarray(X[i])} for i in cores],
        core_ids=cores).results
    xn0 = [r["xn0"] for r in res]
    xn0T = [r["xn0T"] for r in res]

    res = run_bass_kernel_spmd(
        build_phase1(), [dict(attn_w, xn0T=xn0T[i]) for i in cores],
        core_ids=cores).results
    attn0T = [r["attn0T"] for r in res]

    res = run_bass_kernel_spmd(
        build_phase2(), [dict(ffn_wd, xn0T=xn0T[i]) for i in cores],
        core_ids=cores).results
    f0T = [r["f0T"] for r in res]

    res = run_bass_kernel_spmd(
        build_phase3a(),
        [{"x": np.ascontiguousarray(X[i]), "xn0": xn0[i],
          "attn0T": attn0T[i], "f0T": f0T[i]} for i in cores],
        core_ids=cores).results
    x1 = [r["x1"] for r in res]

    res = run_bass_kernel_spmd(
        build_phase3b(), [{"x1": x1[i]} for i in cores],
        core_ids=cores).results
    xn1Tb = [r["xn1Tb"] for r in res]
    xn1Tf = [r["xn1Tf"] for r in res]

    res = run_bass_kernel_spmd(
        build_phase4(),
        [dict(l1a_wd, xn1Tb=xn1Tb[i], xn1Tf=xn1Tf[i]) for i in cores],
        core_ids=cores).results
    attn1T = [r["attn1T"] for r in res]

    res = run_bass_kernel_spmd(
        build_phase5(), [dict(l1f_wd, xn1Tb=xn1Tb[i]) for i in cores],
        core_ids=cores).results
    ffn1T = [r["ffn1T"] for r in res]

    res = run_bass_kernel_spmd(
        build_phase6(),
        [{"x1": x1[i], "attn1T": attn1T[i], "ffn1T": ffn1T[i]}
         for i in cores],
        core_ids=cores).results
    return np.stack([np.asarray(r["out"], np.float32) for r in res])
